# version 03
# baseline (speedup 1.0000x reference)
import numpy as np
import bisect
from contextlib import ExitStack
import concourse.bass as bass
import concourse.mybir as mybir
from concourse.bass_utils import run_bass_kernel_spmd

F32 = mybir.dt.float32
BF16 = mybir.dt.bfloat16
AF = mybir.ActivationFunctionType
OP = mybir.AluOpType
AX = mybir.AxisListType
from bass_rust import ReduceOp as RO

ENG = ("sync", "tensor", "vector", "scalar", "gpsimd")
NCORES = 8
D = 2048
DSSM = 4096
NH = 64
HD = 64
DST = 128
CONVD = 4352
DIN = 8512
I0 = 5632
H1 = 8192
SCALE_L1 = float(DST ** -0.5)


NDSEM = 14


class Prog:

    def __init__(self):
        self.ops = {e: [] for e in ENG}
        self.dnames = ["D%d" % i for i in range(NDSEM)]
        self.cnt = {e: 0 for e in ENG}
        self.cnt.update({d: 0 for d in self.dnames})
        self.vc = {e: {} for e in ENG}
        self.hist_c = {e: [0] for e in ENG}
        self.hist_s = {e: [{}] for e in ENG}
        self.wmap = {}
        self.rmap = {}
        self.nwaits = 0
        self.fence = {e: 0 for e in ENG}
        self.drr = 0
        self.small = {"wcol", "wcol2", "winv", "winvB", "s1gk", "s1uk", "s2all",
                      "dccar", "bd", "rowdc", "rowprev", "grow", "gcol", "carry",
                      "ct2", "off1", "offc", "s1c", "th1", "th2", "s2c", "s2cl1",
                      "dtmp", "dacc1", "dacc2", "pmc", "amx3", "b2hm", "drh",
                      "lnra", "lnrb", "lnrc", "dclj", "b2s", "b2i", "d2i", "d2s",
                      "rmsr", "acol", "dtb", "dcp0", "scolB"}

    def barrier(self):
        self.bar = {s: c for s, c in self.cnt.items() if c > 0}
        self.bar_todo = set(ENG)

    def emit(self, eng, fn, reads=(), writes=(), dma=False, sreads=()):
        need = {}

        def add(e2, c2):
            if e2 != eng and c2 > need.get(e2, 0):
                need[e2] = c2

        if getattr(self, "bar_todo", None) and eng in self.bar_todo:
            self.bar_todo.discard(eng)
            for s, c in self.bar.items():
                if s == eng:
                    if c > self.fence[eng]:
                        need[eng] = max(need.get(eng, 0), c)
                else:
                    add(s, c)

        for k in reads:
            t = self.wmap.get(k)
            if t:
                add(*t)
        for k in writes:
            t = self.wmap.get(k)
            if t:
                add(*t)
            for e2, c2 in self.rmap.get(k, {}).items():
                add(e2, c2)
        for k in sreads:
            t = self.wmap.get(k)
            if t and t[0] == eng and t[1] > self.fence[eng]:
                need[eng] = max(need.get(eng, 0), t[1])
        for k in reads:
            if k in self.small:
                t = self.wmap.get(k)
                if t and t[0] == eng and t[1] > self.fence[eng]:
                    need[eng] = max(need.get(eng, 0), t[1])
        if dma:
            selfneed = 0
            for k in list(reads) + list(writes):
                t = self.wmap.get(k)
                if t and t[0] == eng:
                    selfneed = max(selfneed, t[1])
            for k in writes:
                c2 = self.rmap.get(k, {}).get(eng, 0)
                selfneed = max(selfneed, c2)
            if selfneed > 0:
                need[eng] = max(need.get(eng, 0), selfneed)
            dn = self.dnames[self.drr % NDSEM]
            self.drr += 1
            if self.cnt[dn] > 0:
                add(dn, self.cnt[dn])
        waits = []
        if need.get(eng, 0) > self.fence[eng]:
            waits.append((eng, need[eng]))
            self.fence[eng] = need[eng]
        need.pop(eng, None)
        v = self.vc[eng]
        for e2, c2 in sorted(need.items()):
            if v.get(e2, 0) >= c2:
                continue
            waits.append((e2, c2))
            if e2 in self.hist_c:
                i = bisect.bisect_right(self.hist_c[e2], c2) - 1
                for f, cf in self.hist_s[e2][i].items():
                    if cf > v.get(f, 0):
                        v[f] = cf
            if c2 > v.get(e2, 0):
                v[e2] = c2
        self.nwaits += len(waits)
        if dma:
            sem = dn
            inc = 16
        else:
            sem = eng
            inc = 1
        self.cnt[sem] += inc
        cafter = self.cnt[sem]
        if not dma:
            v[eng] = cafter
            self.hist_c[eng].append(cafter)
            self.hist_s[eng].append(dict(v))
        self.ops[eng].append((waits, fn, inc, sem))
        for k in writes:
            self.wmap[k] = (sem, cafter)
            self.rmap[k] = {}
        for k in reads:
            self.rmap.setdefault(k, {})[sem] = cafter

    def finalize(self, nc, es):
        sems = {e: es.enter_context(nc.semaphore("sem_" + e)) for e in ENG}
        for d in self.dnames:
            sems[d] = es.enter_context(nc.semaphore("sem_" + d))
        fin = [(e, self.cnt[e]) for e in list(ENG) + self.dnames
               if e != "sync" and self.cnt[e] > 0]
        self.ops["sync"].append((fin, None, 0, "sync"))
        blk = es.enter_context(nc.Block())
        for eng in ENG:
            dec = getattr(blk, eng)
            lst = self.ops[eng]
            if not lst:
                continue

            def mk(lst=lst, eng=eng):
                @dec
                def _(e):
                    for waits, fn, inc, sem in lst:
                        for (e2, vv) in waits:
                            e.wait_ge(sems[e2], vv)
                        if fn is not None:
                            inst = fn(e)
                            inst.then_inc(sems[sem], inc)
            mk()


def split_cols(off, end, step):
    out = []
    while off < end:
        ln = min(step, end - off)
        out.append((off, ln))
        off += ln
    return out


def build(T, debug=False):
    HT = T // 2
    NCH = HT // 128
    nc = bass.Bass("TRN2", target_bir_lowering=False, debug=False)
    pr = Prog()

    def dram(name, shape, dt=F32, kind=None):
        if kind:
            return nc.dram_tensor(name, shape, dt, kind=kind)
        return nc.dram_tensor(name, shape, dt)

    uT = dram("uT", [D, T + 3], kind="ExternalInput").ap()
    w_in_p = dram("w_in_p", [67, 128, 2048], kind="ExternalInput").ap()
    convw = dram("convw", [128, 34 * 4], kind="ExternalInput").ap()
    convb = dram("convb", [128, 34], kind="ExternalInput").ap()
    dtb_d = dram("dtb", [64, 1], kind="ExternalInput").ap()
    alog_d = dram("alog", [64, 1], kind="ExternalInput").ap()
    drep_d = dram("drep", [128, 32], kind="ExternalInput").ap()
    nw_d = dram("nw", [128, 32], kind="ExternalInput").ap()
    w_out_p = dram("w_out_p", [16, 128, 4096], kind="ExternalInput").ap()
    w_gup_p = dram("w_gup_p", [88, 128, 2048], kind="ExternalInput").ap()
    w_down_p = dram("w_down_p", [16, 128, 5632], kind="ExternalInput").ap()
    wq_p = dram("wq_p", [16, 128, 2048], kind="ExternalInput").ap()
    wk_p = dram("wk_p", [16, 128, 2048], kind="ExternalInput").ap()
    wv_p = dram("wv_p", [16, 128, 2048], kind="ExternalInput").ap()
    wo_p = dram("wo_p", [16, 128, 2048], kind="ExternalInput").ap()
    bq_d = dram("bq", [128, 16], kind="ExternalInput").ap()
    bk_d = dram("bk", [128, 16], kind="ExternalInput").ap()
    bv_d = dram("bv", [128, 16], kind="ExternalInput").ap()
    bo_d = dram("bo", [128, 16], kind="ExternalInput").ap()
    cfc_p = dram("cfc_p", [64, 128, 2048], kind="ExternalInput").ap()
    cprT_p = dram("cprT_p", [16, 128, 8192], kind="ExternalInput").ap()
    eye_d = dram("eye", [128, 128], kind="ExternalInput").ap()
    ohp_d = dram("ohp", [64, 4096], kind="ExternalInput").ap()
    am_d = dram("am", [128, 128], kind="ExternalInput").ap()
    mask8_d = dram("mask8", [128, 8], kind="ExternalInput").ap()
    maskexp_d = dram("maskexp", [128, 16], kind="ExternalInput").ap()
    outT = dram("outT", [D, T], kind="ExternalOutput").ap()

    sz_s = dram("sz_s", [32, 128, T]).ap()
    yg_s = dram("yg_s", [32, 128, T]).ap()
    x1_s = dram("x1_s", [16, 128, T]).ap()
    qx_s = dram("qx_s", [16, 128, T], BF16).ap()
    h0_s = dram("h0_s", [44, 128, T]).ap()
    q2L0_s = dram("q2L0_s", [16, 128, 5632], BF16).ap()
    x2_s = dram("x2_s", [16, 128, T]).ap()
    kv_s = dram("kv_s", [16, 128, T]).ap()
    pq_s = dram("pq_s", [16, 128, T]).ap()
    x3_s = dram("x3_s", [16, 128, T]).ap()
    x3h_s = dram("x3h_s", [16, 128, T], BF16).ap()
    x3l_s = dram("x3l_s", [16, 128, T], BF16).ap()
    r_s = dram("r_s", [64, 128, T]).ap()
    q2L1_s = dram("q2L1_s", [16, 128, 8192], BF16).ap()
    cc_in = dram("cc_in", [129, 4096])
    cc_out = dram("cc_out", [8 * 129, 4096])
    cc2_in = dram("cc2_in", [16, 128])
    cc2_out = dram("cc2_out", [128, 128])
    dbg = {}
    if debug:
        for nm, shp, dt in [("d_sz", [32, 128, T], F32), ("d_yg", [32, 128, T], F32),
                            ("d_x1", [16, 128, T], F32), ("d_x2", [16, 128, T], F32),
                            ("d_x3", [16, 128, T], F32), ("d_kv", [16, 128, T], F32),
                            ("d_st", [128, 4096], F32), ("d_sin", [128, 4096], F32),
                            ("d_dc", [64, T + 1], F32), ("d_dts", [64, T], F32),
                            ("d_ygt", [32, 128, T], F32), ("d_cdb", [32, 128, T], F32),
                            ("d_p2r", [32, 128, T], F32), ("d_cda", [32, 128, T], F32),
                            ("d_h", [44, 128, T], F32), ("d_qx", [16, 128, T], F32),
                            ("d_q1", [2, 128, 2048], BF16), ("d_s1", [2, 128], F32),
                            ("d_gu", [2, 128, 512], F32), ("d_gy", [128, 2048], F32),
                            ("d_wc", [2, 128], F32), ("d_gq", [128, 2048], F32)]:
            dbg[nm] = dram(nm, shp, dt, kind="ExternalOutput").ap()

    ES = ExitStack()

    def sb(name, shape, dt=F32, stack=None):
        return (stack or ES).enter_context(nc.sbuf_tensor(name, shape, dt))

    def ps(name, shape):
        return ES.enter_context(nc.psum_tensor(name, shape, F32))

    P0 = ps("P0", [128, 512])
    P1 = ps("P1", [128, 512])
    P2 = ps("P2", [128, 512])
    P3 = ps("P3", [128, 512])
    PA = ps("PA", [128, 132])
    PC = ps("PC", [128, 128])
    PR1 = ps("PR1", [1, 512])
    PR2 = ps("PR2", [1, 512])
    PQ = [P0, P1, P2, P3]

    ones = sb("ones", [128, 1024 if T >= 2048 else T])
    ones_col = sb("ones_col", [128, 1])
    ones_row = sb("ones_row", [1, 128])
    eye_sb = sb("eye_sb", [128, 128])
    am_sb = sb("am_sb", [128, 128])
    mask8_sb = sb("mask8_sb", [128, 8])
    maskexp_sb = sb("maskexp_sb", [128, 16])
    sxT = sb("sxT", [128, T])
    s2all = sb("s2all", [128, 16])
    s2cL1 = sb("s2cL1", [128, 1])

    def dma(eng, dst, src, reads, writes):
        pr.emit(eng, lambda e, d=dst, s=src: e.dma_start(out=d, in_=s), reads, writes, dma=True)

    def mm(out, lhsT, rhs, start, stop, reads, writes):
        pr.emit("tensor", lambda e, o=out, l=lhsT, r=rhs, st=start, sp=stop:
                e.matmul(o, l, r, start=st, stop=sp), reads, writes)

    def trans(out, in_, reads, writes, k=128):
        pr.emit("tensor", lambda e, o=out, i=in_, kk=k: e.transpose(o, i, eye_sb[0:kk, 0:kk]),
                tuple(reads) + ("eye",), writes)

    def tt(eng, out, a, b, op, reads, writes):
        pr.emit(eng, lambda e, o=out, x=a, y=b, p=op: e.tensor_tensor(o, x, y, p), reads, writes)

    def _isap(v):
        return v is not None and not isinstance(v, (int, float))

    def ts(eng, out, a, s1, s2, op0, op1, reads, writes):
        sr = tuple(reads) if (_isap(s1) or _isap(s2)) else ()
        if op1 is None:
            pr.emit(eng, lambda e, o=out, x=a, u=s1, p0=op0: e.tensor_scalar(o, x, u, None, p0),
                    reads, writes, sreads=sr)
        else:
            pr.emit(eng, lambda e, o=out, x=a, u=s1, v=s2, p0=op0, p1=op1:
                    e.tensor_scalar(o, x, u, v, p0, p1), reads, writes, sreads=sr)

    def stt(eng, out, in0, sc, in1, op0, op1, reads, writes):
        sr = tuple(reads) if _isap(sc) else ()
        pr.emit(eng, lambda e, o=out, x=in0, s=sc, y=in1, p0=op0, p1=op1:
                e.scalar_tensor_tensor(o, x, s, y, p0, p1), reads, writes, sreads=sr)

    def act(out, in_, func, reads, writes, bias=None, scale=None, accum=None):
        if isinstance(bias, float) or isinstance(scale, float):
            reads = tuple(reads) + ("constap",)
        sr = tuple(reads) if (_isap(bias) or _isap(scale)) else ()

        def f(e, o=out, i=in_, fu=func, b=bias, s=scale, a=accum):
            kw = {}
            if b is not None:
                kw["bias"] = b
            if s is not None:
                kw["scale"] = s
            if a is not None:
                kw["accum_out"] = a
            return e.activation(o, i, fu, **kw)
        pr.emit("scalar", f, reads, writes, sreads=sr)

    def tcopy(eng, out, in_, reads, writes):
        pr.emit(eng, lambda e, o=out, i=in_: e.tensor_copy(o, i), reads, writes)

    def mset(eng, ap, v, writes):
        pr.emit(eng, lambda e, a=ap, c=v: e.memset(a, c), (), writes)

    def redmax_free(eng, out, in_, reads, writes, absval=True):
        pr.emit("vector", lambda e, o=out, i=in_, a=absval:
                e.reduce_max(o, i, axis=AX.X, apply_absolute_value=a), reads, writes)

    def tree_max(tile, key, tmp2, tmp2key, tc):
        for p in (64, 32):
            tcopy("vector", tmp2[0:p, 0:tc], tile[p:2 * p, 0:tc], [key], [tmp2key])
            tt("vector", tile[0:p, 0:tc], tile[0:p, 0:tc], tmp2[0:p, 0:tc], OP.max,
               [key, tmp2key], [key])
        for j in range(tc // 128):
            trans(PC[0:128, 0:32], tile[0:32, j * 128:j * 128 + 128], [key], ["PC"], k=32)
            redmax_free("vector", pmc[:, j:j + 1], PC[0:128, 0:32], ["PC", "pmc"],
                        ["pmc"], absval=False)
        for j in range(tc // 128):
            trans(PC[0:1, 0:128], pmc[:, j:j + 1], ["pmc"], ["PC"], k=128)
            tcopy("vector", tile[0:1, j * 128:j * 128 + 128], PC[0:1, 0:128], ["PC"], [key])

    def scan_op(eng, out, d0, d1, reads, writes):
        pr.emit(eng, lambda e, o=out, a=d0, b=d1:
                e.tensor_tensor_scan(o, a, b, 0.0, OP.mult, OP.add), reads, writes)

    def recip(out, in_, reads, writes):
        pr.emit("vector", lambda e, o=out, i=in_: e.reciprocal(o, i), reads, writes,
                sreads=tuple(reads))

    def coll(name_key, ins_t, outs_t, reads, writes):
        pr.emit("gpsimd", lambda e, i=ins_t, o=outs_t: e.collective_compute(
            "AllGather", OP.bypass, replica_groups=[list(range(NCORES))],
            ins=[i.ap().opt()], outs=[o.ap().opt()]), reads, writes)

    VG = ["vector", "gpsimd"]

    def ladder_round(pref, tile, y, y0, q, cols, bits, eng, reads):
        ts("vector", y, y, 0.5, None, OP.add, None, reads, [pref + "y"])
        tcopy("gpsimd", y0, y, [pref + "y"], [pref + "y0"])
        for c in bits:
            ts("vector", q, y, float(c), float(-c), OP.is_ge, OP.mult, [pref + "y"], [pref + "q"])
            tt("gpsimd", y, y, q, OP.add, [pref + "y", pref + "q"], [pref + "y"])
        tt("gpsimd", q, y0, y, OP.subtract, [pref + "y", pref + "y0"], [pref + "q"])

    lnsq = sb("lnsq", [128, 512])
    lnra_t = sb("lnra_t", [1, 512])
    lnrb_t = sb("lnrb_t", [1, 512])
    lnrc_t = sb("lnrc_t", [1, 512])

    def layer_norm(xs_tiles, ks, tc):
        lnrb = lnrb_t[0:1, 0:tc]
        lnrc = lnrc_t[0:1, 0:tc]
        lnra = lnra_t[0:1, 0:tc]
        for m in range(16):
            mm(PR1[0:1, 0:tc], ones_col[:, :], xs_tiles[m][:, 0:tc], m == 0, m == 15,
               ["onec", ks[m]], ["PR1"])
        for m in range(16):
            act(lnsq[:, 0:tc], xs_tiles[m][:, 0:tc], AF.Square, [ks[m]], ["lnsq"])
            mm(PR2[0:1, 0:tc], ones_col[:, :], lnsq[:, 0:tc], m == 0, m == 15,
               ["onec", "lnsq"], ["PR2"])
        ts("vector", lnra, PR1[0:1, 0:tc], 1.0 / 2048.0, None, OP.mult, None,
           ["PR1"], ["lnra"])
        tt("vector", lnrb, lnra, lnra, OP.mult,
           ["lnra"], ["lnrb"])
        stt("vector", lnrb, PR2[0:1, 0:tc], 1.0 / 2048.0, lnrb,
            OP.mult, OP.subtract, ["PR2", "lnrb"], ["lnrb"])
        act(lnrb, lnrb, AF.Sqrt, ["lnrb"], ["lnrb"], bias=1e-5)
        recip(lnrb, lnrb, ["lnrb"], ["lnrb"])
        tt("vector", lnrc, lnra, lnrb, OP.mult,
           ["lnra", "lnrb"], ["lnrc"])
        mm(P0[:, 0:tc], ones_row[:, :], lnrb, True, True, ["oner", "lnrb"], ["P0"])
        mm(P1[:, 0:tc], ones_row[:, :], lnrc, True, True, ["oner", "lnrc"], ["P1"])
        for m in range(16):
            tt("vector", xs_tiles[m][:, 0:tc], xs_tiles[m][:, 0:tc], P0[:, 0:tc], OP.mult,
               [ks[m], "P0"], [ks[m]])
            tt("vector", xs_tiles[m][:, 0:tc], xs_tiles[m][:, 0:tc], P1[:, 0:tc], OP.subtract,
               [ks[m], "P1"], [ks[m]])

    mset("vector", ones[:, :], 1.0, ["ones"])
    mset("vector", ones_col[:, :], 1.0, ["onec"])
    mset("vector", ones_row[:, :], 1.0, ["oner"])
    pmc = sb("pmc", [128, 8])
    consts = sb("consts", [128, 3])
    nc.const_aps.register(F32, 1e-5, consts[:, 0:1])
    nc.const_aps.register(F32, 1.0 / 4096.0, consts[:, 1:2])
    nc.const_aps.register(F32, -1.0, consts[:, 2:3])
    mset("vector", consts[:, 0:1], 1e-5, ["constap"])
    mset("vector", consts[:, 1:2], 1.0 / 4096.0, ["constap"])
    mset("vector", consts[:, 2:3], -1.0, ["constap"])
    dma("sync", eye_sb[:, :], eye_d[:, :], [], ["eye"])
    dma("sync", am_sb[:, :], am_d[:, :], [], ["am"])
    dma("sync", mask8_sb[:, :], mask8_d[:, :], [], ["mask8"])
    dma("sync", maskexp_sb[:, :], maskexp_d[:, :], [], ["maskexp"])

    ESL0 = ExitStack()
    Dc_pad = sb("Dc_pad", [64, T + 1], stack=ESL0)
    C_full = sb("C_full", [128, T], stack=ESL0)
    cdec = sb("cdec", [64, T], stack=ESL0)
    S_in = sb("S_in", [128, 4096], stack=ESL0)
    dtb_sb = sb("dtb_sb", [64, 1], stack=ESL0)
    acol = sb("acol", [64, 1], stack=ESL0)

    dma("sync", dtb_sb[:, :], dtb_d[:, :], [], ["dtb"])
    dma("sync", acol[:, :], alog_d[:, :], [], ["acol"])
    act(acol[:, :], acol[:, :], AF.Exp, ["acol"], ["acol"])
    ts("vector", acol[:, :], acol[:, :], -1.0, None, OP.mult, None, ["acol"], ["acol"])
    mset("vector", Dc_pad[:, 0:1], 0.0, ["dcp0"])

    ESAs = ExitStack()
    ST = sb("ST", [128, 4096], stack=ESAs)
    drep_sb = sb("drep_sb", [128, 32], stack=ESAs)
    dts_h = sb("dts_h", [64, HT], stack=ESAs)
    logda = sb("logda", [64, HT], stack=ESAs)
    B_half = sb("B_half", [128, HT], stack=ESAs)
    raw = sb("rawc", [128, HT + 3], stack=ESAs)
    convw_sb = sb("convw_sb", [128, 34 * 4], stack=ESAs)
    convb_sb = sb("convb_sb", [128, 34], stack=ESAs)
    BT_c = [sb(f"BT{c}", [128, 128], stack=ESAs) for c in range(NCH)]
    G_c = [sb(f"G{c}", [128, 128], stack=ESAs) for c in range(NCH)]
    DcT_c = [sb(f"DcT{c}", [128, 64], stack=ESAs) for c in range(NCH)]
    dtT_c = [sb(f"dtT{c}", [128, 64], stack=ESAs) for c in range(NCH)]
    sz_gS = [sb(f"sz_g{i}", [128, HT], stack=ESAs) for i in range(2)]
    xc_gS = [sb(f"xc_g{i}", [128, HT], stack=ESAs) for i in range(2)]
    yg_g = sb("yg_g", [128, HT], stack=ESAs)
    xT_sb = sb("xT_sb", [128, 128], stack=ESAs)
    cdecB = sb("cdecB", [128, 128], stack=ESAs)
    E_sb = sb("E_sb", [128, 128], stack=ESAs)
    M_sb = sb("M_sb", [128, 128], stack=ESAs)
    Pm_sb = sb("Pm_sb", [128, 128], stack=ESAs)
    Cd_sb = sb("Cd_sb", [128, 128], stack=ESAs)
    Bd_sb = sb("Bd_sb", [128, 128], stack=ESAs)
    bd_sb = sb("bd_sb", [128, 1], stack=ESAs)
    tmp_sc = sb("tmp_sc", [128, 128], stack=ESAs)
    rowdc = sb("rowdc", [1, 132], stack=ESAs)
    rowprev = sb("rowprev", [1, 64], stack=ESAs)

    ESAu = ExitStack()
    u_t = [sb(f"u{k}", [128, HT + 3], stack=ESAu) for k in range(16)]
    wsl = [sb("wsl0", [128, 2048], stack=ESAu)] * 2

    dma("sync", drep_sb[:, :], drep_d[:, :], [], ["drep"])
    dma("sync", convw_sb[:, :], convw[:, :], [], ["convw"])
    dma("sync", convb_sb[:, :], convb[:, :], [], ["convb"])
    mset("vector", ST[:, :], 0.0, ["ST"])
    mset("vector", rowprev[:, :], 0.0, ["rowprev"])

    def inproj_mm(mc, cols, pslices, wbuf):
        dma("sync", wsl[0][:, :], w_in_p[mc], ["wsl0"], ["wsl0"])
        for ci, (off, ln) in enumerate(cols):
            pap, pkey = pslices[ci]
            for kt in range(16):
                mm(pap[:, 0:ln] if pap is not PA else pap[:, 129:129 + ln],
                   wsl[0][:, kt * 128:kt * 128 + 128],
                   u_t[kt][:, off:off + ln], kt == 0, kt == 15,
                   ["wsl0", f"u{kt}"], [pkey])

    for half in range(2):
        hoff = half * HT
        for kt in range(16):
            dma("sync", u_t[kt][:, :], uT[kt * 128:kt * 128 + 128, hoff:hoff + HT + 3],
                [f"u{kt}"], [f"u{kt}"])
        zcols = split_cols(3, HT + 3, 512)
        xcols = split_cols(0, HT + 3, 512)

        def psl(cols):
            out = []
            for i in range(len(cols)):
                out.append([(P0, "P0"), (P1, "P1"), (PA, "PAt")][i])
            return out

        inproj_mm(66, zcols, psl(zcols), 0)
        for ci, (off, ln) in enumerate(zcols):
            pap, pkey = psl(zcols)[ci]
            src = pap[0:64, 0:ln] if pap is not PA else pap[0:64, 129:129 + ln]
            o = off - 3
            ts("vector", raw[0:64, 0:ln], src, dtb_sb[:, 0:1], None, OP.add, None,
               [pkey, "dtb"], ["spv"])
            act(xc_gS[0][0:64, 0:ln], raw[0:64, 0:ln], AF.Abs, ["spv"], ["spa"])
            act(xc_gS[0][0:64, 0:ln], xc_gS[0][0:64, 0:ln], AF.Exp, ["spa"], ["spa"], scale=-1.0)
            ts("vector", xc_gS[0][0:64, 0:ln], xc_gS[0][0:64, 0:ln], 1.0, None, OP.add, None, ["spa"], ["spa"])
            act(xc_gS[0][0:64, 0:ln], xc_gS[0][0:64, 0:ln], AF.Ln, ["spa"], ["spa"])
            ts("vector", raw[0:64, 0:ln], raw[0:64, 0:ln], 0.0, None, OP.max, None, ["spv"], ["spv"])
            tt("vector", dts_h[:, o:o + ln], xc_gS[0][0:64, 0:ln], raw[0:64, 0:ln], OP.add,
               ["spa", "spv"], ["dts"])
        ts("vector", logda[:, :], dts_h[:, :], acol[:, :], None, OP.mult, None,
           ["dts", "acol"], ["logda"])
        scan_op("vector", Dc_pad[:, 1 + hoff:1 + hoff + HT], ones[0:64, 0:HT], logda[:, :],
                ["ones", "logda"], ["dcseg"])
        if half == 1:
            tcopy("vector", bd_sb[0:64, 0:1], Dc_pad[:, hoff:hoff + 1], ["dcp"], ["dccar"])
            ts("vector", Dc_pad[:, 1 + hoff:1 + hoff + HT], Dc_pad[:, 1 + hoff:1 + hoff + HT],
               bd_sb[0:64, 0:1], None, OP.add, None, ["dcseg", "dccar"], ["dcseg"])
        pr.wmap["dcp"] = pr.wmap["dcseg"]
        pr.barrier()

        for which, mc, cvi in (("B", 64, 32), ("C", 65, 33)):
            inproj_mm(mc, xcols, psl(xcols), 0)
            for ci, (off, ln) in enumerate(xcols):
                pap, pkey = psl(xcols)[ci]
                src = pap[:, 0:ln] if pap is not PA else pap[:, 129:129 + ln]
                tcopy("vector", raw[:, off:off + ln], src, [pkey], ["raw"])
            ts("vector", xc_gS[1][:, :], raw[:, 0:HT], convw_sb[:, cvi * 4:cvi * 4 + 1],
               convb_sb[:, cvi:cvi + 1], OP.mult, OP.add, ["raw", "convw", "convb"], ["cacc1"])
            for d in range(1, 4):
                stt("vector", xc_gS[1][:, :], raw[:, d:d + HT],
                    convw_sb[:, cvi * 4 + d:cvi * 4 + d + 1],
                    xc_gS[1][:, :], OP.mult, OP.add, ["raw", "convw", "cacc1"], ["cacc1"])
            dstt = B_half[:, :] if which == "B" else C_full[:, hoff:hoff + HT]
            dkey = "Bh" if which == "B" else "Cf"
            act(dstt, xc_gS[1][:, :], AF.Silu, ["cacc1"], [dkey])

        for c in range(NCH):
            cs = hoff + c * 128
            trans(PC[:, :], B_half[:, c * 128:c * 128 + 128], ["Bh"], ["PC"])
            tcopy("vector", BT_c[c][:, :], PC[:, :], ["PC"], [f"BT{c}"])
            mm(PC[:, :], B_half[:, c * 128:c * 128 + 128], C_full[:, cs:cs + 128], True, True,
               ["Bh", "Cf"], ["PC"])
            tcopy("vector", G_c[c][:, :], PC[:, :], ["PC"], [f"G{c}"])
            trans(PC[0:128, 0:64], Dc_pad[:, 1 + cs:1 + cs + 128], ["dcp"], ["PC"], k=64)
            tcopy("vector", DcT_c[c][:, :], PC[0:128, 0:64], ["PC"], [f"DcT{c}"])
            trans(PC[0:128, 0:64], dts_h[:, c * 128:c * 128 + 128], ["dts"], ["PC"], k=64)
            tcopy("vector", dtT_c[c][:, :], PC[0:128, 0:64], ["PC"], [f"dtT{c}"])

        def emit_inproj(g):
            p = g % 2
            szg, xcg = sz_gS[p], xc_gS[p]
            kz, kx = "szg%d" % p, "cacc%d" % p
            inproj_mm(g, zcols, psl(zcols), 0)
            for ci, (off, ln) in enumerate(zcols):
                pap, pkey = psl(zcols)[ci]
                src = pap[:, 0:ln] if pap is not PA else pap[:, 129:129 + ln]
                act(szg[:, off - 3:off - 3 + ln], src, AF.Silu, [pkey, kz], [kz])
            dma("scalar", sz_s[g, :, hoff:hoff + HT], szg[:, :], [kz], ["szs%d" % g])
            if debug:
                dma("scalar", dbg["d_sz"][g, :, hoff:hoff + HT], szg[:, :], [kz], ["dbgsz"])
            inproj_mm(32 + g, xcols, psl(xcols), 0)
            for ci, (off, ln) in enumerate(xcols):
                pap, pkey = psl(xcols)[ci]
                src = pap[:, 0:ln] if pap is not PA else pap[:, 129:129 + ln]
                tcopy("vector", raw[:, off:off + ln], src, [pkey], ["raw"])
            ts("vector", xcg[:, :], raw[:, 0:HT], convw_sb[:, g * 4:g * 4 + 1],
               convb_sb[:, g:g + 1], OP.mult, OP.add, ["raw", "convw", "convb", kx], [kx])
            for d in range(1, 4):
                stt("vector", xcg[:, :], raw[:, d:d + HT], convw_sb[:, g * 4 + d:g * 4 + d + 1],
                    xcg[:, :], OP.mult, OP.add, ["raw", "convw", kx], [kx])
            act(xcg[:, :], xcg[:, :], AF.Silu, [kx], [kx])

        def emit_scan(g):
            p = g % 2
            szg, xcg = sz_gS[p], xc_gS[p]
            kz, kx = "szg%d" % p, "cacc%d" % p
            for c in range(NCH):
                cs = hoff + c * 128
                trans(PC[:, :], xcg[:, c * 128:c * 128 + 128], [kx], ["PC"])
                tcopy("vector", xT_sb[:, :], PC[:, :], ["PC"], ["xT"])
                for j in range(2):
                    hh = 2 * g + j
                    trans(PC[0:1, 0:128], DcT_c[c][:, hh:hh + 1], [f"DcT{c}"], ["PC"])
                    tcopy("vector", rowdc[0:1, 1:129], PC[0:1, 0:128], ["PC"], ["rowdc"])
                    tcopy("vector", rowdc[0:1, 0:1], rowprev[0:1, hh:hh + 1],
                          ["rowprev"], ["rowdc"])
                    tcopy("vector", rowprev[0:1, hh:hh + 1], rowdc[0:1, 128:129],
                          ["rowdc"], ["rowprev"])
                    mm(PA[:, 0:129], ones_row[:, :], rowdc[0:1, 0:129], True, True,
                       ["oner", "rowdc"], ["PAw"])
                    ts("vector", cdecB[:, :], PA[:, 1:129], PA[:, 0:1], None, OP.subtract, None,
                       ["PAw"], ["cdB"])
                    act(cdecB[:, :], cdecB[:, :], AF.Exp, ["cdB"], ["cdB"])
                    ts("vector", E_sb[:, :], PA[:, 1:129], DcT_c[c][:, hh:hh + 1], None,
                       OP.subtract, None, ["PAw", f"DcT{c}"], ["E"])
                    tt("gpsimd", E_sb[:, :], E_sb[:, :], am_sb[:, :], OP.add, ["E", "am"], ["E"])
                    act(M_sb[:, :], E_sb[:, :], AF.Exp, ["E"], ["M"])
                    ts("gpsimd", Pm_sb[:, :], G_c[c][:, :], dtT_c[c][:, hh:hh + 1], None,
                       OP.mult, None, [f"G{c}", f"dtT{c}"], ["Pm"])
                    tt("gpsimd", M_sb[:, :], M_sb[:, :], Pm_sb[:, :], OP.mult, ["M", "Pm"], ["M"])
                    tt("gpsimd", Cd_sb[:, :], C_full[:, cs:cs + 128], cdecB[:, :], OP.mult,
                       ["Cf", "cdB"], ["Cd"])
                    mm(P2[64 * j:64 * j + 64, 0:128], xT_sb[:, 64 * j:64 * j + 64], M_sb[:, :],
                       True, False, ["xT", "M"], ["P2w%d" % j])
                    mm(P2[64 * j:64 * j + 64, 0:128], ST[:, hh * 64:hh * 64 + 64], Cd_sb[:, :],
                       False, True, ["ST", "Cd"], ["P2w%d" % j])
                    ts("vector", bd_sb[:, :], PA[:, 128:129], DcT_c[c][:, hh:hh + 1], None,
                       OP.subtract, None, ["PAw", f"DcT{c}"], ["bd"])
                    act(bd_sb[:, :], bd_sb[:, :], AF.Exp, ["bd"], ["bd"])
                    ts("vector", bd_sb[:, :], bd_sb[:, :], dtT_c[c][:, hh:hh + 1], None,
                       OP.mult, None, ["bd", f"dtT{c}"], ["bd"])
                    ts("gpsimd", Bd_sb[:, :], BT_c[c][:, :], bd_sb[:, 0:1], None, OP.mult, None,
                       [f"BT{c}", "bd"], ["Bd"])
                    mm(PC[:, 0:64], Bd_sb[:, :], xT_sb[:, 64 * j:64 * j + 64], True, True,
                       ["Bd", "xT"], ["PC"])
                    stt("vector", ST[:, hh * 64:hh * 64 + 64], ST[:, hh * 64:hh * 64 + 64],
                        cdecB[:, 127:128], PC[:, 0:64], OP.mult, OP.add,
                        ["ST", "cdB", "PC"], ["ST"])
                stt("vector", tmp_sc[:, :], xcg[:, c * 128:c * 128 + 128],
                    drep_sb[:, g:g + 1], P2[:, 0:128], OP.mult, OP.add,
                    [kx, "drep", "P2w0", "P2w1"], ["tmpsc"])
                tt("gpsimd", yg_g[:, c * 128:c * 128 + 128], tmp_sc[:, :],
                   szg[:, c * 128:c * 128 + 128], OP.mult, ["tmpsc", kz], ["ygg"])
            dma("scalar", yg_s[g, :, hoff:hoff + HT], yg_g[:, :], ["ygg"], ["ygs%d" % g])
            if debug:
                dma("scalar", dbg["d_yg"][g, :, hoff:hoff + HT], yg_g[:, :], ["ygg"], ["dbgyg"])

        for g in range(33):
            if g < 32:
                emit_inproj(g)
            if g >= 1:
                emit_scan(g - 1)

    pr.barrier()
    ESAu.close()
    act(cdec[:, :], Dc_pad[:, 1:T + 1], AF.Exp, ["dcp"], ["cdec"])

    dma("scalar", cc_in.ap()[0:128, :], ST[:, :], ["ST"], ["ccin"])
    dma("scalar", cc_in.ap()[128:129, 0:64], Dc_pad[:, T:T + 1], ["dcp"], ["ccin"])
    coll("cc1", cc_in, cc_out, ["ccin"], ["ccout"])
    if debug:
        dma("scalar", dbg["d_st"][:, :], ST[:, :], ["ST"], ["dbgst"])
        dma("scalar", dbg["d_dc"][:, :], Dc_pad[:, :], ["dcp"], ["dbgdc"])
        dma("scalar", dbg["d_dts"][:, 0:HT], dts_h[:, :], ["dts"], ["dbgdts"])
    ESA2 = ExitStack()
    gat_st = sb("gat_st", [128, 4096], stack=ESA2)
    tmp_st = sb("tmp_st", [128, 4096], stack=ESA2)
    dclj = sb("dclj", [1, 64], stack=ESA2)
    grow = sb("grow", [1, 64], stack=ESA2)
    gcol_sb = sb("gcol_sb", [128, 64], stack=ESA2)
    mset("vector", S_in[:, :], 0.0, ["Sin"])
    for j in range(NCORES):
        dma("sync", gat_st[:, :], cc_out.ap()[j * 129:j * 129 + 128, :], ["ccout", "gatst"], ["gatst"])
        dma("sync", dclj[:, :], cc_out.ap()[j * 129 + 128:j * 129 + 129, 0:64], ["ccout", "dclj"], ["dclj"])
        act(grow[:, :], dclj[:, :], AF.Exp, ["dclj"], ["grow"])
        ts("vector", grow[:, :], grow[:, :], 1.0, None, OP.subtract, None, ["grow"], ["grow"])
        ts("vector", grow[:, :], grow[:, :], mask8_sb[0:1, j:j + 1], 1.0, OP.mult, OP.add,
           ["grow", "mask8"], ["grow"])
        mm(PC[:, 0:64], ones_row[:, :], grow[:, :], True, True, ["oner", "grow"], ["PC"])
        tcopy("vector", gcol_sb[:, :], PC[:, 0:64], ["PC"], ["gcol"])
        for ch in range(8):
            ts("gpsimd", tmp_st[:, ch * 512:(ch + 1) * 512], gat_st[:, ch * 512:(ch + 1) * 512],
               mask8_sb[:, j:j + 1], None, OP.mult, None, ["gatst", "mask8"], ["tmpst"])
        for h in range(64):
            stt("vector", S_in[:, h * 64:h * 64 + 64], S_in[:, h * 64:h * 64 + 64],
                gcol_sb[:, h:h + 1], tmp_st[:, h * 64:h * 64 + 64], OP.mult, OP.add,
                ["Sin", "gcol", "tmpst"], ["Sin"])
    if debug:
        dma("scalar", dbg["d_sin"][:, :], S_in[:, :], ["Sin"], ["dbgsin"])
    pr.barrier()
    ESA2.close()
    pr.barrier()
    ESAs.close()

    ES3 = ExitStack()
    ygt = [sb(f"ygt{g}", [128, 512], stack=ES3) for g in range(32)]
    szt = sb("szt", [128, 512], stack=ES3)
    CdA = sb("CdA", [128, 512], stack=ES3)
    cdB2 = sb("cdB2", [128, 512], stack=ES3)
    ohp_sb = sb("ohp_sb", [64, 4096], stack=ES3)
    nw_sb = sb("nw_sb", [128, 32], stack=ES3)
    wosl = [sb("wosl0", [128, 2048], stack=ES3)] * 2
    xs_t = [sb(f"xs3_{m}", [128, 512], stack=ES3) for m in range(16)]
    sq3 = sb("sq3", [128, 512], stack=ES3)
    rowa = sb("rowa", [1, 512], stack=ES3)
    amx3 = sb("amx3", [128, 512], stack=ES3)
    invB = sb("invB3", [128, 512], stack=ES3)
    absx = sb("absx3", [128, 512], stack=ES3)
    lyy = sb("lyy3", [128, 512], stack=ES3)
    ly0 = sb("ly03", [128, 512], stack=ES3)
    lq = sb("lq3", [128, 512], stack=ES3)
    sgn = sb("sgn3", [128, 512], stack=ES3)
    qxb = sb("qxb3", [128, 512], BF16, stack=ES3)
    dma("sync", nw_sb[:, :], nw_d[:, :], [], ["nw"])
    dma("sync", ohp_sb[:, :], ohp_d[:, :], [], ["ohp"])

    TCH = split_cols(0, T, 512)
    for ti, (t0, tc) in enumerate(TCH):
        for g in range(32):
            dma("sync", ygt[g][:, 0:tc], yg_s[g, :, t0:t0 + tc], ["ygs%d" % g, f"3y{g}"], [f"3y{g}"])
            dma("sync", szt[:, 0:tc], sz_s[g, :, t0:t0 + tc], ["szs%d" % g, "szt"], ["szt"])
            mm(P3[:, 0:tc], ohp_sb[:, g * 128:g * 128 + 128], cdec[:, t0:t0 + tc],
               True, True, ["ohp", "cdec"], ["P3"])
            tcopy("vector", cdB2[:, 0:tc], P3[:, 0:tc], ["P3"], ["cdB2"])
            for j in range(2):
                hh = 2 * g + j
                mm(P2[64 * j:64 * j + 64, 0:tc], S_in[:, hh * 64:hh * 64 + 64],
                   C_full[:, t0:t0 + tc], True, True, ["Sin", "Cf"], ["P2"])
            if debug:
                dma("scalar", dbg["d_cdb"][g, :, t0:t0 + tc], cdB2[:, 0:tc],
                    ["cdB2"], ["dbgcdb"])
                tcopy("vector", CdA[:, 0:tc], P2[:, 0:tc], ["P2"], ["CdA"])
                dma("scalar", dbg["d_p2r"][g, :, t0:t0 + tc], CdA[:, 0:tc],
                    ["CdA"], ["dbgp2r"])
            tt("vector", CdA[:, 0:tc], P2[:, 0:tc], cdB2[:, 0:tc], OP.mult,
               ["P2", "cdB2"], ["CdA"])
            tt("gpsimd", CdA[:, 0:tc], CdA[:, 0:tc], szt[:, 0:tc], OP.mult,
               ["CdA", "szt"], ["CdA"])
            if debug:
                dma("scalar", dbg["d_cda"][g, :, t0:t0 + tc], CdA[:, 0:tc],
                    ["CdA"], ["dbgcda"])
            tt("gpsimd", ygt[g][:, 0:tc], ygt[g][:, 0:tc], CdA[:, 0:tc], OP.add,
               [f"3y{g}", "CdA"], [f"3y{g}"])
            if debug:
                dma("scalar", dbg["d_ygt"][g, :, t0:t0 + tc], ygt[g][:, 0:tc],
                    [f"3y{g}"], ["dbgygt"])
            act(sq3[:, 0:tc], ygt[g][:, 0:tc], AF.Square, [f"3y{g}"], ["sq3"])
            mm(PR1[0:1, 0:tc], ones_col[:, :], sq3[:, 0:tc], g == 0, g == 31,
               ["onec", "sq3"], ["PR1"])
        act(rowa[0:1, 0:tc], PR1[0:1, 0:tc], AF.Sqrt, ["PR1"], ["rmsr"],
            bias=1e-5, scale=1.0 / 4096.0)
        recip(rowa[0:1, 0:tc], rowa[0:1, 0:tc], ["rmsr"], ["rmsr"])
        mm(P3[:, 0:tc], ones_row[:, :], rowa[0:1, 0:tc], True, True, ["oner", "rmsr"], ["P3"])
        for g in range(32):
            ts("gpsimd", ygt[g][:, 0:tc], ygt[g][:, 0:tc], nw_sb[:, g:g + 1], None, OP.mult,
               None, [f"3y{g}", "nw"], [f"3y{g}"])
            tt("vector", ygt[g][:, 0:tc], ygt[g][:, 0:tc], P3[:, 0:tc], OP.mult,
               [f"3y{g}", "P3"], [f"3y{g}"])
        for m in range(16):
            pq_ = PQ[m % 4]
            pk = "P%d" % (m % 4)
            for h in range(2):
                dma("sync", wosl[0][:, :], w_out_p[m, :, h * 2048:h * 2048 + 2048],
                    ["wo0"], ["wo0"])
                for k in range(16):
                    mm(pq_[:, 0:tc], wosl[0][:, k * 128:k * 128 + 128],
                       ygt[h * 16 + k][:, 0:tc], h == 0 and k == 0, h == 1 and k == 15,
                       ["wo0", f"3y{h * 16 + k}"], [pk])
            dma("sync", xs_t[m][:, 0:tc], uT[m * 128:m * 128 + 128, 3 + t0:3 + t0 + tc],
                [f"3xs{m}"], [f"3xs{m}"])
            tt("vector", xs_t[m][:, 0:tc], xs_t[m][:, 0:tc], pq_[:, 0:tc], OP.add,
               [f"3xs{m}", pk], [f"3xs{m}"])
        layer_norm(xs_t, [f"3xs{m}" for m in range(16)], tc)
        for m in range(16):
            dma("scalar", x1_s[m, :, t0:t0 + tc], xs_t[m][:, 0:tc], [f"3xs{m}"], ["x1s%d" % m])
        for m in range(16):
            act(absx[:, 0:tc], xs_t[m][:, 0:tc], AF.Abs, [f"3xs{m}"], ["la"])
            if m == 0:
                tcopy("vector", amx3[:, 0:tc], absx[:, 0:tc], ["la"], ["amx3"])
            else:
                tt("vector", amx3[:, 0:tc], amx3[:, 0:tc], absx[:, 0:tc], OP.max,
                   ["amx3", "la"], ["amx3"])
        tree_max(amx3, "amx3", sq3, "sq3", tc)
        ts("vector", amx3[0:1, 0:tc], amx3[0:1, 0:tc], 1.0 / 127.0, 1e-8, OP.mult, OP.max,
           ["amx3"], ["amx3"])
        mm(P2[:, 0:tc], ones_row[:, :], amx3[0:1, 0:tc], True, True, ["oner", "amx3"], ["P2"])
        tcopy("vector", sxT[:, t0:t0 + tc], P2[:, 0:tc], ["P2"], ["sxs"])
        recip(invB[:, 0:tc], sxT[:, t0:t0 + tc], ["sxs"], ["invB"])
        for m in range(16):
            eng = VG[m % 2]
            act(absx[:, 0:tc], xs_t[m][:, 0:tc], AF.Abs, [f"3xs{m}"], ["la"])
            act(sgn[:, 0:tc], xs_t[m][:, 0:tc], AF.Sign, [f"3xs{m}"], ["ls"])
            tt(eng, lyy[:, 0:tc], absx[:, 0:tc], invB[:, 0:tc], OP.mult, ["la", "invB"], ["ly"])
            ladder_round("l", lyy[:, 0:tc], lyy[:, 0:tc], ly0[:, 0:tc], lq[:, 0:tc], tc,
                         [64, 32, 16, 8, 4, 2, 1], "vector", ["ly"])
            tt(eng, lq[:, 0:tc], lq[:, 0:tc], sgn[:, 0:tc], OP.mult, ["lq", "ls"], ["lq"])
            tcopy(eng, qxb[:, 0:tc], lq[:, 0:tc], ["lq"], ["qxb"])
            dma("scalar", qx_s[m, :, t0:t0 + tc], qxb[:, 0:tc], ["qxb"], ["qxs%d" % m])
            if debug:
                dma("scalar", dbg["d_qx"][m, :, t0:t0 + tc], lq[:, 0:tc], ["lq"], ["dbgqx"])
        if debug:
            for m in range(16):
                dma("scalar", dbg["d_x1"][m, :, t0:t0 + tc], xs_t[m][:, 0:tc],
                    [f"3xs{m}"], ["dbg1"])
    pr.barrier()
    ES3.close()
    pr.barrier()
    ESL0.close()

    ESB15 = ExitStack()
    HW = 2816
    wdsl = [sb(f"wdsl{h}", [128, HW], stack=ESB15) for h in range(2)]
    wcol = sb("wcol", [128, 1], stack=ESB15)
    wcol2 = sb("wcol2", [128, 1], stack=ESB15)
    winv = sb("winv", [128, 1], stack=ESB15)
    wyS = [sb(f"wyt{i}", [128, HW], stack=ESB15) for i in range(2)]
    wy0S = [sb(f"wy0t{i}", [128, HW], stack=ESB15) for i in range(2)]
    wqS = [sb(f"wqt{i}", [128, HW], stack=ESB15) for i in range(2)]
    wsgnS = [sb(f"wsgnt{i}", [128, HW], stack=ESB15) for i in range(2)]
    q2sl = sb("q2sl", [128, 5632], BF16, stack=ESB15)
    for m in range(16):
        eng = VG[m % 2]
        for h in range(2):
            dma("sync", wdsl[h][:, :], w_down_p[m, :, h * HW:h * HW + HW],
                ["wdsl%d" % h], ["wdsl%d" % h])
        redmax_free(eng, wcol[:, :], wdsl[0][:, :], ["wdsl0", "wcol"], ["wcol"], absval=True)
        redmax_free(eng, wcol2[:, :], wdsl[1][:, :], ["wdsl1", "wcol2"], ["wcol2"], absval=True)
        tt("vector", wcol[:, :], wcol[:, :], wcol2[:, :], OP.max, ["wcol", "wcol2"], ["wcol"])
        ts("vector", wcol[:, :], wcol[:, :], 1.0 / 127.0, 1e-8, OP.mult, OP.max, ["wcol"], ["wcol"])
        tcopy("vector", s2all[:, m:m + 1], wcol[:, :], ["wcol"], ["s2all"])
        recip(winv[:, :], wcol[:, :], ["wcol", "winv"], ["winv"])
        for h in range(2):
            pp = (2 * m + h) % 2
            wy, wy0, wq, wsgn = wyS[pp], wy0S[pp], wqS[pp], wsgnS[pp]
            pf = "w%d" % pp
            act(wy[:, :], wdsl[h][:, :], AF.Abs, ["wdsl%d" % h, pf + "y"], [pf + "y"])
            act(wsgn[:, :], wdsl[h][:, :], AF.Sign, ["wdsl%d" % h, pf + "s"], [pf + "s"])
            ts("vector", wy[:, :], wy[:, :], winv[:, :], None, OP.mult, None,
               [pf + "y", "winv"], [pf + "y"])
            ladder_round(pf, wy[:, :], wy[:, :], wy0[:, :], wq[:, :], HW,
                         [64, 32, 16, 8, 4, 2, 1], "vector", [pf + "y"])
            tt(eng, wq[:, :], wq[:, :], wsgn[:, :], OP.mult, [pf + "q", pf + "s"], [pf + "q"])
            for kt in range(22):
                gkt = h * 22 + kt
                trans(PC[:, :], wq[:, kt * 128:kt * 128 + 128], [pf + "q"], ["PC"])
                tcopy("vector", q2sl[:, gkt * 128:gkt * 128 + 128], PC[:, :], ["PC"], ["q2sl"])
        dma("scalar", q2L0_s[m], q2sl[:, :], ["q2sl"], ["q2L0"])
    pr.barrier()
    ESB15.close()

    ESB = ExitStack()
    qx_t = [sb(f"qx{k}", [128, T], BF16, stack=ESB) for k in range(16)]
    for k in range(16):
        dma("sync", qx_t[k][:, :], qx_s[k], ["qxs%d" % k, f"qx{k}"], [f"qx{k}"])
    wgsl = sb("wgsl", [128, 2048], stack=ESB)
    wcol = sb("wcolB", [128, 1], stack=ESB)
    winv = sb("winvB", [128, 1], stack=ESB)
    gyS = [sb(f"gyt{i}", [128, 2048], stack=ESB) for i in range(2)]
    gy0S = [sb(f"gy0t{i}", [128, 2048], stack=ESB) for i in range(2)]
    gqS = [sb(f"gqt{i}", [128, 2048], stack=ESB) for i in range(2)]
    gsgnS = [sb(f"gsgnt{i}", [128, 2048], stack=ESB) for i in range(2)]
    s1g = sb("s1g", [128, 1], stack=ESB)
    s1u = sb("s1u", [128, 1], stack=ESB)
    q1g = sb("q1g", [128, 2048], BF16, stack=ESB)
    q1u = sb("q1u", [128, 2048], BF16, stack=ESB)
    gt_sb = sb("gt_sb", [128, 512], stack=ESB)
    ut_sb = sb("ut_sb", [128, 512], stack=ESB)

    def quant_gup(mc, scol, skey, qdst, qkey):
        pp = mc % 2
        gy, gy0, gq, gsgn = gyS[pp], gy0S[pp], gqS[pp], gsgnS[pp]
        pf = "g%d" % pp
        dma("sync", wgsl[:, :], w_gup_p[mc], ["wgsl"], ["wgsl"])
        redmax_free("vector", wcol[:, :], wgsl[:, :], ["wgsl", "wcol"], ["wcol"], absval=True)
        ts("vector", scol, wcol[:, :], 1.0 / 127.0, 1e-8, OP.mult, OP.max, ["wcol"], [skey])
        recip(winv[:, :], scol, [skey, "winv"], ["winv"])
        act(gy[:, :], wgsl[:, :], AF.Abs, ["wgsl", pf + "y"], [pf + "y"])
        act(gsgn[:, :], wgsl[:, :], AF.Sign, ["wgsl", pf + "s"], [pf + "s"])
        ts("vector", gy[:, :], gy[:, :], winv[:, :], None, OP.mult, None,
           [pf + "y", "winv"], [pf + "y"])
        ladder_round(pf, gy[:, :], gy[:, :], gy0[:, :], gq[:, :], 2048,
                     [64, 32, 16, 8, 4, 2, 1], "vector", [pf + "y"])
        tt("gpsimd", gq[:, :], gq[:, :], gsgn[:, :], OP.mult, [pf + "q", pf + "s"], [pf + "q"])
        for kt in range(16):
            trans(PC[:, :], gq[:, kt * 128:kt * 128 + 128], [pf + "q"], ["PC"])
            tcopy("vector", qdst[:, kt * 128:kt * 128 + 128], PC[:, :], ["PC"], [qkey])

    for p in range(44):
        quant_gup(p, s1g[:, :], "s1gk", q1g, "q1gk")
        quant_gup(44 + p, s1u[:, :], "s1uk", q1u, "q1uk")
        if debug and p == 0:
            dma("scalar", dbg["d_q1"][0], q1g[:, :], ["q1gk"], ["dbgq1"])
            dma("scalar", dbg["d_q1"][1], q1u[:, :], ["q1uk"], ["dbgq1"])
            dma("scalar", dbg["d_s1"][0:1, :], s1g[:, :], ["s1gk"], ["dbgs1"])
            dma("scalar", dbg["d_s1"][1:2, :], s1u[:, :], ["s1uk"], ["dbgs1"])
        for ti, (t0, tc) in enumerate(TCH):
            for kt in range(16):
                mm(P0[:, 0:tc], q1g[:, kt * 128:kt * 128 + 128], qx_t[kt][:, t0:t0 + tc],
                   kt == 0, kt == 15, ["q1gk", f"qx{kt}"], ["P0"])
            for kt in range(16):
                mm(P1[:, 0:tc], q1u[:, kt * 128:kt * 128 + 128], qx_t[kt][:, t0:t0 + tc],
                   kt == 0, kt == 15, ["q1uk", f"qx{kt}"], ["P1"])
            stt("vector", gt_sb[:, 0:tc], P0[:, 0:tc], s1g[:, :], sxT[:, t0:t0 + tc],
                OP.mult, OP.mult, ["P0", "s1gk", "sxs"], ["gt"])
            stt("vector", ut_sb[:, 0:tc], P1[:, 0:tc], s1u[:, :], sxT[:, t0:t0 + tc],
                OP.mult, OP.mult, ["P1", "s1uk", "sxs"], ["ut"])
            if debug and p == 0 and ti == 0:
                dma("scalar", dbg["d_gu"][0, :, 0:tc], gt_sb[:, 0:tc], ["gt"], ["dbggu"])
                dma("scalar", dbg["d_gu"][1, :, 0:tc], ut_sb[:, 0:tc], ["ut"], ["dbggu"])
            act(gt_sb[:, 0:tc], gt_sb[:, 0:tc], AF.Silu, ["gt"], ["gt"])
            tt("gpsimd", gt_sb[:, 0:tc], gt_sb[:, 0:tc], ut_sb[:, 0:tc], OP.mult,
               ["gt", "ut"], ["gt"])
            dma("scalar", h0_s[p, :, t0:t0 + tc], gt_sb[:, 0:tc], ["gt"], ["h0s%d" % p])
            if debug:
                dma("scalar", dbg["d_h"][p, :, t0:t0 + tc], gt_sb[:, 0:tc], ["gt"], ["dbgh"])
    pr.barrier()
    ESB.close()

    ESB2 = ExitStack()
    ht_t = [sb(f"ht{p}", [128, 256], stack=ESB2) for p in range(44)]
    hq_t = [sb(f"hq{p}", [128, 256], BF16, stack=ESB2) for p in range(44)]
    q2dl = [sb(f"q2dl{i}", [128, 5632], BF16, stack=ESB2) for i in range(2)]
    x2t = [sb(f"x2t{m}", [128, 256], stack=ESB2) for m in range(16)]
    b2hm = sb("b2hm", [128, 256], stack=ESB2)
    b2inv = sb("b2inv", [128, 256], stack=ESB2)
    b2aS = [sb(f"b2at{i}", [128, 256], stack=ESB2) for i in range(2)]
    b2sS = [sb(f"b2st{i}", [128, 256], stack=ESB2) for i in range(2)]
    b2yS = [sb(f"b2yt{i}", [128, 256], stack=ESB2) for i in range(2)]
    b2y0S = [sb(f"b2y0t{i}", [128, 256], stack=ESB2) for i in range(2)]
    b2qS = [sb(f"b2qt{i}", [128, 256], stack=ESB2) for i in range(2)]
    b2sq = sb("b2sq", [128, 256], stack=ESB2)
    shB = sb("shB", [128, 256], stack=ESB2)
    TC2 = split_cols(0, T, 256)
    for ti, (t0, tc) in enumerate(TC2):
        for p in range(44):
            dma("sync", ht_t[p][:, 0:tc], h0_s[p, :, t0:t0 + tc], ["h0s%d" % p, f"ht{p}"], [f"ht{p}"])
            act(b2aS[p % 2][:, 0:tc], ht_t[p][:, 0:tc], AF.Abs,
                [f"ht{p}", "b2%da" % (p % 2)], ["b2%da" % (p % 2)])
            if p == 0:
                tcopy("vector", b2hm[:, 0:tc], b2aS[0][:, 0:tc], ["b20a"], ["b2hm"])
            else:
                tt("vector", b2hm[:, 0:tc], b2hm[:, 0:tc], b2aS[p % 2][:, 0:tc], OP.max,
                   ["b2hm", "b2%da" % (p % 2)], ["b2hm"])
        tree_max(b2hm, "b2hm", b2yS[0], "b20y", tc)
        ts("vector", b2hm[0:1, 0:tc], b2hm[0:1, 0:tc], 1.0 / 127.0, 1e-8, OP.mult, OP.max,
           ["b2hm"], ["b2hm"])
        mm(P2[:, 0:tc], ones_row[:, :], b2hm[0:1, 0:tc], True, True, ["oner", "b2hm"], ["P2"])
        tcopy("vector", shB[:, 0:tc], P2[:, 0:tc], ["P2"], ["shB"])
        recip(b2inv[:, 0:tc], shB[:, 0:tc], ["shB"], ["b2invB"])
        for p in range(44):
            pp = p % 2
            pf = "b2%d" % pp
            ba, bs_, by, by0, bq_ = b2aS[pp], b2sS[pp], b2yS[pp], b2y0S[pp], b2qS[pp]
            act(ba[:, 0:tc], ht_t[p][:, 0:tc], AF.Abs, [f"ht{p}", pf + "a"], [pf + "a"])
            act(bs_[:, 0:tc], ht_t[p][:, 0:tc], AF.Sign, [f"ht{p}", pf + "s"], [pf + "s"])
            tt("gpsimd", by[:, 0:tc], ba[:, 0:tc], b2inv[:, 0:tc], OP.mult,
               [pf + "a", "b2invB", pf + "y"], [pf + "y"])
            ladder_round(pf, by[:, 0:tc], by[:, 0:tc], by0[:, 0:tc], bq_[:, 0:tc], tc,
                         [64, 32, 16, 8, 4, 2, 1], "vector", [pf + "y"])
            tt("gpsimd", bq_[:, 0:tc], bq_[:, 0:tc], bs_[:, 0:tc], OP.mult,
               [pf + "q", pf + "s"], [pf + "q"])
            tcopy("vector", hq_t[p][:, 0:tc], bq_[:, 0:tc], [pf + "q"], [f"hq{p}"])
        for m in range(16):
            dma("sync", q2dl[m % 2][:, :], q2L0_s[m], ["q2L0", "q2d%d" % (m % 2)], ["q2d%d" % (m % 2)])
            pq_ = PQ[m % 4]
            pk = "P%d" % (m % 4)
            for kt in range(44):
                mm(pq_[:, 0:tc], q2dl[m % 2][:, kt * 128:kt * 128 + 128], hq_t[kt][:, 0:tc],
                   kt == 0, kt == 43, ["q2d%d" % (m % 2), f"hq{kt}"], [pk])
            dma("sync", x2t[m][:, 0:tc], x1_s[m, :, t0:t0 + tc], ["x1s%d" % m, f"B2x{m}"], [f"B2x{m}"])
            stt("vector", b2sq[:, 0:tc], pq_[:, 0:tc], s2all[:, m:m + 1], shB[:, 0:tc],
                OP.mult, OP.mult, [pk, "s2all", "shB"], ["b2o"])
            tt("vector", x2t[m][:, 0:tc], x2t[m][:, 0:tc], b2sq[:, 0:tc], OP.add,
               [f"B2x{m}", "b2o"], [f"B2x{m}"])
        layer_norm(x2t, [f"B2x{m}" for m in range(16)], tc)
        for m in range(16):
            dma("scalar", x2_s[m, :, t0:t0 + tc], x2t[m][:, 0:tc], [f"B2x{m}"], ["x2s%d" % m])
            if debug:
                dma("scalar", dbg["d_x2"][m, :, t0:t0 + tc], x2t[m][:, 0:tc],
                    [f"B2x{m}"], ["dbg2"])
    pr.barrier()
    ESB2.close()

    ESC = ExitStack()
    x2c = [sb(f"x2c{k}", [128, 512], stack=ESC) for k in range(16)]
    wsl2 = [sb(f"wsl2_{i}", [128, 2048], stack=ESC) for i in range(2)]
    bq_sb = sb("bq_sb", [128, 16], stack=ESC)
    bk_sb = sb("bk_sb", [128, 16], stack=ESC)
    bv_sb = sb("bv_sb", [128, 16], stack=ESC)
    bo_sb = sb("bo_sb", [128, 16], stack=ESC)
    carry = sb("carry", [128, 16], stack=ESC)
    kmin = sb("kmin", [128, 512], stack=ESC)
    ek = sb("ek", [128, 512], stack=ESC)
    krel = sb("krel", [128, 512], stack=ESC)
    pkt = sb("pkt", [128, 512], stack=ESC)
    vt_sb = sb("vt_sb", [128, 512], stack=ESC)
    kvt = sb("kvt", [128, 512], stack=ESC)
    dma("sync", bq_sb[:, :], bq_d[:, :], [], ["bq"])
    dma("sync", bk_sb[:, :], bk_d[:, :], [], ["bk"])
    dma("sync", bv_sb[:, :], bv_d[:, :], [], ["bv"])
    dma("sync", bo_sb[:, :], bo_d[:, :], [], ["bo"])
    mset("vector", carry[:, :], 0.0, ["carry"])

    def phi_from_psum(pap, pk, bcol, bkey, out, okey):
        ts("vector", kmin[:, 0:tc], pap[:, 0:tc], bcol, 0.0, OP.add, OP.min,
           [pk, bkey], ["kmin"])
        act(ek[:, 0:tc], kmin[:, 0:tc], AF.Exp, ["kmin"], ["ek"])
        ts("vector", krel[:, 0:tc], pap[:, 0:tc], bcol, 0.0, OP.add, OP.max,
           [pk, bkey], ["krel"])
        tt("gpsimd", out[:, 0:tc], ek[:, 0:tc], krel[:, 0:tc], OP.add, ["ek", "krel"], [okey])

    for ti, (t0, tc) in enumerate(TCH):
        for k in range(16):
            dma("sync", x2c[k][:, 0:tc], x2_s[k, :, t0:t0 + tc], ["x2s%d" % k, f"x2c{k}"], [f"x2c{k}"])
        for m in range(16):
            wb = m % 2
            dma("sync", wsl2[wb][:, :], wk_p[m], ["wsl2_%d" % wb], ["wsl2_%d" % wb])
            for kt in range(16):
                mm(P0[:, 0:tc], wsl2[wb][:, kt * 128:kt * 128 + 128], x2c[kt][:, 0:tc],
                   kt == 0, kt == 15, ["wsl2_%d" % wb, f"x2c{kt}"], ["P0"])
            phi_from_psum(P0, "P0", bk_sb[:, m:m + 1], "bk", pkt, "pkt")
            dma("sync", wsl2[wb][:, :], wv_p[m], ["wsl2_%d" % wb], ["wsl2_%d" % wb])
            for kt in range(16):
                mm(P1[:, 0:tc], wsl2[wb][:, kt * 128:kt * 128 + 128], x2c[kt][:, 0:tc],
                   kt == 0, kt == 15, ["wsl2_%d" % wb, f"x2c{kt}"], ["P1"])
            ts("vector", vt_sb[:, 0:tc], P1[:, 0:tc], bv_sb[:, m:m + 1], None, OP.add, None,
               ["P1", "bv"], ["vt"])
            tt("gpsimd", pkt[:, 0:tc], pkt[:, 0:tc], vt_sb[:, 0:tc], OP.mult,
               ["pkt", "vt"], ["pkt"])
            scan_op("vector", kvt[:, 0:tc], ones[:, 0:tc], pkt[:, 0:tc], ["ones", "pkt"], ["kvt"])
            ts("gpsimd", kvt[:, 0:tc], kvt[:, 0:tc], carry[:, m:m + 1], None, OP.add, None,
               ["kvt", "carry"], ["kvt"])
            tcopy("vector", carry[:, m:m + 1], kvt[:, tc - 1:tc], ["kvt", "carry"], ["carry"])
            dma("scalar", kv_s[m, :, t0:t0 + tc], kvt[:, 0:tc], ["kvt"], ["kvs%d" % m])
            if debug:
                dma("scalar", dbg["d_kv"][m, :, t0:t0 + tc], kvt[:, 0:tc], ["kvt"], ["dbgkv"])
            dma("sync", wsl2[wb][:, :], wq_p[m], ["wsl2_%d" % wb], ["wsl2_%d" % wb])
            for kt in range(16):
                mm(P2[:, 0:tc], wsl2[wb][:, kt * 128:kt * 128 + 128], x2c[kt][:, 0:tc],
                   kt == 0, kt == 15, ["wsl2_%d" % wb, f"x2c{kt}"], ["P2"])
            phi_from_psum(P2, "P2", bq_sb[:, m:m + 1], "bq", pkt, "pkt")
            dma("scalar", pq_s[m, :, t0:t0 + tc], pkt[:, 0:tc], ["pkt"], ["pqs%d" % m])

    ct2_sb = sb("ct2_sb", [16, 128], stack=ESC)
    gat2_sb = sb("gat2_sb", [128, 128], stack=ESC)
    offc_sb = sb("offc_sb", [128, 16], stack=ESC)
    trans(PC[0:16, 0:128], carry[:, :], ["carry"], ["PC"])
    tcopy("vector", ct2_sb[:, :], PC[0:16, 0:128], ["PC"], ["ct2"])
    dma("scalar", cc2_in.ap()[:, :], ct2_sb[:, :], ["ct2"], ["cc2i"])
    coll("cc2", cc2_in, cc2_out, ["cc2i"], ["cc2o"])
    dma("sync", gat2_sb[:, :], cc2_out.ap()[:, :], ["cc2o"], ["gat2"])
    mm(PC[0:16, 0:128], maskexp_sb[:, :], gat2_sb[:, :], True, True, ["maskexp", "gat2"], ["PC"])
    tcopy("vector", ct2_sb[:, :], PC[0:16, 0:128], ["PC"], ["off1"])
    trans(PC[:, 0:16], ct2_sb[:, :], ["off1"], ["PC"], k=16)
    tcopy("vector", offc_sb[:, :], PC[:, 0:16], ["PC"], ["offc"])

    kvc = [sb(f"kvc{k}", [128, 512], stack=ESC) for k in range(16)]
    x3h_t = sb("x3h_t", [128, 512], BF16, stack=ESC)
    x3l_t = sb("x3l_t", [128, 512], BF16, stack=ESC)
    c2r = [sb(f"c2r{i}", [1, 512], stack=ESC) for i in range(3)]
    c2sq = sb("c2sq", [128, 512], stack=ESC)
    for ti, (t0, tc) in enumerate(TCH):
        for k in range(16):
            dma("sync", kvc[k][:, 0:tc], kv_s[k, :, t0:t0 + tc], ["kvs%d" % k, f"kvc{k}"], [f"kvc{k}"])
            ts("vector", kvc[k][:, 0:tc], kvc[k][:, 0:tc], offc_sb[:, k:k + 1], SCALE_L1,
               OP.add, OP.mult, [f"kvc{k}", "offc"], [f"kvc{k}"])
            dma("sync", x2c[k][:, 0:tc], pq_s[k, :, t0:t0 + tc], ["pqs%d" % k, f"x2c{k}"], [f"x2c{k}"])
            tt("gpsimd", kvc[k][:, 0:tc], kvc[k][:, 0:tc], x2c[k][:, 0:tc], OP.mult,
               [f"kvc{k}", f"x2c{k}"], [f"kvc{k}"])
        for m in range(16):
            wb = m % 2
            dma("sync", wsl2[wb][:, :], wo_p[m], ["wsl2_%d" % wb], ["wsl2_%d" % wb])
            pq_ = PQ[m % 4]
            pk = "P%d" % (m % 4)
            for kt in range(16):
                mm(pq_[:, 0:tc], wsl2[wb][:, kt * 128:kt * 128 + 128], kvc[kt][:, 0:tc],
                   kt == 0, kt == 15, ["wsl2_%d" % wb, f"kvc{kt}"], [pk])
            dma("sync", x2c[m][:, 0:tc], x2_s[m, :, t0:t0 + tc], ["x2s%d" % m, f"x2c{m}"], [f"x2c{m}"])
            ts("vector", c2sq[:, 0:tc], pq_[:, 0:tc], bo_sb[:, m:m + 1], None, OP.add, None,
               [pk, "bo"], ["c2o"])
            tt("vector", x2c[m][:, 0:tc], x2c[m][:, 0:tc], c2sq[:, 0:tc], OP.add,
               [f"x2c{m}", "c2o"], [f"x2c{m}"])
        layer_norm(x2c, [f"x2c{m}" for m in range(16)], tc)
        for m in range(16):
            dma("scalar", x3_s[m, :, t0:t0 + tc], x2c[m][:, 0:tc], [f"x2c{m}"], ["x3s%d" % m])
            tcopy("vector", x3h_t[:, 0:tc], x2c[m][:, 0:tc], [f"x2c{m}"], ["x3ht"])
            tt("vector", x3l_t[:, 0:tc], x2c[m][:, 0:tc], x3h_t[:, 0:tc], OP.subtract,
               [f"x2c{m}", "x3ht"], ["x3lt"])
            dma("scalar", x3h_s[m, :, t0:t0 + tc], x3h_t[:, 0:tc], ["x3ht"], ["x3hs%d" % m])
            dma("scalar", x3l_s[m, :, t0:t0 + tc], x3l_t[:, 0:tc], ["x3lt"], ["x3ls%d" % m])
            if debug:
                dma("scalar", dbg["d_x3"][m, :, t0:t0 + tc], x2c[m][:, 0:tc],
                    [f"x2c{m}"], ["dbg3"])
    pr.barrier()
    ESC.close()

    ESX = ExitStack()
    xmT = sb("xmT", [128, T], stack=ESX)
    ESDw = ExitStack()
    s1c = sb("s1c", [128, 1], stack=ESDw)
    th1c = sb("th1c", [128, 1], stack=ESDw)
    s2c_ = sb("s2c_", [128, 1], stack=ESDw)
    th2c = sb("th2c", [128, 1], stack=ESDw)

    ESD0 = ExitStack()
    djunk = sb("djunk", [128, 8192], stack=ESD0)
    dacc1 = sb("dacc1", [128, 1], stack=ESD0)
    dacc2 = sb("dacc2", [128, 1], stack=ESD0)
    dtmp = sb("dtmp", [128, 1], stack=ESD0)
    cfsl = [sb(f"cfsl{i}", [128, 2048], stack=ESD0) for i in range(2)]
    cpsl = sb("cpsl", [128, 8192], stack=ESD0)
    dsg = sb("dsg", [128, 8192], stack=ESD0)
    dqb = sb("dqb", [128, 8192], BF16, stack=ESD0)
    mset("vector", dacc1[:, :], 0.0, ["dacc1"])
    mset("vector", dacc2[:, :], 0.0, ["dacc2"])
    for mc in range(64):
        wb = mc % 2
        dma("sync", cfsl[wb][:, :], cfc_p[mc], ["cfsl%d" % wb], ["cfsl%d" % wb])
        act(djunk[:, 0:2048], cfsl[wb][:, :], AF.Abs, ["cfsl%d" % wb, "djunk", "dtmp"],
            ["djunk", "dtmp"], accum=dtmp[:, :])
        tt("vector", dacc1[:, :], dacc1[:, :], dtmp[:, :], OP.add, ["dacc1", "dtmp"], ["dacc1"])
    for mc in range(16):
        dma("sync", cpsl[:, :], cprT_p[mc], ["cpsl"], ["cpsl"])
        act(djunk[:, :], cpsl[:, :], AF.Abs, ["cpsl", "djunk", "dtmp"], ["djunk", "dtmp"],
            accum=dtmp[:, :])
        tt("vector", dacc2[:, :], dacc2[:, :], dtmp[:, :], OP.add, ["dacc2", "dtmp"], ["dacc2"])
    for (dac, dk, cA, ck, tA, tk) in ((dacc1, "dacc1", s1c, "s1c", th1c, "th1"),
                                      (dacc2, "dacc2", s2c_, "s2c", th2c, "th2")):
        mm(PR1[0:1, 0:1], ones_col[:, :], dac[:, 0:1], True, True, ["onec", dk], ["PR1"])
        tcopy("vector", dac[0:1, 0:1], PR1[0:1, 0:1], ["PR1", dk], [dk])
        ts("vector", dac[0:1, 0:1], dac[0:1, 0:1], 1.0 / (2048.0 * 8192.0), 1e-5,
           OP.mult, OP.max, [dk], [dk])
        mm(PC[:, 0:1], ones_row[:, :], dac[0:1, 0:1], True, True, ["oner", dk], ["PC"])
        tcopy("vector", cA[:, :], PC[:, 0:1], ["PC"], [ck])
        ts("vector", tA[:, :], cA[:, :], 0.5, None, OP.mult, None, [ck], [tk])
    tcopy("vector", s2cL1[:, :], s2c_[:, :], ["s2c"], ["s2cl1"])

    def ternary(slab, skey, jnk, jkey, sgnb, sgkey, qout, qkey, th, thkey, n, eng):
        act(jnk[:, 0:n], slab, AF.Abs, [skey, jkey], [jkey])
        act(sgnb[:, 0:n], slab, AF.Sign, [skey, sgkey], [sgkey])
        ts("vector", jnk[:, 0:n], jnk[:, 0:n], th, None, OP.is_gt, None,
           [jkey, thkey], [jkey])
        tt(eng, jnk[:, 0:n], jnk[:, 0:n], sgnb[:, 0:n], OP.mult, [jkey, sgkey], [jkey])
        tcopy(eng, qout[:, 0:n], jnk[:, 0:n], [jkey], [qkey])

    for mc in range(16):
        dma("sync", cpsl[:, :], cprT_p[mc], ["cpsl"], ["cpsl"])
        ternary(cpsl[:, :], "cpsl", djunk, "djunk", dsg, "dsg", dqb, "dqb",
                th2c[:, :], "th2", 8192, VG[mc % 2])
        dma("scalar", q2L1_s[mc], dqb[:, :], ["dqb"], ["q2L1"])
    pr.barrier()
    ESD0.close()

    ESD1 = ExitStack()
    x3h_r = [sb(f"x3hr{k}", [128, T], BF16, stack=ESD1) for k in range(16)]
    x3l_r = [sb(f"x3lr{k}", [128, T], BF16, stack=ESD1) for k in range(16)]
    for k in range(16):
        dma("sync", x3h_r[k][:, :], x3h_s[k], ["x3hs%d" % k, f"x3hr{k}"], [f"x3hr{k}"])
        dma("sync", x3l_r[k][:, :], x3l_s[k], ["x3ls%d" % k, f"x3lr{k}"], [f"x3lr{k}"])
    cfsl = [sb(f"cfslB{i}", [128, 2048], stack=ESD1) for i in range(2)]
    dj2 = sb("dj2", [128, 2048], stack=ESD1)
    dsg2 = sb("dsg2", [128, 2048], stack=ESD1)
    dqb2 = sb("dqb2", [128, 2048], BF16, stack=ESD1)
    drh = sb("drh", [128, 512], stack=ESD1)
    drt = sb("drt", [128, 512], stack=ESD1)
    mset("vector", xmT[:, :], 0.0, ["rmaxr"])
    for mc in range(64):
        wb = mc % 2
        dma("sync", cfsl[wb][:, :], cfc_p[mc], ["cfsl%d" % wb], ["cfsl%d" % wb])
        ternary(cfsl[wb][:, :], "cfsl%d" % wb, dj2, "dj2", dsg2, "dsg2", dqb2, "dqb2",
                th1c[:, :], "th1", 2048, VG[mc % 2])
        for ti, (t0, tc) in enumerate(TCH):
            pq_ = PQ[ti % 4]
            pk = "P%d" % (ti % 4)
            for kt in range(16):
                mm(pq_[:, 0:tc], dqb2[:, kt * 128:kt * 128 + 128], x3h_r[kt][:, t0:t0 + tc],
                   kt == 0, False, ["dqb2", f"x3hr{kt}"], [pk])
            for kt in range(16):
                mm(pq_[:, 0:tc], dqb2[:, kt * 128:kt * 128 + 128], x3l_r[kt][:, t0:t0 + tc],
                   False, kt == 15, ["dqb2", f"x3lr{kt}"], [pk])
            ts("vector", drh[:, 0:tc], pq_[:, 0:tc], s1c[:, :], 0.0, OP.mult, OP.max,
               [pk, "s1c"], ["drh"])
            act(drt[:, 0:tc], drh[:, 0:tc], AF.Square, ["drh"], ["drt"])
            tt("vector", xmT[:, t0:t0 + tc], xmT[:, t0:t0 + tc], drt[:, 0:tc], OP.max,
               ["rmaxr", "drt"], ["rmaxr"])
            dma("scalar", r_s[mc, :, t0:t0 + tc], drt[:, 0:tc], ["drt"], ["rs%d" % mc])
    for (t0, tc) in TCH:
        tcopy("vector", drh[:, 0:tc], xmT[:, t0:t0 + tc], ["rmaxr"], ["drh"])
        tree_max(drh, "drh", drt, "drt", tc)
        ts("vector", drh[0:1, 0:tc], drh[0:1, 0:tc], 1.0, 1e-5, OP.mult, OP.max,
           ["drh"], ["drh"])
        mm(P2[:, 0:tc], ones_row[:, :], drh[0:1, 0:tc], True, True, ["oner", "drh"], ["P2"])
        tcopy("vector", xmT[:, t0:t0 + tc], P2[:, 0:tc], ["P2", "rmaxr"], ["xmr"])
    pr.barrier()
    ESD1.close()
    pr.barrier()
    ESDw.close()

    ESD2 = ExitStack()
    rt_t = [sb(f"rt_t{i}", [128, 256], stack=ESD2) for i in range(2)]
    hqL = [sb(f"hqL{k}", [128, 256], BF16, stack=ESD2) for k in range(64)]
    q2sl2 = [sb(f"q2sl2{i}", [128, 8192], BF16, stack=ESD2) for i in range(2)]
    x3c = [sb(f"x3c{m}", [128, 256], stack=ESD2) for m in range(16)]
    d2inv = sb("d2inv", [128, 256], stack=ESD2)
    d2yS = [sb(f"d2yt{i}", [128, 256], stack=ESD2) for i in range(2)]
    d2y0S = [sb(f"d2y0t{i}", [128, 256], stack=ESD2) for i in range(2)]
    d2qS = [sb(f"d2qt{i}", [128, 256], stack=ESD2) for i in range(2)]
    d2sq = sb("d2sq", [128, 256], stack=ESD2)
    d2o = sb("d2o", [128, 256], stack=ESD2)
    d2sc = sb("d2sc", [128, 256], stack=ESD2)
    for ti, (t0, tc) in enumerate(TC2):
        recip(d2inv[:, 0:tc], xmT[:, t0:t0 + tc], ["xmr"], ["d2invB"])
        ts("vector", d2inv[:, 0:tc], d2inv[:, 0:tc], 255.0, None, OP.mult, None,
           ["d2invB"], ["d2invB"])
        ts("vector", d2sc[:, 0:tc], xmT[:, t0:t0 + tc], s2cL1[:, 0:1], 1.0 / 255.0,
           OP.mult, OP.mult, ["xmr", "s2cl1"], ["d2scB"])
        for kt in range(64):
            pp = kt % 2
            pf = "d2%d" % pp
            dy, dy0, dq = d2yS[pp], d2y0S[pp], d2qS[pp]
            dma("sync", rt_t[pp][:, 0:tc], r_s[kt, :, t0:t0 + tc],
                ["rs%d" % kt, "rt%d" % pp], ["rt%d" % pp])
            tt("gpsimd", dy[:, 0:tc], rt_t[pp][:, 0:tc], d2inv[:, 0:tc], OP.mult,
               ["rt%d" % pp, "d2invB", pf + "y"], [pf + "y"])
            ladder_round(pf, dy[:, 0:tc], dy[:, 0:tc], dy0[:, 0:tc], dq[:, 0:tc], tc,
                         [128, 64, 32, 16, 8, 4, 2, 1], "vector", [pf + "y"])
            tcopy("vector", hqL[kt][:, 0:tc], dq[:, 0:tc], [pf + "q"], [f"hqL{kt}"])
        for m in range(16):
            wb = m % 2
            dma("sync", q2sl2[wb][:, :], q2L1_s[m], ["q2L1", "q2s2%d" % wb], ["q2s2%d" % wb])
            pq_ = PQ[m % 4]
            pk = "P%d" % (m % 4)
            for kt in range(64):
                mm(pq_[:, 0:tc], q2sl2[wb][:, kt * 128:kt * 128 + 128], hqL[kt][:, 0:tc],
                   kt == 0, kt == 63, ["q2s2%d" % wb, f"hqL{kt}"], [pk])
            dma("sync", x3c[m][:, 0:tc], x3_s[m, :, t0:t0 + tc], ["x3s%d" % m, f"x3c{m}"], [f"x3c{m}"])
            tt("vector", d2o[:, 0:tc], pq_[:, 0:tc], d2sc[:, 0:tc], OP.mult,
               [pk, "d2scB"], ["d2o"])
            tt("vector", x3c[m][:, 0:tc], x3c[m][:, 0:tc], d2o[:, 0:tc], OP.add,
               [f"x3c{m}", "d2o"], [f"x3c{m}"])
        layer_norm(x3c, [f"x3c{m}" for m in range(16)], tc)
        for m in range(16):
            dma("scalar", outT[m * 128:m * 128 + 128, t0:t0 + tc], x3c[m][:, 0:tc],
                [f"x3c{m}"], ["outd"])
    pr.barrier()
    ESD2.close()
    ESX.close()

    pr.finalize(nc, ES)
    ES.close()
    nc._pr = pr
    return nc


def _pack_lhsT(w, mtile=128):
    M, K = w.shape
    nmc = (M + mtile - 1) // mtile
    pad = nmc * mtile - M
    if pad:
        w = np.concatenate([w, np.zeros((pad, K), w.dtype)], 0)
    nkt = K // 128
    a = w.reshape(nmc, mtile, nkt, 128).transpose(0, 3, 2, 1)
    return np.ascontiguousarray(a.reshape(nmc, 128, K))


def kernel(L0_attn_u, L0_attn_W_in, L0_attn_conv_w, L0_attn_conv_b, L0_attn_dt_bias,
           L0_attn_A_log, L0_attn_D, L0_attn_norm_w, L0_attn_W_out, L0_ffn_w_gate_up,
           L0_ffn_w_down, L1_attn_Wq, L1_attn_bq, L1_attn_Wk, L1_attn_bk, L1_attn_Wv,
           L1_attn_bv, L1_attn_Wo, L1_attn_bo, L1_ffn_c_fc, L1_ffn_c_proj):
    S = L0_attn_u.shape[1]
    T = S // NCORES
    nc = build(T)
    maps = make_inmaps(T, L0_attn_u, L0_attn_W_in, L0_attn_conv_w, L0_attn_conv_b,
                       L0_attn_dt_bias, L0_attn_A_log, L0_attn_D, L0_attn_norm_w,
                       L0_attn_W_out, L0_ffn_w_gate_up, L0_ffn_w_down, L1_attn_Wq,
                       L1_attn_bq, L1_attn_Wk, L1_attn_bk, L1_attn_Wv, L1_attn_bv,
                       L1_attn_Wo, L1_attn_bo, L1_ffn_c_fc, L1_ffn_c_proj)
    res = run_bass_kernel_spmd(nc, maps, core_ids=list(range(NCORES)))
    out = np.concatenate([np.ascontiguousarray(r["outT"]).T for r in res.results], axis=0)
    return out.reshape(1, S, D).astype(np.float32)


def make_inmaps(T, u, W_in, conv_w, conv_b, dt_bias, A_log, Dh, norm_w, W_out,
                w_gate_up, w_down, Wq, bq, Wk, bk, Wv, bv, Wo, bo, c_fc, c_proj):
    f32 = np.float32
    uT_full = np.ascontiguousarray(np.asarray(u, f32)[0].T)
    S = uT_full.shape[1]
    uT_pad = np.concatenate([np.zeros((D, 3), f32), uT_full], axis=1)
    shared = {}
    shared["w_in_p"] = _pack_lhsT(np.asarray(W_in, f32))
    cw = np.asarray(conv_w, f32)[:, 0, :]
    shared["convw"] = np.ascontiguousarray(
        cw.reshape(34, 128, 4).transpose(1, 0, 2).reshape(128, 34 * 4))
    shared["convb"] = np.ascontiguousarray(np.asarray(conv_b, f32).reshape(34, 128).T)
    shared["dtb"] = np.asarray(dt_bias, f32).reshape(64, 1)
    shared["alog"] = np.asarray(A_log, f32).reshape(64, 1)
    shared["drep"] = np.ascontiguousarray(
        np.repeat(np.asarray(Dh, f32), 64).reshape(32, 128).T)
    shared["nw"] = np.ascontiguousarray(np.asarray(norm_w, f32).reshape(32, 128).T)
    shared["w_out_p"] = _pack_lhsT(np.asarray(W_out, f32))
    shared["w_gup_p"] = np.ascontiguousarray(np.asarray(w_gate_up, f32).reshape(88, 128, 2048))
    shared["w_down_p"] = np.ascontiguousarray(np.asarray(w_down, f32).reshape(16, 128, 5632))
    for nm, W in (("wq_p", Wq), ("wk_p", Wk), ("wv_p", Wv), ("wo_p", Wo)):
        shared[nm] = _pack_lhsT(np.asarray(W, f32))
    for nm, b in (("bq", bq), ("bk", bk), ("bv", bv), ("bo", bo)):
        shared[nm] = np.ascontiguousarray(np.asarray(b, f32).reshape(16, 128).T)
    cfc = np.asarray(c_fc, f32)
    shared["cfc_p"] = np.ascontiguousarray(
        cfc.reshape(16, 128, 64, 128).transpose(2, 1, 0, 3).reshape(64, 128, 2048))
    cpr = np.asarray(c_proj, f32)
    shared["cprT_p"] = np.ascontiguousarray(
        cpr.reshape(16, 128, 64, 128).transpose(0, 3, 2, 1).reshape(16, 128, 8192))
    shared["eye"] = np.eye(128, dtype=f32)
    ohp = np.zeros((64, 32, 128), f32)
    for g in range(32):
        ohp[2 * g, g, 0:64] = 1.0
        ohp[2 * g + 1, g, 64:128] = 1.0
    shared["ohp"] = np.ascontiguousarray(ohp.reshape(64, 4096))
    si = np.arange(128)[:, None]
    tj = np.arange(128)[None, :]
    shared["am"] = np.where(si <= tj, 0.0, -1e30).astype(f32)
    maps = []
    for i in range(NCORES):
        m = dict(shared)
        m["uT"] = np.ascontiguousarray(uT_pad[:, i * T:i * T + T + 3])
        m["mask8"] = np.broadcast_to(
            (np.arange(8)[None, :] < i).astype(f32), (128, 8)).copy()
        me = np.zeros((128, 16), f32)
        for j in range(8):
            if j < i:
                me[16 * j:16 * j + 16, :] = np.eye(16, dtype=f32)
        m["maskexp"] = me
        maps.append(m)
    return maps


# version 04
# speedup vs baseline: 1.1982x; 1.1982x over previous
import numpy as np
import bisect
from contextlib import ExitStack
import concourse.bass as bass
import concourse.mybir as mybir
from concourse.bass_utils import run_bass_kernel_spmd

F32 = mybir.dt.float32
BF16 = mybir.dt.bfloat16
AF = mybir.ActivationFunctionType
OP = mybir.AluOpType
AX = mybir.AxisListType
from bass_rust import ReduceOp as RO

ENG = ("sync", "tensor", "vector", "scalar", "gpsimd")
NCORES = 8
D = 2048
DSSM = 4096
NH = 64
HD = 64
DST = 128
CONVD = 4352
DIN = 8512
I0 = 5632
H1 = 8192
SCALE_L1 = float(DST ** -0.5)


NDSEM = 14


class Prog:

    def __init__(self):
        self.ops = {e: [] for e in ENG}
        self.dnames = ["D%d" % i for i in range(NDSEM)]
        self.cnt = {e: 0 for e in ENG}
        self.cnt.update({d: 0 for d in self.dnames})
        self.vc = {e: {} for e in ENG}
        self.hist_c = {e: [0] for e in ENG}
        self.hist_s = {e: [{}] for e in ENG}
        self.wmap = {}
        self.rmap = {}
        self.nwaits = 0
        self.fence = {e: 0 for e in ENG}
        self.drr = 0
        self.small = {"wcol", "wcol2", "winv", "winvB", "s1gk", "s1uk", "s2all",
                      "dccar", "bd", "rowdc", "rowprev", "grow", "gcol", "carry",
                      "ct2", "off1", "offc", "s1c", "th1", "th2", "s2c", "s2cl1",
                      "dtmp", "dacc1", "dacc2", "pmc", "amx3", "b2hm", "drh",
                      "lnra", "lnrb", "lnrc", "dclj", "b2s", "b2i", "d2i", "d2s",
                      "rmsr", "acol", "dtb", "dcp0", "scolB"}

    def barrier(self):
        self.bar = {s: c for s, c in self.cnt.items() if c > 0}
        self.bar_todo = set(ENG)

    def emit(self, eng, fn, reads=(), writes=(), dma=False, sreads=()):
        need = {}

        def add(e2, c2):
            if e2 != eng and c2 > need.get(e2, 0):
                need[e2] = c2

        if getattr(self, "bar_todo", None) and eng in self.bar_todo:
            self.bar_todo.discard(eng)
            for s, c in self.bar.items():
                if s == eng:
                    if c > self.fence[eng]:
                        need[eng] = max(need.get(eng, 0), c)
                else:
                    add(s, c)

        for k in reads:
            t = self.wmap.get(k)
            if t:
                add(*t)
        for k in writes:
            t = self.wmap.get(k)
            if t:
                add(*t)
            for e2, c2 in self.rmap.get(k, {}).items():
                add(e2, c2)
        for k in sreads:
            t = self.wmap.get(k)
            if t and t[0] == eng and t[1] > self.fence[eng]:
                need[eng] = max(need.get(eng, 0), t[1])
        for k in reads:
            if k in self.small:
                t = self.wmap.get(k)
                if t and t[0] == eng and t[1] > self.fence[eng]:
                    need[eng] = max(need.get(eng, 0), t[1])
        if dma:
            selfneed = 0
            for k in list(reads) + list(writes):
                t = self.wmap.get(k)
                if t and t[0] == eng:
                    selfneed = max(selfneed, t[1])
            for k in writes:
                c2 = self.rmap.get(k, {}).get(eng, 0)
                selfneed = max(selfneed, c2)
            if selfneed > 0:
                need[eng] = max(need.get(eng, 0), selfneed)
            dn = self.dnames[self.drr % NDSEM]
            self.drr += 1
            if self.cnt[dn] > 0:
                add(dn, self.cnt[dn])
        waits = []
        if need.get(eng, 0) > self.fence[eng]:
            waits.append((eng, need[eng]))
            self.fence[eng] = need[eng]
        need.pop(eng, None)
        v = self.vc[eng]
        for e2, c2 in sorted(need.items()):
            if v.get(e2, 0) >= c2:
                continue
            waits.append((e2, c2))
            if e2 in self.hist_c:
                i = bisect.bisect_right(self.hist_c[e2], c2) - 1
                for f, cf in self.hist_s[e2][i].items():
                    if cf > v.get(f, 0):
                        v[f] = cf
            if c2 > v.get(e2, 0):
                v[e2] = c2
        self.nwaits += len(waits)
        if dma:
            sem = dn
            inc = 16
        else:
            sem = eng
            inc = 1
        self.cnt[sem] += inc
        cafter = self.cnt[sem]
        if not dma:
            v[eng] = cafter
            self.hist_c[eng].append(cafter)
            self.hist_s[eng].append(dict(v))
        self.ops[eng].append((waits, fn, inc, sem))
        for k in writes:
            self.wmap[k] = (sem, cafter)
            self.rmap[k] = {}
        for k in reads:
            self.rmap.setdefault(k, {})[sem] = cafter

    def finalize(self, nc, es):
        sems = {e: es.enter_context(nc.semaphore("sem_" + e)) for e in ENG}
        for d in self.dnames:
            sems[d] = es.enter_context(nc.semaphore("sem_" + d))
        fin = [(e, self.cnt[e]) for e in list(ENG) + self.dnames
               if e != "sync" and self.cnt[e] > 0]
        self.ops["sync"].append((fin, None, 0, "sync"))
        blk = es.enter_context(nc.Block())
        for eng in ENG:
            dec = getattr(blk, eng)
            lst = self.ops[eng]
            if not lst:
                continue

            def mk(lst=lst, eng=eng):
                @dec
                def _(e):
                    for waits, fn, inc, sem in lst:
                        for (e2, vv) in waits:
                            e.wait_ge(sems[e2], vv)
                        if fn is not None:
                            inst = fn(e)
                            inst.then_inc(sems[sem], inc)
            mk()


def split_cols(off, end, step):
    out = []
    while off < end:
        ln = min(step, end - off)
        out.append((off, ln))
        off += ln
    return out


def build(T, debug=False):
    HT = T // 2
    NCH = HT // 128
    nc = bass.Bass("TRN2", target_bir_lowering=False, debug=False)
    pr = Prog()

    def dram(name, shape, dt=F32, kind=None):
        if kind:
            return nc.dram_tensor(name, shape, dt, kind=kind)
        return nc.dram_tensor(name, shape, dt)

    uT = dram("uT", [D, T + 3], kind="ExternalInput").ap()
    w_in_p = dram("w_in_p", [67, 128, 2048], kind="ExternalInput").ap()
    convw = dram("convw", [128, 34 * 4], kind="ExternalInput").ap()
    convb = dram("convb", [128, 34], kind="ExternalInput").ap()
    dtb_d = dram("dtb", [64, 1], kind="ExternalInput").ap()
    alog_d = dram("alog", [64, 1], kind="ExternalInput").ap()
    drep_d = dram("drep", [128, 32], kind="ExternalInput").ap()
    nw_d = dram("nw", [128, 32], kind="ExternalInput").ap()
    w_out_p = dram("w_out_p", [16, 128, 4096], kind="ExternalInput").ap()
    w_gup_p = dram("w_gup_p", [88, 128, 2048], kind="ExternalInput").ap()
    w_down_p = dram("w_down_p", [16, 128, 5632], kind="ExternalInput").ap()
    wq_p = dram("wq_p", [16, 128, 2048], kind="ExternalInput").ap()
    wk_p = dram("wk_p", [16, 128, 2048], kind="ExternalInput").ap()
    wv_p = dram("wv_p", [16, 128, 2048], kind="ExternalInput").ap()
    wo_p = dram("wo_p", [16, 128, 2048], kind="ExternalInput").ap()
    bq_d = dram("bq", [128, 16], kind="ExternalInput").ap()
    bk_d = dram("bk", [128, 16], kind="ExternalInput").ap()
    bv_d = dram("bv", [128, 16], kind="ExternalInput").ap()
    bo_d = dram("bo", [128, 16], kind="ExternalInput").ap()
    cfc_p = dram("cfc_p", [64, 128, 2048], kind="ExternalInput").ap()
    cprT_p = dram("cprT_p", [16, 128, 8192], kind="ExternalInput").ap()
    eye_d = dram("eye", [128, 128], kind="ExternalInput").ap()
    ohp_d = dram("ohp", [64, 4096], kind="ExternalInput").ap()
    am_d = dram("am", [128, 128], kind="ExternalInput").ap()
    mask8_d = dram("mask8", [128, 8], kind="ExternalInput").ap()
    maskexp_d = dram("maskexp", [128, 16], kind="ExternalInput").ap()
    outT = dram("outT", [D, T], kind="ExternalOutput").ap()

    sz_s = dram("sz_s", [32, 128, T]).ap()
    yg_s = dram("yg_s", [32, 128, T]).ap()
    x1_s = dram("x1_s", [16, 128, T]).ap()
    qx_s = dram("qx_s", [16, 128, T], BF16).ap()
    h0_s = dram("h0_s", [44, 128, T]).ap()
    q2L0_s = dram("q2L0_s", [16, 128, 5632], BF16).ap()
    x2_s = dram("x2_s", [16, 128, T]).ap()
    kv_s = dram("kv_s", [16, 128, T]).ap()
    pq_s = dram("pq_s", [16, 128, T]).ap()
    x3_s = dram("x3_s", [16, 128, T]).ap()
    x3h_s = dram("x3h_s", [16, 128, T], BF16).ap()
    x3l_s = dram("x3l_s", [16, 128, T], BF16).ap()
    r_s = dram("r_s", [64, 128, T]).ap()
    q2L1_s = dram("q2L1_s", [16, 128, 8192], BF16).ap()
    cc_in = dram("cc_in", [129, 4096])
    cc_out = dram("cc_out", [8 * 129, 4096])
    cc2_in = dram("cc2_in", [16, 128])
    cc2_out = dram("cc2_out", [128, 128])
    dbg = {}
    if debug:
        for nm, shp, dt in [("d_sz", [32, 128, T], F32), ("d_yg", [32, 128, T], F32),
                            ("d_x1", [16, 128, T], F32), ("d_x2", [16, 128, T], F32),
                            ("d_x3", [16, 128, T], F32), ("d_kv", [16, 128, T], F32),
                            ("d_st", [128, 4096], F32), ("d_sin", [128, 4096], F32),
                            ("d_dc", [64, T + 1], F32), ("d_dts", [64, T], F32),
                            ("d_ygt", [32, 128, T], F32), ("d_cdb", [32, 128, T], F32),
                            ("d_p2r", [32, 128, T], F32), ("d_cda", [32, 128, T], F32),
                            ("d_h", [44, 128, T], F32), ("d_qx", [16, 128, T], F32),
                            ("d_q1", [2, 128, 2048], BF16), ("d_s1", [2, 128], F32),
                            ("d_gu", [2, 128, 512], F32), ("d_gy", [128, 2048], F32),
                            ("d_wc", [2, 128], F32), ("d_gq", [128, 2048], F32)]:
            dbg[nm] = dram(nm, shp, dt, kind="ExternalOutput").ap()

    ES = ExitStack()

    def sb(name, shape, dt=F32, stack=None):
        return (stack or ES).enter_context(nc.sbuf_tensor(name, shape, dt))

    def ps(name, shape):
        return ES.enter_context(nc.psum_tensor(name, shape, F32))

    P0 = ps("P0", [128, 512])
    P1 = ps("P1", [128, 512])
    P2 = ps("P2", [128, 512])
    P3 = ps("P3", [128, 512])
    PA = ps("PA", [128, 132])
    PC = ps("PC", [128, 128])
    PR1 = ps("PR1", [1, 512])
    PR2 = ps("PR2", [1, 512])
    PQ = [P0, P1, P2, P3]

    ones = sb("ones", [128, 1024 if T >= 2048 else T])
    ones_col = sb("ones_col", [128, 1])
    ones_row = sb("ones_row", [1, 128])
    eye_sb = sb("eye_sb", [128, 128])
    am_sb = sb("am_sb", [128, 128])
    mask8_sb = sb("mask8_sb", [128, 8])
    maskexp_sb = sb("maskexp_sb", [128, 16])
    sxT = sb("sxT", [128, T])
    s2all = sb("s2all", [128, 16])
    s2cL1 = sb("s2cL1", [128, 1])

    def dma(eng, dst, src, reads, writes):
        pr.emit(eng, lambda e, d=dst, s=src: e.dma_start(out=d, in_=s), reads, writes, dma=True)

    def mm(out, lhsT, rhs, start, stop, reads, writes):
        pr.emit("tensor", lambda e, o=out, l=lhsT, r=rhs, st=start, sp=stop:
                e.matmul(o, l, r, start=st, stop=sp), reads, writes)

    def trans(out, in_, reads, writes, k=128):
        pr.emit("tensor", lambda e, o=out, i=in_, kk=k: e.transpose(o, i, eye_sb[0:kk, 0:kk]),
                tuple(reads) + ("eye",), writes)

    def tt(eng, out, a, b, op, reads, writes):
        pr.emit(eng, lambda e, o=out, x=a, y=b, p=op: e.tensor_tensor(o, x, y, p), reads, writes)

    def _isap(v):
        return v is not None and not isinstance(v, (int, float))

    def ts(eng, out, a, s1, s2, op0, op1, reads, writes):
        sr = tuple(reads) if (_isap(s1) or _isap(s2)) else ()
        if op1 is None:
            pr.emit(eng, lambda e, o=out, x=a, u=s1, p0=op0: e.tensor_scalar(o, x, u, None, p0),
                    reads, writes, sreads=sr)
        else:
            pr.emit(eng, lambda e, o=out, x=a, u=s1, v=s2, p0=op0, p1=op1:
                    e.tensor_scalar(o, x, u, v, p0, p1), reads, writes, sreads=sr)

    def stt(eng, out, in0, sc, in1, op0, op1, reads, writes):
        sr = tuple(reads) if _isap(sc) else ()
        pr.emit(eng, lambda e, o=out, x=in0, s=sc, y=in1, p0=op0, p1=op1:
                e.scalar_tensor_tensor(o, x, s, y, p0, p1), reads, writes, sreads=sr)

    def act(out, in_, func, reads, writes, bias=None, scale=None, accum=None):
        if isinstance(bias, float) or isinstance(scale, float):
            reads = tuple(reads) + ("constap",)
        sr = tuple(reads) if (_isap(bias) or _isap(scale)) else ()

        def f(e, o=out, i=in_, fu=func, b=bias, s=scale, a=accum):
            kw = {}
            if b is not None:
                kw["bias"] = b
            if s is not None:
                kw["scale"] = s
            if a is not None:
                kw["accum_out"] = a
            return e.activation(o, i, fu, **kw)
        pr.emit("scalar", f, reads, writes, sreads=sr)

    def tcopy(eng, out, in_, reads, writes):
        pr.emit(eng, lambda e, o=out, i=in_: e.tensor_copy(o, i), reads, writes)

    def mset(eng, ap, v, writes):
        pr.emit(eng, lambda e, a=ap, c=v: e.memset(a, c), (), writes)

    def redmax_free(eng, out, in_, reads, writes, absval=True):
        pr.emit("vector", lambda e, o=out, i=in_, a=absval:
                e.reduce_max(o, i, axis=AX.X, apply_absolute_value=a), reads, writes)

    def tree_max(tile, key, tmp2, tmp2key, tc):
        for p in (64, 32):
            tcopy("vector", tmp2[0:p, 0:tc], tile[p:2 * p, 0:tc], [key], [tmp2key])
            tt("vector", tile[0:p, 0:tc], tile[0:p, 0:tc], tmp2[0:p, 0:tc], OP.max,
               [key, tmp2key], [key])
        for j in range(tc // 128):
            trans(PC[0:128, 0:32], tile[0:32, j * 128:j * 128 + 128], [key], ["PC"], k=32)
            redmax_free("vector", pmc[:, j:j + 1], PC[0:128, 0:32], ["PC", "pmc"],
                        ["pmc"], absval=False)
        for j in range(tc // 128):
            trans(PC[0:1, 0:128], pmc[:, j:j + 1], ["pmc"], ["PC"], k=128)
            tcopy("vector", tile[0:1, j * 128:j * 128 + 128], PC[0:1, 0:128], ["PC"], [key])

    def scan_op(eng, out, d0, d1, reads, writes):
        pr.emit(eng, lambda e, o=out, a=d0, b=d1:
                e.tensor_tensor_scan(o, a, b, 0.0, OP.mult, OP.add), reads, writes)

    def recip(out, in_, reads, writes):
        pr.emit("vector", lambda e, o=out, i=in_: e.reciprocal(o, i), reads, writes,
                sreads=tuple(reads))

    def coll(name_key, ins_t, outs_t, reads, writes):
        pr.emit("gpsimd", lambda e, i=ins_t, o=outs_t: e.collective_compute(
            "AllGather", OP.bypass, replica_groups=[list(range(NCORES))],
            ins=[i.ap().opt()], outs=[o.ap().opt()]), reads, writes)

    VG = ["vector", "gpsimd"]

    def ladder_round(pref, tile, y, y0, q, cols, bits, eng, reads):
        ts("vector", y, y, 0.5, None, OP.add, None, reads, [pref + "y"])
        tcopy("vector", y0, y, [pref + "y"], [pref + "y0"])
        for c in bits:
            ts("vector", q, y, float(c), float(-c), OP.is_ge, OP.mult, [pref + "y"], [pref + "q"])
            tt("vector", y, y, q, OP.add, [pref + "y", pref + "q"], [pref + "y"])
        tt("vector", q, y0, y, OP.subtract, [pref + "y", pref + "y0"], [pref + "q"])

    lnsq = sb("lnsq", [128, 512])
    lnra_t = sb("lnra_t", [1, 512])
    lnrb_t = sb("lnrb_t", [1, 512])
    lnrc_t = sb("lnrc_t", [1, 512])

    def layer_norm(xs_tiles, ks, tc):
        lnrb = lnrb_t[0:1, 0:tc]
        lnrc = lnrc_t[0:1, 0:tc]
        lnra = lnra_t[0:1, 0:tc]
        for m in range(16):
            mm(PR1[0:1, 0:tc], ones_col[:, :], xs_tiles[m][:, 0:tc], m == 0, m == 15,
               ["onec", ks[m]], ["PR1"])
        for m in range(16):
            act(lnsq[:, 0:tc], xs_tiles[m][:, 0:tc], AF.Square, [ks[m]], ["lnsq"])
            mm(PR2[0:1, 0:tc], ones_col[:, :], lnsq[:, 0:tc], m == 0, m == 15,
               ["onec", "lnsq"], ["PR2"])
        ts("vector", lnra, PR1[0:1, 0:tc], 1.0 / 2048.0, None, OP.mult, None,
           ["PR1"], ["lnra"])
        tt("vector", lnrb, lnra, lnra, OP.mult,
           ["lnra"], ["lnrb"])
        stt("vector", lnrb, PR2[0:1, 0:tc], 1.0 / 2048.0, lnrb,
            OP.mult, OP.subtract, ["PR2", "lnrb"], ["lnrb"])
        act(lnrb, lnrb, AF.Sqrt, ["lnrb"], ["lnrb"], bias=1e-5)
        recip(lnrb, lnrb, ["lnrb"], ["lnrb"])
        tt("vector", lnrc, lnra, lnrb, OP.mult,
           ["lnra", "lnrb"], ["lnrc"])
        mm(P0[:, 0:tc], ones_row[:, :], lnrb, True, True, ["oner", "lnrb"], ["P0"])
        mm(P1[:, 0:tc], ones_row[:, :], lnrc, True, True, ["oner", "lnrc"], ["P1"])
        for m in range(16):
            tt("vector", xs_tiles[m][:, 0:tc], xs_tiles[m][:, 0:tc], P0[:, 0:tc], OP.mult,
               [ks[m], "P0"], [ks[m]])
            tt("vector", xs_tiles[m][:, 0:tc], xs_tiles[m][:, 0:tc], P1[:, 0:tc], OP.subtract,
               [ks[m], "P1"], [ks[m]])

    mset("vector", ones[:, :], 1.0, ["ones"])
    mset("vector", ones_col[:, :], 1.0, ["onec"])
    mset("vector", ones_row[:, :], 1.0, ["oner"])
    pmc = sb("pmc", [128, 8])
    consts = sb("consts", [128, 3])
    nc.const_aps.register(F32, 1e-5, consts[:, 0:1])
    nc.const_aps.register(F32, 1.0 / 4096.0, consts[:, 1:2])
    nc.const_aps.register(F32, -1.0, consts[:, 2:3])
    mset("vector", consts[:, 0:1], 1e-5, ["constap"])
    mset("vector", consts[:, 1:2], 1.0 / 4096.0, ["constap"])
    mset("vector", consts[:, 2:3], -1.0, ["constap"])
    dma("sync", eye_sb[:, :], eye_d[:, :], [], ["eye"])
    dma("sync", am_sb[:, :], am_d[:, :], [], ["am"])
    dma("sync", mask8_sb[:, :], mask8_d[:, :], [], ["mask8"])
    dma("sync", maskexp_sb[:, :], maskexp_d[:, :], [], ["maskexp"])

    ESL0 = ExitStack()
    Dc_pad = sb("Dc_pad", [64, T + 1], stack=ESL0)
    C_full = sb("C_full", [128, T], stack=ESL0)
    cdec = sb("cdec", [64, T], stack=ESL0)
    S_in = sb("S_in", [128, 4096], stack=ESL0)
    dtb_sb = sb("dtb_sb", [64, 1], stack=ESL0)
    acol = sb("acol", [64, 1], stack=ESL0)

    dma("sync", dtb_sb[:, :], dtb_d[:, :], [], ["dtb"])
    dma("sync", acol[:, :], alog_d[:, :], [], ["acol"])
    act(acol[:, :], acol[:, :], AF.Exp, ["acol"], ["acol"])
    ts("vector", acol[:, :], acol[:, :], -1.0, None, OP.mult, None, ["acol"], ["acol"])
    mset("vector", Dc_pad[:, 0:1], 0.0, ["dcp0"])

    ESAs = ExitStack()
    ST = sb("ST", [128, 4096], stack=ESAs)
    drep_sb = sb("drep_sb", [128, 32], stack=ESAs)
    dts_h = sb("dts_h", [64, HT], stack=ESAs)

    raw = sb("rawc", [128, HT + 3], stack=ESAs)
    yg_g = sb("yg_g", [128, HT], stack=ESAs)
    B_half = yg_g
    logda = raw[64:128, 0:HT]
    convw_sb = sb("convw_sb", [128, 34 * 4], stack=ESAs)
    convb_sb = sb("convb_sb", [128, 34], stack=ESAs)
    BT_c = [sb(f"BT{c}", [128, 128], stack=ESAs) for c in range(NCH)]
    G_c = [sb(f"G{c}", [128, 128], stack=ESAs) for c in range(NCH)]
    DcT_c = [sb(f"DcT{c}", [128, 64], stack=ESAs) for c in range(NCH)]
    dtT_c = [sb(f"dtT{c}", [128, 64], stack=ESAs) for c in range(NCH)]
    sz_gS = [sb(f"sz_g{i}", [128, HT], stack=ESAs) for i in range(2)]
    xc_gS = [sb(f"xc_g{i}", [128, HT], stack=ESAs) for i in range(2)]
    xT_sb = sb("xT_sb", [128, 128], stack=ESAs)
    cdecB = sb("cdecB", [128, 128], stack=ESAs)
    E_sb = sb("E_sb", [128, 128], stack=ESAs)
    M_sb = sb("M_sb", [128, 128], stack=ESAs)
    Pm_sb = sb("Pm_sb", [128, 128], stack=ESAs)
    Cd_sb = sb("Cd_sb", [128, 128], stack=ESAs)
    Bd_sb = sb("Bd_sb", [128, 128], stack=ESAs)
    bd_sb = sb("bd_sb", [128, 1], stack=ESAs)
    tmp_sc = sb("tmp_sc", [128, 128], stack=ESAs)
    rowdc = sb("rowdc", [1, 132], stack=ESAs)
    rowprev = sb("rowprev", [1, 64], stack=ESAs)

    ESAu = ExitStack()
    u_t = [sb(f"u{k}", [128, HT + 3], stack=ESAu) for k in range(16)]
    wsl = [sb(f"wsl{i}", [128, 2048], stack=ESAu) for i in range(2)]

    dma("sync", drep_sb[:, :], drep_d[:, :], [], ["drep"])
    dma("sync", convw_sb[:, :], convw[:, :], [], ["convw"])
    dma("sync", convb_sb[:, :], convb[:, :], [], ["convb"])
    mset("vector", ST[:, :], 0.0, ["ST"])
    mset("vector", rowprev[:, :], 0.0, ["rowprev"])

    def inproj_mm(mc, cols, pslices, wbuf):
        dma("sync", wsl[wbuf][:, :], w_in_p[mc], ["wsl%d" % wbuf], ["wsl%d" % wbuf])
        for ci, (off, ln) in enumerate(cols):
            pap, pkey = pslices[ci]
            for kt in range(16):
                mm(pap[:, 0:ln] if pap is not PA else pap[:, 129:129 + ln],
                   wsl[wbuf][:, kt * 128:kt * 128 + 128],
                   u_t[kt][:, off:off + ln], kt == 0, kt == 15,
                   ["wsl%d" % wbuf, f"u{kt}"], [pkey])

    for half in range(2):
        hoff = half * HT
        for kt in range(16):
            dma("sync", u_t[kt][:, :], uT[kt * 128:kt * 128 + 128, hoff:hoff + HT + 3],
                [f"u{kt}"], [f"u{kt}"])
        zcols = split_cols(3, HT + 3, 512)
        xcols = split_cols(0, HT + 3, 512)

        def psl(cols):
            out = []
            for i in range(len(cols)):
                out.append([(P0, "P0"), (P1, "P1"), (PA, "PAt")][i])
            return out

        inproj_mm(66, zcols, psl(zcols), 0)
        for ci, (off, ln) in enumerate(zcols):
            pap, pkey = psl(zcols)[ci]
            src = pap[0:64, 0:ln] if pap is not PA else pap[0:64, 129:129 + ln]
            o = off - 3
            ts("vector", raw[0:64, 0:ln], src, dtb_sb[:, 0:1], None, OP.add, None,
               [pkey, "dtb"], ["spv"])
            act(xc_gS[0][0:64, 0:ln], raw[0:64, 0:ln], AF.Abs, ["spv"], ["spa"])
            act(xc_gS[0][0:64, 0:ln], xc_gS[0][0:64, 0:ln], AF.Exp, ["spa"], ["spa"], scale=-1.0)
            ts("vector", xc_gS[0][0:64, 0:ln], xc_gS[0][0:64, 0:ln], 1.0, None, OP.add, None, ["spa"], ["spa"])
            act(xc_gS[0][0:64, 0:ln], xc_gS[0][0:64, 0:ln], AF.Ln, ["spa"], ["spa"])
            ts("vector", raw[0:64, 0:ln], raw[0:64, 0:ln], 0.0, None, OP.max, None, ["spv"], ["spv"])
            tt("vector", dts_h[:, o:o + ln], xc_gS[0][0:64, 0:ln], raw[0:64, 0:ln], OP.add,
               ["spa", "spv"], ["dts"])
        ts("vector", logda, dts_h[:, :], acol[:, :], None, OP.mult, None,
           ["dts", "acol"], ["logda"])
        scan_op("vector", Dc_pad[:, 1 + hoff:1 + hoff + HT], ones[64:128, 0:HT], logda,
                ["ones", "logda"], ["dcseg"])
        if half == 1:
            tcopy("vector", bd_sb[0:64, 0:1], Dc_pad[:, hoff:hoff + 1], ["dcp"], ["dccar"])
            ts("vector", Dc_pad[:, 1 + hoff:1 + hoff + HT], Dc_pad[:, 1 + hoff:1 + hoff + HT],
               bd_sb[0:64, 0:1], None, OP.add, None, ["dcseg", "dccar"], ["dcseg"])
        pr.wmap["dcp"] = pr.wmap["dcseg"]
        pr.barrier()

        for which, mc, cvi in (("B", 64, 32), ("C", 65, 33)):
            inproj_mm(mc, xcols, psl(xcols), 0)
            for ci, (off, ln) in enumerate(xcols):
                pap, pkey = psl(xcols)[ci]
                src = pap[:, 0:ln] if pap is not PA else pap[:, 129:129 + ln]
                tcopy("vector", raw[:, off:off + ln], src, [pkey], ["raw"])
            ts("vector", xc_gS[1][:, :], raw[:, 0:HT], convw_sb[:, cvi * 4:cvi * 4 + 1],
               convb_sb[:, cvi:cvi + 1], OP.mult, OP.add, ["raw", "convw", "convb"], ["cacc1"])
            for d in range(1, 4):
                stt("vector", xc_gS[1][:, :], raw[:, d:d + HT],
                    convw_sb[:, cvi * 4 + d:cvi * 4 + d + 1],
                    xc_gS[1][:, :], OP.mult, OP.add, ["raw", "convw", "cacc1"], ["cacc1"])
            dstt = B_half[:, :] if which == "B" else C_full[:, hoff:hoff + HT]
            dkey = "Bh" if which == "B" else "Cf"
            act(dstt, xc_gS[1][:, :], AF.Silu, ["cacc1"], [dkey])

        for c in range(NCH):
            cs = hoff + c * 128
            trans(PC[:, :], B_half[:, c * 128:c * 128 + 128], ["Bh"], ["PC"])
            tcopy("vector", BT_c[c][:, :], PC[:, :], ["PC"], [f"BT{c}"])
            mm(PC[:, :], B_half[:, c * 128:c * 128 + 128], C_full[:, cs:cs + 128], True, True,
               ["Bh", "Cf"], ["PC"])
            tcopy("vector", G_c[c][:, :], PC[:, :], ["PC"], [f"G{c}"])
            trans(PC[0:128, 0:64], Dc_pad[:, 1 + cs:1 + cs + 128], ["dcp"], ["PC"], k=64)
            tcopy("vector", DcT_c[c][:, :], PC[0:128, 0:64], ["PC"], [f"DcT{c}"])
            trans(PC[0:128, 0:64], dts_h[:, c * 128:c * 128 + 128], ["dts"], ["PC"], k=64)
            tcopy("vector", dtT_c[c][:, :], PC[0:128, 0:64], ["PC"], [f"dtT{c}"])

        pr.barrier()
        def emit_inproj(g):
            p = g % 2
            szg, xcg = sz_gS[p], xc_gS[p]
            kz, kx = "szg%d" % p, "cacc%d" % p
            inproj_mm(g, zcols, psl(zcols), g % 2)
            for ci, (off, ln) in enumerate(zcols):
                pap, pkey = psl(zcols)[ci]
                src = pap[:, 0:ln] if pap is not PA else pap[:, 129:129 + ln]
                act(szg[:, off - 3:off - 3 + ln], src, AF.Silu, [pkey, kz], [kz])
            dma("scalar", sz_s[g, :, hoff:hoff + HT], szg[:, :], [kz], ["szs%d" % g])
            if debug:
                dma("scalar", dbg["d_sz"][g, :, hoff:hoff + HT], szg[:, :], [kz], ["dbgsz"])
            inproj_mm(32 + g, xcols, psl(xcols), (g + 1) % 2)
            for ci, (off, ln) in enumerate(xcols):
                pap, pkey = psl(xcols)[ci]
                src = pap[:, 0:ln] if pap is not PA else pap[:, 129:129 + ln]
                tcopy("vector", raw[:, off:off + ln], src, [pkey], ["raw"])
            ts("vector", xcg[:, :], raw[:, 0:HT], convw_sb[:, g * 4:g * 4 + 1],
               convb_sb[:, g:g + 1], OP.mult, OP.add, ["raw", "convw", "convb", kx], [kx])
            for d in range(1, 4):
                stt("vector", xcg[:, :], raw[:, d:d + HT], convw_sb[:, g * 4 + d:g * 4 + d + 1],
                    xcg[:, :], OP.mult, OP.add, ["raw", "convw", kx], [kx])
            act(xcg[:, :], xcg[:, :], AF.Silu, [kx], [kx])

        def emit_scan(g):
            p = g % 2
            szg, xcg = sz_gS[p], xc_gS[p]
            kz, kx = "szg%d" % p, "cacc%d" % p
            for c in range(NCH):
                cs = hoff + c * 128
                trans(PC[:, :], xcg[:, c * 128:c * 128 + 128], [kx], ["PC"])
                tcopy("vector", xT_sb[:, :], PC[:, :], ["PC"], ["xT"])
                for j in range(2):
                    hh = 2 * g + j
                    trans(PC[0:1, 0:128], DcT_c[c][:, hh:hh + 1], [f"DcT{c}"], ["PC"])
                    tcopy("vector", rowdc[0:1, 1:129], PC[0:1, 0:128], ["PC"], ["rowdc"])
                    tcopy("vector", rowdc[0:1, 0:1], rowprev[0:1, hh:hh + 1],
                          ["rowprev"], ["rowdc"])
                    tcopy("vector", rowprev[0:1, hh:hh + 1], rowdc[0:1, 128:129],
                          ["rowdc"], ["rowprev"])
                    mm(PA[:, 0:129], ones_row[:, :], rowdc[0:1, 0:129], True, True,
                       ["oner", "rowdc"], ["PAw"])
                    ts("vector", cdecB[:, :], PA[:, 1:129], PA[:, 0:1], None, OP.subtract, None,
                       ["PAw"], ["cdB"])
                    act(cdecB[:, :], cdecB[:, :], AF.Exp, ["cdB"], ["cdB"])
                    ts("vector", E_sb[:, :], PA[:, 1:129], DcT_c[c][:, hh:hh + 1], None,
                       OP.subtract, None, ["PAw", f"DcT{c}"], ["E"])
                    tt("gpsimd", E_sb[:, :], E_sb[:, :], am_sb[:, :], OP.add, ["E", "am"], ["E"])
                    act(M_sb[:, :], E_sb[:, :], AF.Exp, ["E"], ["M"])
                    ts("gpsimd", Pm_sb[:, :], G_c[c][:, :], dtT_c[c][:, hh:hh + 1], None,
                       OP.mult, None, [f"G{c}", f"dtT{c}"], ["Pm"])
                    tt("gpsimd", M_sb[:, :], M_sb[:, :], Pm_sb[:, :], OP.mult, ["M", "Pm"], ["M"])
                    tt("gpsimd", Cd_sb[:, :], C_full[:, cs:cs + 128], cdecB[:, :], OP.mult,
                       ["Cf", "cdB"], ["Cd"])
                    mm(P2[64 * j:64 * j + 64, 0:128], xT_sb[:, 64 * j:64 * j + 64], M_sb[:, :],
                       True, False, ["xT", "M"], ["P2w%d" % j])
                    mm(P2[64 * j:64 * j + 64, 0:128], ST[:, hh * 64:hh * 64 + 64], Cd_sb[:, :],
                       False, True, ["ST", "Cd"], ["P2w%d" % j])
                    ts("vector", bd_sb[:, :], PA[:, 128:129], DcT_c[c][:, hh:hh + 1], None,
                       OP.subtract, None, ["PAw", f"DcT{c}"], ["bd"])
                    act(bd_sb[:, :], bd_sb[:, :], AF.Exp, ["bd"], ["bd"])
                    ts("vector", bd_sb[:, :], bd_sb[:, :], dtT_c[c][:, hh:hh + 1], None,
                       OP.mult, None, ["bd", f"dtT{c}"], ["bd"])
                    ts("gpsimd", Bd_sb[:, :], BT_c[c][:, :], bd_sb[:, 0:1], None, OP.mult, None,
                       [f"BT{c}", "bd"], ["Bd"])
                    mm(PC[:, 0:64], Bd_sb[:, :], xT_sb[:, 64 * j:64 * j + 64], True, True,
                       ["Bd", "xT"], ["PC"])
                    stt("vector", ST[:, hh * 64:hh * 64 + 64], ST[:, hh * 64:hh * 64 + 64],
                        cdecB[:, 127:128], PC[:, 0:64], OP.mult, OP.add,
                        ["ST", "cdB", "PC"], ["ST"])
                stt("vector", tmp_sc[:, :], xcg[:, c * 128:c * 128 + 128],
                    drep_sb[:, g:g + 1], P2[:, 0:128], OP.mult, OP.add,
                    [kx, "drep", "P2w0", "P2w1"], ["tmpsc"])
                tt("gpsimd", yg_g[:, c * 128:c * 128 + 128], tmp_sc[:, :],
                   szg[:, c * 128:c * 128 + 128], OP.mult, ["tmpsc", kz], ["ygg"])
            dma("scalar", yg_s[g, :, hoff:hoff + HT], yg_g[:, :], ["ygg"], ["ygs%d" % g])
            if debug:
                dma("scalar", dbg["d_yg"][g, :, hoff:hoff + HT], yg_g[:, :], ["ygg"], ["dbgyg"])

        for g in range(33):
            if g < 32:
                emit_inproj(g)
            if g >= 1:
                emit_scan(g - 1)

    pr.barrier()
    ESAu.close()
    act(cdec[:, :], Dc_pad[:, 1:T + 1], AF.Exp, ["dcp"], ["cdec"])

    dma("scalar", cc_in.ap()[0:128, :], ST[:, :], ["ST"], ["ccin"])
    dma("scalar", cc_in.ap()[128:129, 0:64], Dc_pad[:, T:T + 1], ["dcp"], ["ccin"])
    coll("cc1", cc_in, cc_out, ["ccin"], ["ccout"])
    if debug:
        dma("scalar", dbg["d_st"][:, :], ST[:, :], ["ST"], ["dbgst"])
        dma("scalar", dbg["d_dc"][:, :], Dc_pad[:, :], ["dcp"], ["dbgdc"])
        dma("scalar", dbg["d_dts"][:, 0:HT], dts_h[:, :], ["dts"], ["dbgdts"])
    ESA2 = ExitStack()
    gat_st = sb("gat_st", [128, 4096], stack=ESA2)
    tmp_st = sb("tmp_st", [128, 4096], stack=ESA2)
    dclj = sb("dclj", [1, 64], stack=ESA2)
    grow = sb("grow", [1, 64], stack=ESA2)
    gcol_sb = sb("gcol_sb", [128, 64], stack=ESA2)
    mset("vector", S_in[:, :], 0.0, ["Sin"])
    for j in range(NCORES):
        dma("sync", gat_st[:, :], cc_out.ap()[j * 129:j * 129 + 128, :], ["ccout", "gatst"], ["gatst"])
        dma("sync", dclj[:, :], cc_out.ap()[j * 129 + 128:j * 129 + 129, 0:64], ["ccout", "dclj"], ["dclj"])
        act(grow[:, :], dclj[:, :], AF.Exp, ["dclj"], ["grow"])
        ts("vector", grow[:, :], grow[:, :], 1.0, None, OP.subtract, None, ["grow"], ["grow"])
        ts("vector", grow[:, :], grow[:, :], mask8_sb[0:1, j:j + 1], 1.0, OP.mult, OP.add,
           ["grow", "mask8"], ["grow"])
        mm(PC[:, 0:64], ones_row[:, :], grow[:, :], True, True, ["oner", "grow"], ["PC"])
        tcopy("vector", gcol_sb[:, :], PC[:, 0:64], ["PC"], ["gcol"])
        for ch in range(8):
            ts("gpsimd", tmp_st[:, ch * 512:(ch + 1) * 512], gat_st[:, ch * 512:(ch + 1) * 512],
               mask8_sb[:, j:j + 1], None, OP.mult, None, ["gatst", "mask8"], ["tmpst"])
        for h in range(64):
            stt("vector", S_in[:, h * 64:h * 64 + 64], S_in[:, h * 64:h * 64 + 64],
                gcol_sb[:, h:h + 1], tmp_st[:, h * 64:h * 64 + 64], OP.mult, OP.add,
                ["Sin", "gcol", "tmpst"], ["Sin"])
    if debug:
        dma("scalar", dbg["d_sin"][:, :], S_in[:, :], ["Sin"], ["dbgsin"])
    pr.barrier()
    ESA2.close()
    pr.barrier()
    ESAs.close()

    ES3 = ExitStack()
    ygt = [sb(f"ygt{g}", [128, 512], stack=ES3) for g in range(32)]
    szt = sb("szt", [128, 512], stack=ES3)
    CdA = sb("CdA", [128, 512], stack=ES3)
    cdB2 = sb("cdB2", [128, 512], stack=ES3)
    ohp_sb = sb("ohp_sb", [64, 4096], stack=ES3)
    nw_sb = sb("nw_sb", [128, 32], stack=ES3)
    wosl = [sb("wosl0", [128, 2048], stack=ES3)] * 2
    xs_t = [sb(f"xs3_{m}", [128, 512], stack=ES3) for m in range(16)]
    sq3 = sb("sq3", [128, 512], stack=ES3)
    rowa = sb("rowa", [1, 512], stack=ES3)
    amx3 = sb("amx3", [128, 512], stack=ES3)
    invB = sb("invB3", [128, 512], stack=ES3)
    absx = sb("absx3", [128, 512], stack=ES3)
    lyy = sb("lyy3", [128, 512], stack=ES3)
    ly0 = sb("ly03", [128, 512], stack=ES3)
    lq = sb("lq3", [128, 512], stack=ES3)
    sgn = sb("sgn3", [128, 512], stack=ES3)
    qxb = sb("qxb3", [128, 512], BF16, stack=ES3)
    dma("sync", nw_sb[:, :], nw_d[:, :], [], ["nw"])
    dma("sync", ohp_sb[:, :], ohp_d[:, :], [], ["ohp"])

    TCH = split_cols(0, T, 512)
    for ti, (t0, tc) in enumerate(TCH):
        for g in range(32):
            dma("sync", ygt[g][:, 0:tc], yg_s[g, :, t0:t0 + tc], ["ygs%d" % g, f"3y{g}"], [f"3y{g}"])
            dma("sync", szt[:, 0:tc], sz_s[g, :, t0:t0 + tc], ["szs%d" % g, "szt"], ["szt"])
            mm(P3[:, 0:tc], ohp_sb[:, g * 128:g * 128 + 128], cdec[:, t0:t0 + tc],
               True, True, ["ohp", "cdec"], ["P3"])
            tcopy("vector", cdB2[:, 0:tc], P3[:, 0:tc], ["P3"], ["cdB2"])
            for j in range(2):
                hh = 2 * g + j
                mm(P2[64 * j:64 * j + 64, 0:tc], S_in[:, hh * 64:hh * 64 + 64],
                   C_full[:, t0:t0 + tc], True, True, ["Sin", "Cf"], ["P2"])
            if debug:
                dma("scalar", dbg["d_cdb"][g, :, t0:t0 + tc], cdB2[:, 0:tc],
                    ["cdB2"], ["dbgcdb"])
                tcopy("vector", CdA[:, 0:tc], P2[:, 0:tc], ["P2"], ["CdA"])
                dma("scalar", dbg["d_p2r"][g, :, t0:t0 + tc], CdA[:, 0:tc],
                    ["CdA"], ["dbgp2r"])
            tt("vector", CdA[:, 0:tc], P2[:, 0:tc], cdB2[:, 0:tc], OP.mult,
               ["P2", "cdB2"], ["CdA"])
            tt("gpsimd", CdA[:, 0:tc], CdA[:, 0:tc], szt[:, 0:tc], OP.mult,
               ["CdA", "szt"], ["CdA"])
            if debug:
                dma("scalar", dbg["d_cda"][g, :, t0:t0 + tc], CdA[:, 0:tc],
                    ["CdA"], ["dbgcda"])
            tt("gpsimd", ygt[g][:, 0:tc], ygt[g][:, 0:tc], CdA[:, 0:tc], OP.add,
               [f"3y{g}", "CdA"], [f"3y{g}"])
            if debug:
                dma("scalar", dbg["d_ygt"][g, :, t0:t0 + tc], ygt[g][:, 0:tc],
                    [f"3y{g}"], ["dbgygt"])
            act(sq3[:, 0:tc], ygt[g][:, 0:tc], AF.Square, [f"3y{g}"], ["sq3"])
            mm(PR1[0:1, 0:tc], ones_col[:, :], sq3[:, 0:tc], g == 0, g == 31,
               ["onec", "sq3"], ["PR1"])
        act(rowa[0:1, 0:tc], PR1[0:1, 0:tc], AF.Sqrt, ["PR1"], ["rmsr"],
            bias=1e-5, scale=1.0 / 4096.0)
        recip(rowa[0:1, 0:tc], rowa[0:1, 0:tc], ["rmsr"], ["rmsr"])
        mm(P3[:, 0:tc], ones_row[:, :], rowa[0:1, 0:tc], True, True, ["oner", "rmsr"], ["P3"])
        for g in range(32):
            ts("gpsimd", ygt[g][:, 0:tc], ygt[g][:, 0:tc], nw_sb[:, g:g + 1], None, OP.mult,
               None, [f"3y{g}", "nw"], [f"3y{g}"])
            tt("vector", ygt[g][:, 0:tc], ygt[g][:, 0:tc], P3[:, 0:tc], OP.mult,
               [f"3y{g}", "P3"], [f"3y{g}"])
        for m in range(16):
            pq_ = PQ[m % 4]
            pk = "P%d" % (m % 4)
            for h in range(2):
                dma("sync", wosl[0][:, :], w_out_p[m, :, h * 2048:h * 2048 + 2048],
                    ["wo0"], ["wo0"])
                for k in range(16):
                    mm(pq_[:, 0:tc], wosl[0][:, k * 128:k * 128 + 128],
                       ygt[h * 16 + k][:, 0:tc], h == 0 and k == 0, h == 1 and k == 15,
                       ["wo0", f"3y{h * 16 + k}"], [pk])
            dma("sync", xs_t[m][:, 0:tc], uT[m * 128:m * 128 + 128, 3 + t0:3 + t0 + tc],
                [f"3xs{m}"], [f"3xs{m}"])
            tt("vector", xs_t[m][:, 0:tc], xs_t[m][:, 0:tc], pq_[:, 0:tc], OP.add,
               [f"3xs{m}", pk], [f"3xs{m}"])
        layer_norm(xs_t, [f"3xs{m}" for m in range(16)], tc)
        for m in range(16):
            dma("scalar", x1_s[m, :, t0:t0 + tc], xs_t[m][:, 0:tc], [f"3xs{m}"], ["x1s%d" % m])
        for m in range(16):
            act(absx[:, 0:tc], xs_t[m][:, 0:tc], AF.Abs, [f"3xs{m}"], ["la"])
            if m == 0:
                tcopy("vector", amx3[:, 0:tc], absx[:, 0:tc], ["la"], ["amx3"])
            else:
                tt("vector", amx3[:, 0:tc], amx3[:, 0:tc], absx[:, 0:tc], OP.max,
                   ["amx3", "la"], ["amx3"])
        tree_max(amx3, "amx3", sq3, "sq3", tc)
        ts("vector", amx3[0:1, 0:tc], amx3[0:1, 0:tc], 1.0 / 127.0, 1e-8, OP.mult, OP.max,
           ["amx3"], ["amx3"])
        mm(P2[:, 0:tc], ones_row[:, :], amx3[0:1, 0:tc], True, True, ["oner", "amx3"], ["P2"])
        tcopy("vector", sxT[:, t0:t0 + tc], P2[:, 0:tc], ["P2"], ["sxs"])
        recip(invB[:, 0:tc], sxT[:, t0:t0 + tc], ["sxs"], ["invB"])
        for m in range(16):
            eng = VG[m % 2]
            act(absx[:, 0:tc], xs_t[m][:, 0:tc], AF.Abs, [f"3xs{m}"], ["la"])
            act(sgn[:, 0:tc], xs_t[m][:, 0:tc], AF.Sign, [f"3xs{m}"], ["ls"])
            tt(eng, lyy[:, 0:tc], absx[:, 0:tc], invB[:, 0:tc], OP.mult, ["la", "invB"], ["ly"])
            ladder_round("l", lyy[:, 0:tc], lyy[:, 0:tc], ly0[:, 0:tc], lq[:, 0:tc], tc,
                         [64, 32, 16, 8, 4, 2, 1], "vector", ["ly"])
            tt(eng, lq[:, 0:tc], lq[:, 0:tc], sgn[:, 0:tc], OP.mult, ["lq", "ls"], ["lq"])
            tcopy(eng, qxb[:, 0:tc], lq[:, 0:tc], ["lq"], ["qxb"])
            dma("scalar", qx_s[m, :, t0:t0 + tc], qxb[:, 0:tc], ["qxb"], ["qxs%d" % m])
            if debug:
                dma("scalar", dbg["d_qx"][m, :, t0:t0 + tc], lq[:, 0:tc], ["lq"], ["dbgqx"])
        if debug:
            for m in range(16):
                dma("scalar", dbg["d_x1"][m, :, t0:t0 + tc], xs_t[m][:, 0:tc],
                    [f"3xs{m}"], ["dbg1"])
    pr.barrier()
    ES3.close()
    pr.barrier()
    ESL0.close()

    ESB15 = ExitStack()
    HW = 2816
    wdsl = [sb(f"wdsl{h}", [128, HW], stack=ESB15) for h in range(2)]
    wcol = sb("wcol", [128, 1], stack=ESB15)
    wcol2 = sb("wcol2", [128, 1], stack=ESB15)
    winv = sb("winv", [128, 1], stack=ESB15)
    wyS = [sb(f"wyt{i}", [128, HW], stack=ESB15) for i in range(2)]
    wy0S = [sb(f"wy0t{i}", [128, HW], stack=ESB15) for i in range(2)]
    wqS = [sb(f"wqt{i}", [128, HW], stack=ESB15) for i in range(2)]
    wsgnS = [sb(f"wsgnt{i}", [128, HW], stack=ESB15) for i in range(2)]
    q2sl = sb("q2sl", [128, 5632], BF16, stack=ESB15)
    for m in range(16):
        eng = VG[m % 2]
        for h in range(2):
            dma("sync", wdsl[h][:, :], w_down_p[m, :, h * HW:h * HW + HW],
                ["wdsl%d" % h], ["wdsl%d" % h])
        redmax_free(eng, wcol[:, :], wdsl[0][:, :], ["wdsl0", "wcol"], ["wcol"], absval=True)
        redmax_free(eng, wcol2[:, :], wdsl[1][:, :], ["wdsl1", "wcol2"], ["wcol2"], absval=True)
        tt("vector", wcol[:, :], wcol[:, :], wcol2[:, :], OP.max, ["wcol", "wcol2"], ["wcol"])
        ts("vector", wcol[:, :], wcol[:, :], 1.0 / 127.0, 1e-8, OP.mult, OP.max, ["wcol"], ["wcol"])
        tcopy("vector", s2all[:, m:m + 1], wcol[:, :], ["wcol"], ["s2all"])
        recip(winv[:, :], wcol[:, :], ["wcol", "winv"], ["winv"])
        for h in range(2):
            pp = (2 * m + h) % 2
            wy, wy0, wq, wsgn = wyS[pp], wy0S[pp], wqS[pp], wsgnS[pp]
            pf = "w%d" % pp
            act(wy[:, :], wdsl[h][:, :], AF.Abs, ["wdsl%d" % h, pf + "y"], [pf + "y"])
            act(wsgn[:, :], wdsl[h][:, :], AF.Sign, ["wdsl%d" % h, pf + "s"], [pf + "s"])
            ts("vector", wy[:, :], wy[:, :], winv[:, :], None, OP.mult, None,
               [pf + "y", "winv"], [pf + "y"])
            ladder_round(pf, wy[:, :], wy[:, :], wy0[:, :], wq[:, :], HW,
                         [64, 32, 16, 8, 4, 2, 1], "vector", [pf + "y"])
            tt(eng, wq[:, :], wq[:, :], wsgn[:, :], OP.mult, [pf + "q", pf + "s"], [pf + "q"])
            for kt in range(22):
                gkt = h * 22 + kt
                trans(PC[:, :], wq[:, kt * 128:kt * 128 + 128], [pf + "q"], ["PC"])
                tcopy("vector", q2sl[:, gkt * 128:gkt * 128 + 128], PC[:, :], ["PC"], ["q2sl"])
        dma("scalar", q2L0_s[m], q2sl[:, :], ["q2sl"], ["q2L0"])
    pr.barrier()
    ESB15.close()

    ESB = ExitStack()
    qx_t = [sb(f"qx{k}", [128, T], BF16, stack=ESB) for k in range(16)]
    for k in range(16):
        dma("sync", qx_t[k][:, :], qx_s[k], ["qxs%d" % k, f"qx{k}"], [f"qx{k}"])
    wgsl = sb("wgsl", [128, 2048], stack=ESB)
    wcol = sb("wcolB", [128, 1], stack=ESB)
    winv = sb("winvB", [128, 1], stack=ESB)
    gyS = [sb(f"gyt{i}", [128, 2048], stack=ESB) for i in range(2)]
    gy0S = [sb(f"gy0t{i}", [128, 2048], stack=ESB) for i in range(2)]
    gqS = [sb(f"gqt{i}", [128, 2048], stack=ESB) for i in range(2)]
    gsgnS = [sb(f"gsgnt{i}", [128, 2048], stack=ESB) for i in range(2)]
    s1g = sb("s1g", [128, 1], stack=ESB)
    s1u = sb("s1u", [128, 1], stack=ESB)
    q1g = sb("q1g", [128, 2048], BF16, stack=ESB)
    q1u = sb("q1u", [128, 2048], BF16, stack=ESB)
    gt_sb = sb("gt_sb", [128, 512], stack=ESB)
    ut_sb = sb("ut_sb", [128, 512], stack=ESB)

    def quant_gup(mc, scol, skey, qdst, qkey):
        pp = mc % 2
        gy, gy0, gq, gsgn = gyS[pp], gy0S[pp], gqS[pp], gsgnS[pp]
        pf = "g%d" % pp
        dma("sync", wgsl[:, :], w_gup_p[mc], ["wgsl"], ["wgsl"])
        redmax_free("vector", wcol[:, :], wgsl[:, :], ["wgsl", "wcol"], ["wcol"], absval=True)
        ts("vector", scol, wcol[:, :], 1.0 / 127.0, 1e-8, OP.mult, OP.max, ["wcol"], [skey])
        recip(winv[:, :], scol, [skey, "winv"], ["winv"])
        act(gy[:, :], wgsl[:, :], AF.Abs, ["wgsl", pf + "y"], [pf + "y"])
        act(gsgn[:, :], wgsl[:, :], AF.Sign, ["wgsl", pf + "s"], [pf + "s"])
        ts("vector", gy[:, :], gy[:, :], winv[:, :], None, OP.mult, None,
           [pf + "y", "winv"], [pf + "y"])
        ladder_round(pf, gy[:, :], gy[:, :], gy0[:, :], gq[:, :], 2048,
                     [64, 32, 16, 8, 4, 2, 1], "vector", [pf + "y"])
        tt("vector", gq[:, :], gq[:, :], gsgn[:, :], OP.mult, [pf + "q", pf + "s"], [pf + "q"])
        for kt in range(16):
            trans(PC[:, :], gq[:, kt * 128:kt * 128 + 128], [pf + "q"], ["PC"])
            tcopy("vector", qdst[:, kt * 128:kt * 128 + 128], PC[:, :], ["PC"], [qkey])

    for p in range(44):
        quant_gup(p, s1g[:, :], "s1gk", q1g, "q1gk")
        quant_gup(44 + p, s1u[:, :], "s1uk", q1u, "q1uk")
        if debug and p == 0:
            dma("scalar", dbg["d_q1"][0], q1g[:, :], ["q1gk"], ["dbgq1"])
            dma("scalar", dbg["d_q1"][1], q1u[:, :], ["q1uk"], ["dbgq1"])
            dma("scalar", dbg["d_s1"][0:1, :], s1g[:, :], ["s1gk"], ["dbgs1"])
            dma("scalar", dbg["d_s1"][1:2, :], s1u[:, :], ["s1uk"], ["dbgs1"])
        for ti, (t0, tc) in enumerate(TCH):
            for kt in range(16):
                mm(P0[:, 0:tc], q1g[:, kt * 128:kt * 128 + 128], qx_t[kt][:, t0:t0 + tc],
                   kt == 0, kt == 15, ["q1gk", f"qx{kt}"], ["P0"])
            for kt in range(16):
                mm(P1[:, 0:tc], q1u[:, kt * 128:kt * 128 + 128], qx_t[kt][:, t0:t0 + tc],
                   kt == 0, kt == 15, ["q1uk", f"qx{kt}"], ["P1"])
            stt("vector", gt_sb[:, 0:tc], P0[:, 0:tc], s1g[:, :], sxT[:, t0:t0 + tc],
                OP.mult, OP.mult, ["P0", "s1gk", "sxs"], ["gt"])
            stt("vector", ut_sb[:, 0:tc], P1[:, 0:tc], s1u[:, :], sxT[:, t0:t0 + tc],
                OP.mult, OP.mult, ["P1", "s1uk", "sxs"], ["ut"])
            if debug and p == 0 and ti == 0:
                dma("scalar", dbg["d_gu"][0, :, 0:tc], gt_sb[:, 0:tc], ["gt"], ["dbggu"])
                dma("scalar", dbg["d_gu"][1, :, 0:tc], ut_sb[:, 0:tc], ["ut"], ["dbggu"])
            act(gt_sb[:, 0:tc], gt_sb[:, 0:tc], AF.Silu, ["gt"], ["gt"])
            tt("gpsimd", gt_sb[:, 0:tc], gt_sb[:, 0:tc], ut_sb[:, 0:tc], OP.mult,
               ["gt", "ut"], ["gt"])
            dma("scalar", h0_s[p, :, t0:t0 + tc], gt_sb[:, 0:tc], ["gt"], ["h0s%d" % p])
            if debug:
                dma("scalar", dbg["d_h"][p, :, t0:t0 + tc], gt_sb[:, 0:tc], ["gt"], ["dbgh"])
    pr.barrier()
    ESB.close()

    ESB2 = ExitStack()
    ht_t = [sb(f"ht{p}", [128, 256], stack=ESB2) for p in range(44)]
    hq_t = [sb(f"hq{p}", [128, 256], BF16, stack=ESB2) for p in range(44)]
    q2dl = [sb(f"q2dl{i}", [128, 5632], BF16, stack=ESB2) for i in range(2)]
    x2t = [sb(f"x2t{m}", [128, 256], stack=ESB2) for m in range(16)]
    b2hm = sb("b2hm", [128, 256], stack=ESB2)
    b2inv = sb("b2inv", [128, 256], stack=ESB2)
    b2aS = [sb(f"b2at{i}", [128, 256], stack=ESB2) for i in range(2)]
    b2sS = [sb(f"b2st{i}", [128, 256], stack=ESB2) for i in range(2)]
    b2yS = [sb(f"b2yt{i}", [128, 256], stack=ESB2) for i in range(2)]
    b2y0S = [sb(f"b2y0t{i}", [128, 256], stack=ESB2) for i in range(2)]
    b2qS = [sb(f"b2qt{i}", [128, 256], stack=ESB2) for i in range(2)]
    b2sq = sb("b2sq", [128, 256], stack=ESB2)
    shB = sb("shB", [128, 256], stack=ESB2)
    TC2 = split_cols(0, T, 256)
    for ti, (t0, tc) in enumerate(TC2):
        for p in range(44):
            dma("sync", ht_t[p][:, 0:tc], h0_s[p, :, t0:t0 + tc], ["h0s%d" % p, f"ht{p}"], [f"ht{p}"])
            act(b2aS[p % 2][:, 0:tc], ht_t[p][:, 0:tc], AF.Abs,
                [f"ht{p}", "b2%da" % (p % 2)], ["b2%da" % (p % 2)])
            if p == 0:
                tcopy("vector", b2hm[:, 0:tc], b2aS[0][:, 0:tc], ["b20a"], ["b2hm"])
            else:
                tt("vector", b2hm[:, 0:tc], b2hm[:, 0:tc], b2aS[p % 2][:, 0:tc], OP.max,
                   ["b2hm", "b2%da" % (p % 2)], ["b2hm"])
        tree_max(b2hm, "b2hm", b2yS[0], "b20y", tc)
        ts("vector", b2hm[0:1, 0:tc], b2hm[0:1, 0:tc], 1.0 / 127.0, 1e-8, OP.mult, OP.max,
           ["b2hm"], ["b2hm"])
        mm(P2[:, 0:tc], ones_row[:, :], b2hm[0:1, 0:tc], True, True, ["oner", "b2hm"], ["P2"])
        tcopy("vector", shB[:, 0:tc], P2[:, 0:tc], ["P2"], ["shB"])
        recip(b2inv[:, 0:tc], shB[:, 0:tc], ["shB"], ["b2invB"])
        for p in range(44):
            pp = p % 2
            pf = "b2%d" % pp
            ba, bs_, by, by0, bq_ = b2aS[pp], b2sS[pp], b2yS[pp], b2y0S[pp], b2qS[pp]
            act(ba[:, 0:tc], ht_t[p][:, 0:tc], AF.Abs, [f"ht{p}", pf + "a"], [pf + "a"])
            act(bs_[:, 0:tc], ht_t[p][:, 0:tc], AF.Sign, [f"ht{p}", pf + "s"], [pf + "s"])
            tt("gpsimd", by[:, 0:tc], ba[:, 0:tc], b2inv[:, 0:tc], OP.mult,
               [pf + "a", "b2invB", pf + "y"], [pf + "y"])
            ladder_round(pf, by[:, 0:tc], by[:, 0:tc], by0[:, 0:tc], bq_[:, 0:tc], tc,
                         [64, 32, 16, 8, 4, 2, 1], "vector", [pf + "y"])
            tt("gpsimd", bq_[:, 0:tc], bq_[:, 0:tc], bs_[:, 0:tc], OP.mult,
               [pf + "q", pf + "s"], [pf + "q"])
            tcopy("vector", hq_t[p][:, 0:tc], bq_[:, 0:tc], [pf + "q"], [f"hq{p}"])
        for m in range(16):
            dma("sync", q2dl[m % 2][:, :], q2L0_s[m], ["q2L0", "q2d%d" % (m % 2)], ["q2d%d" % (m % 2)])
            pq_ = PQ[m % 4]
            pk = "P%d" % (m % 4)
            for kt in range(44):
                mm(pq_[:, 0:tc], q2dl[m % 2][:, kt * 128:kt * 128 + 128], hq_t[kt][:, 0:tc],
                   kt == 0, kt == 43, ["q2d%d" % (m % 2), f"hq{kt}"], [pk])
            dma("sync", x2t[m][:, 0:tc], x1_s[m, :, t0:t0 + tc], ["x1s%d" % m, f"B2x{m}"], [f"B2x{m}"])
            stt("vector", b2sq[:, 0:tc], pq_[:, 0:tc], s2all[:, m:m + 1], shB[:, 0:tc],
                OP.mult, OP.mult, [pk, "s2all", "shB"], ["b2o"])
            tt("vector", x2t[m][:, 0:tc], x2t[m][:, 0:tc], b2sq[:, 0:tc], OP.add,
               [f"B2x{m}", "b2o"], [f"B2x{m}"])
        layer_norm(x2t, [f"B2x{m}" for m in range(16)], tc)
        for m in range(16):
            dma("scalar", x2_s[m, :, t0:t0 + tc], x2t[m][:, 0:tc], [f"B2x{m}"], ["x2s%d" % m])
            if debug:
                dma("scalar", dbg["d_x2"][m, :, t0:t0 + tc], x2t[m][:, 0:tc],
                    [f"B2x{m}"], ["dbg2"])
    pr.barrier()
    ESB2.close()

    ESC = ExitStack()
    x2c = [sb(f"x2c{k}", [128, 512], stack=ESC) for k in range(16)]
    wsl2 = [sb(f"wsl2_{i}", [128, 2048], stack=ESC) for i in range(2)]
    bq_sb = sb("bq_sb", [128, 16], stack=ESC)
    bk_sb = sb("bk_sb", [128, 16], stack=ESC)
    bv_sb = sb("bv_sb", [128, 16], stack=ESC)
    bo_sb = sb("bo_sb", [128, 16], stack=ESC)
    carry = sb("carry", [128, 16], stack=ESC)
    kmin = sb("kmin", [128, 512], stack=ESC)
    ek = sb("ek", [128, 512], stack=ESC)
    krel = sb("krel", [128, 512], stack=ESC)
    pkt = sb("pkt", [128, 512], stack=ESC)
    vt_sb = sb("vt_sb", [128, 512], stack=ESC)
    kvt = sb("kvt", [128, 512], stack=ESC)
    dma("sync", bq_sb[:, :], bq_d[:, :], [], ["bq"])
    dma("sync", bk_sb[:, :], bk_d[:, :], [], ["bk"])
    dma("sync", bv_sb[:, :], bv_d[:, :], [], ["bv"])
    dma("sync", bo_sb[:, :], bo_d[:, :], [], ["bo"])
    mset("vector", carry[:, :], 0.0, ["carry"])

    def phi_from_psum(pap, pk, bcol, bkey, out, okey):
        ts("vector", kmin[:, 0:tc], pap[:, 0:tc], bcol, 0.0, OP.add, OP.min,
           [pk, bkey], ["kmin"])
        act(ek[:, 0:tc], kmin[:, 0:tc], AF.Exp, ["kmin"], ["ek"])
        ts("vector", krel[:, 0:tc], pap[:, 0:tc], bcol, 0.0, OP.add, OP.max,
           [pk, bkey], ["krel"])
        tt("gpsimd", out[:, 0:tc], ek[:, 0:tc], krel[:, 0:tc], OP.add, ["ek", "krel"], [okey])

    for ti, (t0, tc) in enumerate(TCH):
        for k in range(16):
            dma("sync", x2c[k][:, 0:tc], x2_s[k, :, t0:t0 + tc], ["x2s%d" % k, f"x2c{k}"], [f"x2c{k}"])
        for m in range(16):
            wb = m % 2
            dma("sync", wsl2[wb][:, :], wk_p[m], ["wsl2_%d" % wb], ["wsl2_%d" % wb])
            for kt in range(16):
                mm(P0[:, 0:tc], wsl2[wb][:, kt * 128:kt * 128 + 128], x2c[kt][:, 0:tc],
                   kt == 0, kt == 15, ["wsl2_%d" % wb, f"x2c{kt}"], ["P0"])
            phi_from_psum(P0, "P0", bk_sb[:, m:m + 1], "bk", pkt, "pkt")
            dma("sync", wsl2[wb][:, :], wv_p[m], ["wsl2_%d" % wb], ["wsl2_%d" % wb])
            for kt in range(16):
                mm(P1[:, 0:tc], wsl2[wb][:, kt * 128:kt * 128 + 128], x2c[kt][:, 0:tc],
                   kt == 0, kt == 15, ["wsl2_%d" % wb, f"x2c{kt}"], ["P1"])
            ts("vector", vt_sb[:, 0:tc], P1[:, 0:tc], bv_sb[:, m:m + 1], None, OP.add, None,
               ["P1", "bv"], ["vt"])
            tt("gpsimd", pkt[:, 0:tc], pkt[:, 0:tc], vt_sb[:, 0:tc], OP.mult,
               ["pkt", "vt"], ["pkt"])
            scan_op("vector", kvt[:, 0:tc], ones[:, 0:tc], pkt[:, 0:tc], ["ones", "pkt"], ["kvt"])
            ts("gpsimd", kvt[:, 0:tc], kvt[:, 0:tc], carry[:, m:m + 1], None, OP.add, None,
               ["kvt", "carry"], ["kvt"])
            tcopy("vector", carry[:, m:m + 1], kvt[:, tc - 1:tc], ["kvt", "carry"], ["carry"])
            dma("scalar", kv_s[m, :, t0:t0 + tc], kvt[:, 0:tc], ["kvt"], ["kvs%d" % m])
            if debug:
                dma("scalar", dbg["d_kv"][m, :, t0:t0 + tc], kvt[:, 0:tc], ["kvt"], ["dbgkv"])
            dma("sync", wsl2[wb][:, :], wq_p[m], ["wsl2_%d" % wb], ["wsl2_%d" % wb])
            for kt in range(16):
                mm(P2[:, 0:tc], wsl2[wb][:, kt * 128:kt * 128 + 128], x2c[kt][:, 0:tc],
                   kt == 0, kt == 15, ["wsl2_%d" % wb, f"x2c{kt}"], ["P2"])
            phi_from_psum(P2, "P2", bq_sb[:, m:m + 1], "bq", pkt, "pkt")
            dma("scalar", pq_s[m, :, t0:t0 + tc], pkt[:, 0:tc], ["pkt"], ["pqs%d" % m])

    ct2_sb = sb("ct2_sb", [16, 128], stack=ESC)
    gat2_sb = sb("gat2_sb", [128, 128], stack=ESC)
    offc_sb = sb("offc_sb", [128, 16], stack=ESC)
    trans(PC[0:16, 0:128], carry[:, :], ["carry"], ["PC"])
    tcopy("vector", ct2_sb[:, :], PC[0:16, 0:128], ["PC"], ["ct2"])
    dma("scalar", cc2_in.ap()[:, :], ct2_sb[:, :], ["ct2"], ["cc2i"])
    coll("cc2", cc2_in, cc2_out, ["cc2i"], ["cc2o"])
    dma("sync", gat2_sb[:, :], cc2_out.ap()[:, :], ["cc2o"], ["gat2"])
    mm(PC[0:16, 0:128], maskexp_sb[:, :], gat2_sb[:, :], True, True, ["maskexp", "gat2"], ["PC"])
    tcopy("vector", ct2_sb[:, :], PC[0:16, 0:128], ["PC"], ["off1"])
    trans(PC[:, 0:16], ct2_sb[:, :], ["off1"], ["PC"], k=16)
    tcopy("vector", offc_sb[:, :], PC[:, 0:16], ["PC"], ["offc"])

    kvc = [sb(f"kvc{k}", [128, 512], stack=ESC) for k in range(16)]
    x3h_t = sb("x3h_t", [128, 512], BF16, stack=ESC)
    x3l_t = sb("x3l_t", [128, 512], BF16, stack=ESC)
    c2r = [sb(f"c2r{i}", [1, 512], stack=ESC) for i in range(3)]
    c2sq = sb("c2sq", [128, 512], stack=ESC)
    for ti, (t0, tc) in enumerate(TCH):
        for k in range(16):
            dma("sync", kvc[k][:, 0:tc], kv_s[k, :, t0:t0 + tc], ["kvs%d" % k, f"kvc{k}"], [f"kvc{k}"])
            ts("vector", kvc[k][:, 0:tc], kvc[k][:, 0:tc], offc_sb[:, k:k + 1], SCALE_L1,
               OP.add, OP.mult, [f"kvc{k}", "offc"], [f"kvc{k}"])
            dma("sync", x2c[k][:, 0:tc], pq_s[k, :, t0:t0 + tc], ["pqs%d" % k, f"x2c{k}"], [f"x2c{k}"])
            tt("gpsimd", kvc[k][:, 0:tc], kvc[k][:, 0:tc], x2c[k][:, 0:tc], OP.mult,
               [f"kvc{k}", f"x2c{k}"], [f"kvc{k}"])
        for m in range(16):
            wb = m % 2
            dma("sync", wsl2[wb][:, :], wo_p[m], ["wsl2_%d" % wb], ["wsl2_%d" % wb])
            pq_ = PQ[m % 4]
            pk = "P%d" % (m % 4)
            for kt in range(16):
                mm(pq_[:, 0:tc], wsl2[wb][:, kt * 128:kt * 128 + 128], kvc[kt][:, 0:tc],
                   kt == 0, kt == 15, ["wsl2_%d" % wb, f"kvc{kt}"], [pk])
            dma("sync", x2c[m][:, 0:tc], x2_s[m, :, t0:t0 + tc], ["x2s%d" % m, f"x2c{m}"], [f"x2c{m}"])
            ts("vector", c2sq[:, 0:tc], pq_[:, 0:tc], bo_sb[:, m:m + 1], None, OP.add, None,
               [pk, "bo"], ["c2o"])
            tt("vector", x2c[m][:, 0:tc], x2c[m][:, 0:tc], c2sq[:, 0:tc], OP.add,
               [f"x2c{m}", "c2o"], [f"x2c{m}"])
        layer_norm(x2c, [f"x2c{m}" for m in range(16)], tc)
        for m in range(16):
            dma("scalar", x3_s[m, :, t0:t0 + tc], x2c[m][:, 0:tc], [f"x2c{m}"], ["x3s%d" % m])
            tcopy("vector", x3h_t[:, 0:tc], x2c[m][:, 0:tc], [f"x2c{m}"], ["x3ht"])
            tt("vector", x3l_t[:, 0:tc], x2c[m][:, 0:tc], x3h_t[:, 0:tc], OP.subtract,
               [f"x2c{m}", "x3ht"], ["x3lt"])
            dma("scalar", x3h_s[m, :, t0:t0 + tc], x3h_t[:, 0:tc], ["x3ht"], ["x3hs%d" % m])
            dma("scalar", x3l_s[m, :, t0:t0 + tc], x3l_t[:, 0:tc], ["x3lt"], ["x3ls%d" % m])
            if debug:
                dma("scalar", dbg["d_x3"][m, :, t0:t0 + tc], x2c[m][:, 0:tc],
                    [f"x2c{m}"], ["dbg3"])
    pr.barrier()
    ESC.close()

    ESX = ExitStack()
    xmT = sb("xmT", [128, T], stack=ESX)
    ESDw = ExitStack()
    s1c = sb("s1c", [128, 1], stack=ESDw)
    th1c = sb("th1c", [128, 1], stack=ESDw)
    s2c_ = sb("s2c_", [128, 1], stack=ESDw)
    th2c = sb("th2c", [128, 1], stack=ESDw)

    ESD0 = ExitStack()
    djunk = sb("djunk", [128, 8192], stack=ESD0)
    dacc1 = sb("dacc1", [128, 1], stack=ESD0)
    dacc2 = sb("dacc2", [128, 1], stack=ESD0)
    dtmp = sb("dtmp", [128, 1], stack=ESD0)
    cfsl = [sb(f"cfsl{i}", [128, 2048], stack=ESD0) for i in range(2)]
    cpsl = sb("cpsl", [128, 8192], stack=ESD0)
    dsg = sb("dsg", [128, 8192], stack=ESD0)
    dqb = sb("dqb", [128, 8192], BF16, stack=ESD0)
    mset("vector", dacc1[:, :], 0.0, ["dacc1"])
    mset("vector", dacc2[:, :], 0.0, ["dacc2"])
    for mc in range(64):
        wb = mc % 2
        dma("sync", cfsl[wb][:, :], cfc_p[mc], ["cfsl%d" % wb], ["cfsl%d" % wb])
        act(djunk[:, 0:2048], cfsl[wb][:, :], AF.Abs, ["cfsl%d" % wb, "djunk", "dtmp"],
            ["djunk", "dtmp"], accum=dtmp[:, :])
        tt("vector", dacc1[:, :], dacc1[:, :], dtmp[:, :], OP.add, ["dacc1", "dtmp"], ["dacc1"])
    for mc in range(16):
        dma("sync", cpsl[:, :], cprT_p[mc], ["cpsl"], ["cpsl"])
        act(djunk[:, :], cpsl[:, :], AF.Abs, ["cpsl", "djunk", "dtmp"], ["djunk", "dtmp"],
            accum=dtmp[:, :])
        tt("vector", dacc2[:, :], dacc2[:, :], dtmp[:, :], OP.add, ["dacc2", "dtmp"], ["dacc2"])
    for (dac, dk, cA, ck, tA, tk) in ((dacc1, "dacc1", s1c, "s1c", th1c, "th1"),
                                      (dacc2, "dacc2", s2c_, "s2c", th2c, "th2")):
        mm(PR1[0:1, 0:1], ones_col[:, :], dac[:, 0:1], True, True, ["onec", dk], ["PR1"])
        tcopy("vector", dac[0:1, 0:1], PR1[0:1, 0:1], ["PR1", dk], [dk])
        ts("vector", dac[0:1, 0:1], dac[0:1, 0:1], 1.0 / (2048.0 * 8192.0), 1e-5,
           OP.mult, OP.max, [dk], [dk])
        mm(PC[:, 0:1], ones_row[:, :], dac[0:1, 0:1], True, True, ["oner", dk], ["PC"])
        tcopy("vector", cA[:, :], PC[:, 0:1], ["PC"], [ck])
        ts("vector", tA[:, :], cA[:, :], 0.5, None, OP.mult, None, [ck], [tk])
    tcopy("vector", s2cL1[:, :], s2c_[:, :], ["s2c"], ["s2cl1"])

    def ternary(slab, skey, jnk, jkey, sgnb, sgkey, qout, qkey, th, thkey, n, eng):
        act(jnk[:, 0:n], slab, AF.Abs, [skey, jkey], [jkey])
        act(sgnb[:, 0:n], slab, AF.Sign, [skey, sgkey], [sgkey])
        ts("vector", jnk[:, 0:n], jnk[:, 0:n], th, None, OP.is_gt, None,
           [jkey, thkey], [jkey])
        tt(eng, jnk[:, 0:n], jnk[:, 0:n], sgnb[:, 0:n], OP.mult, [jkey, sgkey], [jkey])
        tcopy(eng, qout[:, 0:n], jnk[:, 0:n], [jkey], [qkey])

    for mc in range(16):
        dma("sync", cpsl[:, :], cprT_p[mc], ["cpsl"], ["cpsl"])
        ternary(cpsl[:, :], "cpsl", djunk, "djunk", dsg, "dsg", dqb, "dqb",
                th2c[:, :], "th2", 8192, VG[mc % 2])
        dma("scalar", q2L1_s[mc], dqb[:, :], ["dqb"], ["q2L1"])
    pr.barrier()
    ESD0.close()

    ESD1 = ExitStack()
    x3h_r = [sb(f"x3hr{k}", [128, T], BF16, stack=ESD1) for k in range(16)]
    x3l_r = [sb(f"x3lr{k}", [128, T], BF16, stack=ESD1) for k in range(16)]
    for k in range(16):
        dma("sync", x3h_r[k][:, :], x3h_s[k], ["x3hs%d" % k, f"x3hr{k}"], [f"x3hr{k}"])
        dma("sync", x3l_r[k][:, :], x3l_s[k], ["x3ls%d" % k, f"x3lr{k}"], [f"x3lr{k}"])
    cfsl = [sb(f"cfslB{i}", [128, 2048], stack=ESD1) for i in range(2)]
    dj2 = sb("dj2", [128, 2048], stack=ESD1)
    dsg2 = sb("dsg2", [128, 2048], stack=ESD1)
    dqb2 = sb("dqb2", [128, 2048], BF16, stack=ESD1)
    drh = sb("drh", [128, 512], stack=ESD1)
    drt = sb("drt", [128, 512], stack=ESD1)
    mset("vector", xmT[:, :], 0.0, ["rmaxr"])
    for mc in range(64):
        wb = mc % 2
        dma("sync", cfsl[wb][:, :], cfc_p[mc], ["cfsl%d" % wb], ["cfsl%d" % wb])
        ternary(cfsl[wb][:, :], "cfsl%d" % wb, dj2, "dj2", dsg2, "dsg2", dqb2, "dqb2",
                th1c[:, :], "th1", 2048, VG[mc % 2])
        for ti, (t0, tc) in enumerate(TCH):
            pq_ = PQ[ti % 4]
            pk = "P%d" % (ti % 4)
            for kt in range(16):
                mm(pq_[:, 0:tc], dqb2[:, kt * 128:kt * 128 + 128], x3h_r[kt][:, t0:t0 + tc],
                   kt == 0, False, ["dqb2", f"x3hr{kt}"], [pk])
            for kt in range(16):
                mm(pq_[:, 0:tc], dqb2[:, kt * 128:kt * 128 + 128], x3l_r[kt][:, t0:t0 + tc],
                   False, kt == 15, ["dqb2", f"x3lr{kt}"], [pk])
            ts("vector", drh[:, 0:tc], pq_[:, 0:tc], s1c[:, :], 0.0, OP.mult, OP.max,
               [pk, "s1c"], ["drh"])
            act(drt[:, 0:tc], drh[:, 0:tc], AF.Square, ["drh"], ["drt"])
            tt("vector", xmT[:, t0:t0 + tc], xmT[:, t0:t0 + tc], drt[:, 0:tc], OP.max,
               ["rmaxr", "drt"], ["rmaxr"])
            dma("scalar", r_s[mc, :, t0:t0 + tc], drt[:, 0:tc], ["drt"], ["rs%d" % mc])
    for (t0, tc) in TCH:
        tcopy("vector", drh[:, 0:tc], xmT[:, t0:t0 + tc], ["rmaxr"], ["drh"])
        tree_max(drh, "drh", drt, "drt", tc)
        ts("vector", drh[0:1, 0:tc], drh[0:1, 0:tc], 1.0, 1e-5, OP.mult, OP.max,
           ["drh"], ["drh"])
        mm(P2[:, 0:tc], ones_row[:, :], drh[0:1, 0:tc], True, True, ["oner", "drh"], ["P2"])
        tcopy("vector", xmT[:, t0:t0 + tc], P2[:, 0:tc], ["P2", "rmaxr"], ["xmr"])
    pr.barrier()
    ESD1.close()
    pr.barrier()
    ESDw.close()

    ESD2 = ExitStack()
    rt_t = [sb(f"rt_t{i}", [128, 256], stack=ESD2) for i in range(2)]
    hqL = [sb(f"hqL{k}", [128, 256], BF16, stack=ESD2) for k in range(64)]
    q2sl2 = [sb(f"q2sl2{i}", [128, 8192], BF16, stack=ESD2) for i in range(2)]
    x3c = [sb(f"x3c{m}", [128, 256], stack=ESD2) for m in range(16)]
    d2inv = sb("d2inv", [128, 256], stack=ESD2)
    d2yS = [sb(f"d2yt{i}", [128, 256], stack=ESD2) for i in range(2)]
    d2y0S = [sb(f"d2y0t{i}", [128, 256], stack=ESD2) for i in range(2)]
    d2qS = [sb(f"d2qt{i}", [128, 256], stack=ESD2) for i in range(2)]
    d2sq = sb("d2sq", [128, 256], stack=ESD2)
    d2o = sb("d2o", [128, 256], stack=ESD2)
    d2sc = sb("d2sc", [128, 256], stack=ESD2)
    for ti, (t0, tc) in enumerate(TC2):
        recip(d2inv[:, 0:tc], xmT[:, t0:t0 + tc], ["xmr"], ["d2invB"])
        ts("vector", d2inv[:, 0:tc], d2inv[:, 0:tc], 255.0, None, OP.mult, None,
           ["d2invB"], ["d2invB"])
        ts("vector", d2sc[:, 0:tc], xmT[:, t0:t0 + tc], s2cL1[:, 0:1], 1.0 / 255.0,
           OP.mult, OP.mult, ["xmr", "s2cl1"], ["d2scB"])
        for kt in range(64):
            pp = kt % 2
            pf = "d2%d" % pp
            dy, dy0, dq = d2yS[pp], d2y0S[pp], d2qS[pp]
            dma("sync", rt_t[pp][:, 0:tc], r_s[kt, :, t0:t0 + tc],
                ["rs%d" % kt, "rt%d" % pp], ["rt%d" % pp])
            tt("gpsimd", dy[:, 0:tc], rt_t[pp][:, 0:tc], d2inv[:, 0:tc], OP.mult,
               ["rt%d" % pp, "d2invB", pf + "y"], [pf + "y"])
            ladder_round(pf, dy[:, 0:tc], dy[:, 0:tc], dy0[:, 0:tc], dq[:, 0:tc], tc,
                         [128, 64, 32, 16, 8, 4, 2, 1], "vector", [pf + "y"])
            tcopy("vector", hqL[kt][:, 0:tc], dq[:, 0:tc], [pf + "q"], [f"hqL{kt}"])
        for m in range(16):
            wb = m % 2
            dma("sync", q2sl2[wb][:, :], q2L1_s[m], ["q2L1", "q2s2%d" % wb], ["q2s2%d" % wb])
            pq_ = PQ[m % 4]
            pk = "P%d" % (m % 4)
            for kt in range(64):
                mm(pq_[:, 0:tc], q2sl2[wb][:, kt * 128:kt * 128 + 128], hqL[kt][:, 0:tc],
                   kt == 0, kt == 63, ["q2s2%d" % wb, f"hqL{kt}"], [pk])
            dma("sync", x3c[m][:, 0:tc], x3_s[m, :, t0:t0 + tc], ["x3s%d" % m, f"x3c{m}"], [f"x3c{m}"])
            tt("vector", d2o[:, 0:tc], pq_[:, 0:tc], d2sc[:, 0:tc], OP.mult,
               [pk, "d2scB"], ["d2o"])
            tt("vector", x3c[m][:, 0:tc], x3c[m][:, 0:tc], d2o[:, 0:tc], OP.add,
               [f"x3c{m}", "d2o"], [f"x3c{m}"])
        layer_norm(x3c, [f"x3c{m}" for m in range(16)], tc)
        for m in range(16):
            dma("scalar", outT[m * 128:m * 128 + 128, t0:t0 + tc], x3c[m][:, 0:tc],
                [f"x3c{m}"], ["outd"])
    pr.barrier()
    ESD2.close()
    ESX.close()

    pr.finalize(nc, ES)
    ES.close()
    nc._pr = pr
    return nc


def _pack_lhsT(w, mtile=128):
    M, K = w.shape
    nmc = (M + mtile - 1) // mtile
    pad = nmc * mtile - M
    if pad:
        w = np.concatenate([w, np.zeros((pad, K), w.dtype)], 0)
    nkt = K // 128
    a = w.reshape(nmc, mtile, nkt, 128).transpose(0, 3, 2, 1)
    return np.ascontiguousarray(a.reshape(nmc, 128, K))


def kernel(L0_attn_u, L0_attn_W_in, L0_attn_conv_w, L0_attn_conv_b, L0_attn_dt_bias,
           L0_attn_A_log, L0_attn_D, L0_attn_norm_w, L0_attn_W_out, L0_ffn_w_gate_up,
           L0_ffn_w_down, L1_attn_Wq, L1_attn_bq, L1_attn_Wk, L1_attn_bk, L1_attn_Wv,
           L1_attn_bv, L1_attn_Wo, L1_attn_bo, L1_ffn_c_fc, L1_ffn_c_proj):
    S = L0_attn_u.shape[1]
    T = S // NCORES
    nc = build(T)
    maps = make_inmaps(T, L0_attn_u, L0_attn_W_in, L0_attn_conv_w, L0_attn_conv_b,
                       L0_attn_dt_bias, L0_attn_A_log, L0_attn_D, L0_attn_norm_w,
                       L0_attn_W_out, L0_ffn_w_gate_up, L0_ffn_w_down, L1_attn_Wq,
                       L1_attn_bq, L1_attn_Wk, L1_attn_bk, L1_attn_Wv, L1_attn_bv,
                       L1_attn_Wo, L1_attn_bo, L1_ffn_c_fc, L1_ffn_c_proj)
    res = run_bass_kernel_spmd(nc, maps, core_ids=list(range(NCORES)))
    out = np.concatenate([np.ascontiguousarray(r["outT"]).T for r in res.results], axis=0)
    return out.reshape(1, S, D).astype(np.float32)


def make_inmaps(T, u, W_in, conv_w, conv_b, dt_bias, A_log, Dh, norm_w, W_out,
                w_gate_up, w_down, Wq, bq, Wk, bk, Wv, bv, Wo, bo, c_fc, c_proj):
    f32 = np.float32
    uT_full = np.ascontiguousarray(np.asarray(u, f32)[0].T)
    S = uT_full.shape[1]
    uT_pad = np.concatenate([np.zeros((D, 3), f32), uT_full], axis=1)
    shared = {}
    shared["w_in_p"] = _pack_lhsT(np.asarray(W_in, f32))
    cw = np.asarray(conv_w, f32)[:, 0, :]
    shared["convw"] = np.ascontiguousarray(
        cw.reshape(34, 128, 4).transpose(1, 0, 2).reshape(128, 34 * 4))
    shared["convb"] = np.ascontiguousarray(np.asarray(conv_b, f32).reshape(34, 128).T)
    shared["dtb"] = np.asarray(dt_bias, f32).reshape(64, 1)
    shared["alog"] = np.asarray(A_log, f32).reshape(64, 1)
    shared["drep"] = np.ascontiguousarray(
        np.repeat(np.asarray(Dh, f32), 64).reshape(32, 128).T)
    shared["nw"] = np.ascontiguousarray(np.asarray(norm_w, f32).reshape(32, 128).T)
    shared["w_out_p"] = _pack_lhsT(np.asarray(W_out, f32))
    shared["w_gup_p"] = np.ascontiguousarray(np.asarray(w_gate_up, f32).reshape(88, 128, 2048))
    shared["w_down_p"] = np.ascontiguousarray(np.asarray(w_down, f32).reshape(16, 128, 5632))
    for nm, W in (("wq_p", Wq), ("wk_p", Wk), ("wv_p", Wv), ("wo_p", Wo)):
        shared[nm] = _pack_lhsT(np.asarray(W, f32))
    for nm, b in (("bq", bq), ("bk", bk), ("bv", bv), ("bo", bo)):
        shared[nm] = np.ascontiguousarray(np.asarray(b, f32).reshape(16, 128).T)
    cfc = np.asarray(c_fc, f32)
    shared["cfc_p"] = np.ascontiguousarray(
        cfc.reshape(16, 128, 64, 128).transpose(2, 1, 0, 3).reshape(64, 128, 2048))
    cpr = np.asarray(c_proj, f32)
    shared["cprT_p"] = np.ascontiguousarray(
        cpr.reshape(16, 128, 64, 128).transpose(0, 3, 2, 1).reshape(16, 128, 8192))
    shared["eye"] = np.eye(128, dtype=f32)
    ohp = np.zeros((64, 32, 128), f32)
    for g in range(32):
        ohp[2 * g, g, 0:64] = 1.0
        ohp[2 * g + 1, g, 64:128] = 1.0
    shared["ohp"] = np.ascontiguousarray(ohp.reshape(64, 4096))
    si = np.arange(128)[:, None]
    tj = np.arange(128)[None, :]
    shared["am"] = np.where(si <= tj, 0.0, -1e30).astype(f32)
    maps = []
    for i in range(NCORES):
        m = dict(shared)
        m["uT"] = np.ascontiguousarray(uT_pad[:, i * T:i * T + T + 3])
        m["mask8"] = np.broadcast_to(
            (np.arange(8)[None, :] < i).astype(f32), (128, 8)).copy()
        me = np.zeros((128, 16), f32)
        for j in range(8):
            if j < i:
                me[16 * j:16 * j + 16, :] = np.eye(16, dtype=f32)
        m["maskexp"] = me
        maps.append(m)
    return maps


# version 05 (final)
# speedup vs baseline: 1.2249x; 1.0223x over previous
import numpy as np
import bisect
from contextlib import ExitStack
import concourse.bass as bass
import concourse.mybir as mybir
from concourse.bass_utils import run_bass_kernel_spmd

F32 = mybir.dt.float32
BF16 = mybir.dt.bfloat16
AF = mybir.ActivationFunctionType
OP = mybir.AluOpType
AX = mybir.AxisListType
from bass_rust import ReduceOp as RO

ENG = ("sync", "tensor", "vector", "scalar", "gpsimd")
NCORES = 8
D = 2048
DSSM = 4096
NH = 64
HD = 64
DST = 128
CONVD = 4352
DIN = 8512
I0 = 5632
H1 = 8192
SCALE_L1 = float(DST ** -0.5)


NDSEM = 14


class Prog:

    def __init__(self):
        self.ops = {e: [] for e in ENG}
        self.dnames = ["D%d" % i for i in range(NDSEM)]
        self.cnt = {e: 0 for e in ENG}
        self.cnt.update({d: 0 for d in self.dnames})
        self.vc = {e: {} for e in ENG}
        self.hist_c = {e: [0] for e in ENG}
        self.hist_s = {e: [{}] for e in ENG}
        self.wmap = {}
        self.rmap = {}
        self.nwaits = 0
        self.fence = {e: 0 for e in ENG}
        self.drr = 0
        self.small = {"wcol", "wcol2", "winv", "winvB", "s1gk", "s1uk", "s2all",
                      "dccar", "bd0", "bd1", "rowdc0", "rowdc1", "rowprev", "grow", "gcol", "carry",
                      "ct2", "off1", "offc", "s1c", "th1", "th2", "s2c", "s2cl1",
                      "dtmp", "dacc1", "dacc2", "pmc", "amx3", "b2hm", "drh",
                      "lnra", "lnrb", "lnrc", "dclj", "b2s", "b2i", "d2i", "d2s",
                      "rmsr", "acol", "dtb", "dcp0", "scolB"}

    def barrier(self):
        self.bar = {s: c for s, c in self.cnt.items() if c > 0}
        self.bar_todo = set(ENG)

    def emit(self, eng, fn, reads=(), writes=(), dma=False, sreads=()):
        need = {}

        def add(e2, c2):
            if e2 != eng and c2 > need.get(e2, 0):
                need[e2] = c2

        if getattr(self, "bar_todo", None) and eng in self.bar_todo:
            self.bar_todo.discard(eng)
            for s, c in self.bar.items():
                if s == eng:
                    if c > self.fence[eng]:
                        need[eng] = max(need.get(eng, 0), c)
                else:
                    add(s, c)

        for k in reads:
            t = self.wmap.get(k)
            if t:
                add(*t)
        for k in writes:
            t = self.wmap.get(k)
            if t:
                add(*t)
            for e2, c2 in self.rmap.get(k, {}).items():
                add(e2, c2)
        for k in sreads:
            t = self.wmap.get(k)
            if t and t[0] == eng and t[1] > self.fence[eng]:
                need[eng] = max(need.get(eng, 0), t[1])
        for k in reads:
            if k in self.small:
                t = self.wmap.get(k)
                if t and t[0] == eng and t[1] > self.fence[eng]:
                    need[eng] = max(need.get(eng, 0), t[1])
        if dma:
            selfneed = 0
            for k in list(reads) + list(writes):
                t = self.wmap.get(k)
                if t and t[0] == eng:
                    selfneed = max(selfneed, t[1])
            for k in writes:
                c2 = self.rmap.get(k, {}).get(eng, 0)
                selfneed = max(selfneed, c2)
            if selfneed > 0:
                need[eng] = max(need.get(eng, 0), selfneed)
            dn = self.dnames[self.drr % NDSEM]
            self.drr += 1
            if self.cnt[dn] > 0:
                add(dn, self.cnt[dn])
        waits = []
        if need.get(eng, 0) > self.fence[eng]:
            waits.append((eng, need[eng]))
            self.fence[eng] = need[eng]
        need.pop(eng, None)
        v = self.vc[eng]
        for e2, c2 in sorted(need.items()):
            if v.get(e2, 0) >= c2:
                continue
            waits.append((e2, c2))
            if e2 in self.hist_c:
                i = bisect.bisect_right(self.hist_c[e2], c2) - 1
                for f, cf in self.hist_s[e2][i].items():
                    if cf > v.get(f, 0):
                        v[f] = cf
            if c2 > v.get(e2, 0):
                v[e2] = c2
        self.nwaits += len(waits)
        if dma:
            sem = dn
            inc = 16
        else:
            sem = eng
            inc = 1
        self.cnt[sem] += inc
        cafter = self.cnt[sem]
        if not dma:
            v[eng] = cafter
            self.hist_c[eng].append(cafter)
            self.hist_s[eng].append(dict(v))
        self.ops[eng].append((waits, fn, inc, sem))
        for k in writes:
            self.wmap[k] = (sem, cafter)
            self.rmap[k] = {}
        for k in reads:
            self.rmap.setdefault(k, {})[sem] = cafter

    def finalize(self, nc, es):
        sems = {e: es.enter_context(nc.semaphore("sem_" + e)) for e in ENG}
        for d in self.dnames:
            sems[d] = es.enter_context(nc.semaphore("sem_" + d))
        fin = [(e, self.cnt[e]) for e in list(ENG) + self.dnames
               if e != "sync" and self.cnt[e] > 0]
        self.ops["sync"].append((fin, None, 0, "sync"))
        blk = es.enter_context(nc.Block())
        for eng in ENG:
            dec = getattr(blk, eng)
            lst = self.ops[eng]
            if not lst:
                continue

            def mk(lst=lst, eng=eng):
                @dec
                def _(e):
                    for waits, fn, inc, sem in lst:
                        for (e2, vv) in waits:
                            e.wait_ge(sems[e2], vv)
                        if fn is not None:
                            inst = fn(e)
                            inst.then_inc(sems[sem], inc)
            mk()


def split_cols(off, end, step):
    out = []
    while off < end:
        ln = min(step, end - off)
        out.append((off, ln))
        off += ln
    return out


def build(T, debug=False):
    HT = T // 2
    NCH = HT // 128
    nc = bass.Bass("TRN2", target_bir_lowering=False, debug=False)
    pr = Prog()

    def dram(name, shape, dt=F32, kind=None):
        if kind:
            return nc.dram_tensor(name, shape, dt, kind=kind)
        return nc.dram_tensor(name, shape, dt)

    uT = dram("uT", [D, T + 3], kind="ExternalInput").ap()
    w_in_p = dram("w_in_p", [67, 128, 2048], kind="ExternalInput").ap()
    convw = dram("convw", [128, 34 * 4], kind="ExternalInput").ap()
    convb = dram("convb", [128, 34], kind="ExternalInput").ap()
    dtb_d = dram("dtb", [64, 1], kind="ExternalInput").ap()
    alog_d = dram("alog", [64, 1], kind="ExternalInput").ap()
    drep_d = dram("drep", [128, 32], kind="ExternalInput").ap()
    nw_d = dram("nw", [128, 32], kind="ExternalInput").ap()
    w_out_p = dram("w_out_p", [16, 128, 4096], kind="ExternalInput").ap()
    w_gup_p = dram("w_gup_p", [88, 128, 2048], kind="ExternalInput").ap()
    w_down_p = dram("w_down_p", [16, 128, 5632], kind="ExternalInput").ap()
    wq_p = dram("wq_p", [16, 128, 2048], kind="ExternalInput").ap()
    wk_p = dram("wk_p", [16, 128, 2048], kind="ExternalInput").ap()
    wv_p = dram("wv_p", [16, 128, 2048], kind="ExternalInput").ap()
    wo_p = dram("wo_p", [16, 128, 2048], kind="ExternalInput").ap()
    bq_d = dram("bq", [128, 16], kind="ExternalInput").ap()
    bk_d = dram("bk", [128, 16], kind="ExternalInput").ap()
    bv_d = dram("bv", [128, 16], kind="ExternalInput").ap()
    bo_d = dram("bo", [128, 16], kind="ExternalInput").ap()
    cfc_p = dram("cfc_p", [64, 128, 2048], kind="ExternalInput").ap()
    cprT_p = dram("cprT_p", [16, 128, 8192], kind="ExternalInput").ap()
    eye_d = dram("eye", [128, 128], kind="ExternalInput").ap()
    ohp_d = dram("ohp", [64, 4096], kind="ExternalInput").ap()
    am_d = dram("am", [128, 128], kind="ExternalInput").ap()
    mask8_d = dram("mask8", [128, 8], kind="ExternalInput").ap()
    maskexp_d = dram("maskexp", [128, 16], kind="ExternalInput").ap()
    outT = dram("outT", [D, T], kind="ExternalOutput").ap()

    sz_s = dram("sz_s", [32, 128, T]).ap()
    yg_s = dram("yg_s", [32, 128, T]).ap()
    x1_s = dram("x1_s", [16, 128, T]).ap()
    qx_s = dram("qx_s", [16, 128, T], BF16).ap()
    h0_s = dram("h0_s", [44, 128, T]).ap()
    q2L0_s = dram("q2L0_s", [16, 128, 5632], BF16).ap()
    x2_s = dram("x2_s", [16, 128, T]).ap()
    kv_s = dram("kv_s", [16, 128, T]).ap()
    pq_s = dram("pq_s", [16, 128, T]).ap()
    x3_s = dram("x3_s", [16, 128, T]).ap()
    x3h_s = dram("x3h_s", [16, 128, T], BF16).ap()
    x3l_s = dram("x3l_s", [16, 128, T], BF16).ap()
    r_s = dram("r_s", [64, 128, T]).ap()
    q2L1_s = dram("q2L1_s", [16, 128, 8192], BF16).ap()
    cc_in = dram("cc_in", [129, 4096])
    cc_out = dram("cc_out", [8 * 129, 4096])
    cc2_in = dram("cc2_in", [16, 128])
    cc2_out = dram("cc2_out", [128, 128])
    dbg = {}
    if debug:
        for nm, shp, dt in [("d_sz", [32, 128, T], F32), ("d_yg", [32, 128, T], F32),
                            ("d_x1", [16, 128, T], F32), ("d_x2", [16, 128, T], F32),
                            ("d_x3", [16, 128, T], F32), ("d_kv", [16, 128, T], F32),
                            ("d_st", [128, 4096], F32), ("d_sin", [128, 4096], F32),
                            ("d_dc", [64, T + 1], F32), ("d_dts", [64, T], F32),
                            ("d_ygt", [32, 128, T], F32), ("d_cdb", [32, 128, T], F32),
                            ("d_p2r", [32, 128, T], F32), ("d_cda", [32, 128, T], F32),
                            ("d_h", [44, 128, T], F32), ("d_qx", [16, 128, T], F32),
                            ("d_q1", [2, 128, 2048], BF16), ("d_s1", [2, 128], F32),
                            ("d_gu", [2, 128, 512], F32), ("d_gy", [128, 2048], F32),
                            ("d_wc", [2, 128], F32), ("d_gq", [128, 2048], F32)]:
            dbg[nm] = dram(nm, shp, dt, kind="ExternalOutput").ap()

    ES = ExitStack()

    def sb(name, shape, dt=F32, stack=None):
        return (stack or ES).enter_context(nc.sbuf_tensor(name, shape, dt))

    def ps(name, shape):
        return ES.enter_context(nc.psum_tensor(name, shape, F32))

    P0 = ps("P0", [128, 512])
    P1 = ps("P1", [128, 512])
    P2 = ps("P2", [128, 512])
    P3 = ps("P3", [128, 512])
    PA = ps("PA", [128, 132])
    PC = ps("PC", [128, 128])
    PR1 = ps("PR1", [1, 512])
    PR2 = ps("PR2", [1, 512])
    PQ = [P0, P1, P2, P3]

    ones = sb("ones", [128, 1024 if T >= 2048 else T])
    ones_col = sb("ones_col", [128, 1])
    ones_row = sb("ones_row", [1, 128])
    eye_sb = sb("eye_sb", [128, 128])
    am_sb = sb("am_sb", [128, 128])
    mask8_sb = sb("mask8_sb", [128, 8])
    maskexp_sb = sb("maskexp_sb", [128, 16])
    sxT = sb("sxT", [128, T])
    s2all = sb("s2all", [128, 16])
    s2cL1 = sb("s2cL1", [128, 1])

    def dma(eng, dst, src, reads, writes):
        pr.emit(eng, lambda e, d=dst, s=src: e.dma_start(out=d, in_=s), reads, writes, dma=True)

    def mm(out, lhsT, rhs, start, stop, reads, writes):
        pr.emit("tensor", lambda e, o=out, l=lhsT, r=rhs, st=start, sp=stop:
                e.matmul(o, l, r, start=st, stop=sp), reads, writes)

    def trans(out, in_, reads, writes, k=128):
        pr.emit("tensor", lambda e, o=out, i=in_, kk=k: e.transpose(o, i, eye_sb[0:kk, 0:kk]),
                tuple(reads) + ("eye",), writes)

    def tt(eng, out, a, b, op, reads, writes):
        pr.emit(eng, lambda e, o=out, x=a, y=b, p=op: e.tensor_tensor(o, x, y, p), reads, writes)

    def _isap(v):
        return v is not None and not isinstance(v, (int, float))

    def ts(eng, out, a, s1, s2, op0, op1, reads, writes):
        sr = tuple(reads) if (_isap(s1) or _isap(s2)) else ()
        if op1 is None:
            pr.emit(eng, lambda e, o=out, x=a, u=s1, p0=op0: e.tensor_scalar(o, x, u, None, p0),
                    reads, writes, sreads=sr)
        else:
            pr.emit(eng, lambda e, o=out, x=a, u=s1, v=s2, p0=op0, p1=op1:
                    e.tensor_scalar(o, x, u, v, p0, p1), reads, writes, sreads=sr)

    def stt(eng, out, in0, sc, in1, op0, op1, reads, writes):
        sr = tuple(reads) if _isap(sc) else ()
        pr.emit(eng, lambda e, o=out, x=in0, s=sc, y=in1, p0=op0, p1=op1:
                e.scalar_tensor_tensor(o, x, s, y, p0, p1), reads, writes, sreads=sr)

    def act(out, in_, func, reads, writes, bias=None, scale=None, accum=None):
        if isinstance(bias, float) or isinstance(scale, float):
            reads = tuple(reads) + ("constap",)
        sr = tuple(reads) if (_isap(bias) or _isap(scale)) else ()

        def f(e, o=out, i=in_, fu=func, b=bias, s=scale, a=accum):
            kw = {}
            if b is not None:
                kw["bias"] = b
            if s is not None:
                kw["scale"] = s
            if a is not None:
                kw["accum_out"] = a
            return e.activation(o, i, fu, **kw)
        pr.emit("scalar", f, reads, writes, sreads=sr)

    def tcopy(eng, out, in_, reads, writes):
        pr.emit(eng, lambda e, o=out, i=in_: e.tensor_copy(o, i), reads, writes)

    def mset(eng, ap, v, writes):
        pr.emit(eng, lambda e, a=ap, c=v: e.memset(a, c), (), writes)

    def redmax_free(eng, out, in_, reads, writes, absval=True):
        pr.emit("vector", lambda e, o=out, i=in_, a=absval:
                e.reduce_max(o, i, axis=AX.X, apply_absolute_value=a), reads, writes)

    def tree_max(tile, key, tmp2, tmp2key, tc):
        for p in (64, 32):
            tcopy("vector", tmp2[0:p, 0:tc], tile[p:2 * p, 0:tc], [key], [tmp2key])
            tt("vector", tile[0:p, 0:tc], tile[0:p, 0:tc], tmp2[0:p, 0:tc], OP.max,
               [key, tmp2key], [key])
        for j in range(tc // 128):
            trans(PC[0:128, 0:32], tile[0:32, j * 128:j * 128 + 128], [key], ["PC"], k=32)
            redmax_free("vector", pmc[:, j:j + 1], PC[0:128, 0:32], ["PC", "pmc"],
                        ["pmc"], absval=False)
        for j in range(tc // 128):
            trans(PC[0:1, 0:128], pmc[:, j:j + 1], ["pmc"], ["PC"], k=128)
            tcopy("vector", tile[0:1, j * 128:j * 128 + 128], PC[0:1, 0:128], ["PC"], [key])

    def scan_op(eng, out, d0, d1, reads, writes):
        pr.emit(eng, lambda e, o=out, a=d0, b=d1:
                e.tensor_tensor_scan(o, a, b, 0.0, OP.mult, OP.add), reads, writes)

    def recip(out, in_, reads, writes):
        pr.emit("vector", lambda e, o=out, i=in_: e.reciprocal(o, i), reads, writes,
                sreads=tuple(reads))

    def coll(name_key, ins_t, outs_t, reads, writes):
        pr.emit("gpsimd", lambda e, i=ins_t, o=outs_t: e.collective_compute(
            "AllGather", OP.bypass, replica_groups=[list(range(NCORES))],
            ins=[i.ap().opt()], outs=[o.ap().opt()]), reads, writes)

    VG = ["vector", "gpsimd"]

    def ladder_round(pref, tile, y, y0, q, cols, bits, eng, reads):
        ts("vector", y, y, 0.5, None, OP.add, None, reads, [pref + "y"])
        tcopy("vector", y0, y, [pref + "y"], [pref + "y0"])
        for c in bits:
            ts("vector", q, y, float(c), float(-c), OP.is_ge, OP.mult, [pref + "y"], [pref + "q"])
            tt("vector", y, y, q, OP.add, [pref + "y", pref + "q"], [pref + "y"])
        tt("vector", q, y0, y, OP.subtract, [pref + "y", pref + "y0"], [pref + "q"])

    lnsq = sb("lnsq", [128, 512])
    lnra_t = sb("lnra_t", [1, 512])
    lnrb_t = sb("lnrb_t", [1, 512])
    lnrc_t = sb("lnrc_t", [1, 512])

    def layer_norm(xs_tiles, ks, tc):
        lnrb = lnrb_t[0:1, 0:tc]
        lnrc = lnrc_t[0:1, 0:tc]
        lnra = lnra_t[0:1, 0:tc]
        for m in range(16):
            mm(PR1[0:1, 0:tc], ones_col[:, :], xs_tiles[m][:, 0:tc], m == 0, m == 15,
               ["onec", ks[m]], ["PR1"])
        for m in range(16):
            act(lnsq[:, 0:tc], xs_tiles[m][:, 0:tc], AF.Square, [ks[m]], ["lnsq"])
            mm(PR2[0:1, 0:tc], ones_col[:, :], lnsq[:, 0:tc], m == 0, m == 15,
               ["onec", "lnsq"], ["PR2"])
        ts("vector", lnra, PR1[0:1, 0:tc], 1.0 / 2048.0, None, OP.mult, None,
           ["PR1"], ["lnra"])
        tt("vector", lnrb, lnra, lnra, OP.mult,
           ["lnra"], ["lnrb"])
        stt("vector", lnrb, PR2[0:1, 0:tc], 1.0 / 2048.0, lnrb,
            OP.mult, OP.subtract, ["PR2", "lnrb"], ["lnrb"])
        act(lnrb, lnrb, AF.Sqrt, ["lnrb"], ["lnrb"], bias=1e-5)
        recip(lnrb, lnrb, ["lnrb"], ["lnrb"])
        tt("vector", lnrc, lnra, lnrb, OP.mult,
           ["lnra", "lnrb"], ["lnrc"])
        mm(P0[:, 0:tc], ones_row[:, :], lnrb, True, True, ["oner", "lnrb"], ["P0"])
        mm(P1[:, 0:tc], ones_row[:, :], lnrc, True, True, ["oner", "lnrc"], ["P1"])
        for m in range(16):
            tt("vector", xs_tiles[m][:, 0:tc], xs_tiles[m][:, 0:tc], P0[:, 0:tc], OP.mult,
               [ks[m], "P0"], [ks[m]])
            tt("vector", xs_tiles[m][:, 0:tc], xs_tiles[m][:, 0:tc], P1[:, 0:tc], OP.subtract,
               [ks[m], "P1"], [ks[m]])

    mset("vector", ones[:, :], 1.0, ["ones"])
    mset("vector", ones_col[:, :], 1.0, ["onec"])
    mset("vector", ones_row[:, :], 1.0, ["oner"])
    pmc = sb("pmc", [128, 8])
    consts = sb("consts", [128, 3])
    nc.const_aps.register(F32, 1e-5, consts[:, 0:1])
    nc.const_aps.register(F32, 1.0 / 4096.0, consts[:, 1:2])
    nc.const_aps.register(F32, -1.0, consts[:, 2:3])
    mset("vector", consts[:, 0:1], 1e-5, ["constap"])
    mset("vector", consts[:, 1:2], 1.0 / 4096.0, ["constap"])
    mset("vector", consts[:, 2:3], -1.0, ["constap"])
    dma("sync", eye_sb[:, :], eye_d[:, :], [], ["eye"])
    dma("sync", am_sb[:, :], am_d[:, :], [], ["am"])
    dma("sync", mask8_sb[:, :], mask8_d[:, :], [], ["mask8"])
    dma("sync", maskexp_sb[:, :], maskexp_d[:, :], [], ["maskexp"])

    ESL0 = ExitStack()
    Dc_pad = sb("Dc_pad", [64, T + 1], stack=ESL0)
    C_full = sb("C_full", [128, T], stack=ESL0)
    cdec = sb("cdec", [64, T], stack=ESL0)
    S_in = sb("S_in", [128, 4096], stack=ESL0)
    dtb_sb = sb("dtb_sb", [64, 1], stack=ESL0)
    acol = sb("acol", [64, 1], stack=ESL0)

    dma("sync", dtb_sb[:, :], dtb_d[:, :], [], ["dtb"])
    dma("sync", acol[:, :], alog_d[:, :], [], ["acol"])
    act(acol[:, :], acol[:, :], AF.Exp, ["acol"], ["acol"])
    ts("vector", acol[:, :], acol[:, :], -1.0, None, OP.mult, None, ["acol"], ["acol"])
    mset("vector", Dc_pad[:, 0:1], 0.0, ["dcp0"])

    ESAs = ExitStack()
    ST = sb("ST", [128, 4096], stack=ESAs)
    drep_sb = sb("drep_sb", [128, 32], stack=ESAs)
    dts_h = sb("dts_h", [64, HT], stack=ESAs)

    raw = sb("rawc", [128, HT + 3], stack=ESAs)
    yg_g = sb("yg_g", [128, HT], stack=ESAs)
    B_half = yg_g
    logda = raw[64:128, 0:HT]
    convw_sb = sb("convw_sb", [128, 34 * 4], stack=ESAs)
    convb_sb = sb("convb_sb", [128, 34], stack=ESAs)
    BT_c = [sb(f"BT{c}", [128, 128], stack=ESAs) for c in range(NCH)]
    G_c = [sb(f"G{c}", [128, 128], stack=ESAs) for c in range(NCH)]
    DcT_c = [sb(f"DcT{c}", [128, 64], stack=ESAs) for c in range(NCH)]
    dtT_c = [sb(f"dtT{c}", [128, 64], stack=ESAs) for c in range(NCH)]
    sz_gS = [sb(f"sz_g{i}", [128, HT], stack=ESAs) for i in range(2)]
    xc_gS = [sb(f"xc_g{i}", [128, HT], stack=ESAs) for i in range(2)]
    xT_sb = sb("xT_sb", [128, 128], stack=ESAs)
    cdecBS = [sb(f"cdecB{i}", [128, 128], stack=ESAs) for i in range(2)]
    E_sbS = [sb(f"E_sb{i}", [128, 128], stack=ESAs) for i in range(2)]
    M_sbS = [sb(f"M_sb{i}", [128, 128], stack=ESAs) for i in range(2)]
    Pm_sbS = [sb(f"Pm_sb{i}", [128, 128], stack=ESAs) for i in range(2)]
    Cd_sbS = [sb(f"Cd_sb{i}", [128, 128], stack=ESAs) for i in range(2)]
    Bd_sbS = [sb(f"Bd_sb{i}", [128, 128], stack=ESAs) for i in range(2)]
    bd_sbS = [sb(f"bd_sb{i}", [128, 1], stack=ESAs) for i in range(2)]
    tmp_sc = sb("tmp_sc", [128, 128], stack=ESAs)
    rowdcS = [sb(f"rowdc{i}", [1, 132], stack=ESAs) for i in range(2)]
    rowprev = sb("rowprev", [1, 64], stack=ESAs)

    ESAu = ExitStack()
    u_t = [sb(f"u{k}", [128, HT + 3], stack=ESAu) for k in range(16)]
    wsl = [sb(f"wsl{i}", [128, 2048], stack=ESAu) for i in range(2)]

    dma("sync", drep_sb[:, :], drep_d[:, :], [], ["drep"])
    dma("sync", convw_sb[:, :], convw[:, :], [], ["convw"])
    dma("sync", convb_sb[:, :], convb[:, :], [], ["convb"])
    mset("vector", ST[:, :], 0.0, ["ST"])
    mset("vector", rowprev[:, :], 0.0, ["rowprev"])

    def inproj_mm(mc, cols, pslices, wbuf):
        dma("sync", wsl[wbuf][:, :], w_in_p[mc], ["wsl%d" % wbuf], ["wsl%d" % wbuf])
        for ci, (off, ln) in enumerate(cols):
            pap, pkey = pslices[ci]
            for kt in range(16):
                mm(pap[:, 0:ln] if pap is not PA else pap[:, 129:129 + ln],
                   wsl[wbuf][:, kt * 128:kt * 128 + 128],
                   u_t[kt][:, off:off + ln], kt == 0, kt == 15,
                   ["wsl%d" % wbuf, f"u{kt}"], [pkey])

    for half in range(2):
        hoff = half * HT
        for kt in range(16):
            dma("sync", u_t[kt][:, :], uT[kt * 128:kt * 128 + 128, hoff:hoff + HT + 3],
                [f"u{kt}"], [f"u{kt}"])
        zcols = split_cols(3, HT + 3, 512)
        xcols = split_cols(0, HT + 3, 512)

        def psl(cols):
            out = []
            for i in range(len(cols)):
                out.append([(P0, "P0"), (P1, "P1"), (PA, "PAt")][i])
            return out

        inproj_mm(66, zcols, psl(zcols), 0)
        for ci, (off, ln) in enumerate(zcols):
            pap, pkey = psl(zcols)[ci]
            src = pap[0:64, 0:ln] if pap is not PA else pap[0:64, 129:129 + ln]
            o = off - 3
            ts("vector", raw[0:64, 0:ln], src, dtb_sb[:, 0:1], None, OP.add, None,
               [pkey, "dtb"], ["spv"])
            act(xc_gS[0][0:64, 0:ln], raw[0:64, 0:ln], AF.Abs, ["spv"], ["spa"])
            act(xc_gS[0][0:64, 0:ln], xc_gS[0][0:64, 0:ln], AF.Exp, ["spa"], ["spa"], scale=-1.0)
            ts("vector", xc_gS[0][0:64, 0:ln], xc_gS[0][0:64, 0:ln], 1.0, None, OP.add, None, ["spa"], ["spa"])
            act(xc_gS[0][0:64, 0:ln], xc_gS[0][0:64, 0:ln], AF.Ln, ["spa"], ["spa"])
            ts("vector", raw[0:64, 0:ln], raw[0:64, 0:ln], 0.0, None, OP.max, None, ["spv"], ["spv"])
            tt("vector", dts_h[:, o:o + ln], xc_gS[0][0:64, 0:ln], raw[0:64, 0:ln], OP.add,
               ["spa", "spv"], ["dts"])
        ts("vector", logda, dts_h[:, :], acol[:, :], None, OP.mult, None,
           ["dts", "acol"], ["logda"])
        scan_op("vector", Dc_pad[:, 1 + hoff:1 + hoff + HT], ones[64:128, 0:HT], logda,
                ["ones", "logda"], ["dcseg"])
        if half == 1:
            tcopy("vector", bd_sbS[0][0:64, 0:1], Dc_pad[:, hoff:hoff + 1], ["dcp"], ["dccar"])
            ts("vector", Dc_pad[:, 1 + hoff:1 + hoff + HT], Dc_pad[:, 1 + hoff:1 + hoff + HT],
               bd_sbS[0][0:64, 0:1], None, OP.add, None, ["dcseg", "dccar"], ["dcseg"])
        pr.wmap["dcp"] = pr.wmap["dcseg"]
        pr.barrier()

        for which, mc, cvi in (("B", 64, 32), ("C", 65, 33)):
            inproj_mm(mc, xcols, psl(xcols), 0)
            for ci, (off, ln) in enumerate(xcols):
                pap, pkey = psl(xcols)[ci]
                src = pap[:, 0:ln] if pap is not PA else pap[:, 129:129 + ln]
                tcopy("vector", raw[:, off:off + ln], src, [pkey], ["raw"])
            ts("vector", xc_gS[1][:, :], raw[:, 0:HT], convw_sb[:, cvi * 4:cvi * 4 + 1],
               convb_sb[:, cvi:cvi + 1], OP.mult, OP.add, ["raw", "convw", "convb"], ["cacc1"])
            for d in range(1, 4):
                stt("vector", xc_gS[1][:, :], raw[:, d:d + HT],
                    convw_sb[:, cvi * 4 + d:cvi * 4 + d + 1],
                    xc_gS[1][:, :], OP.mult, OP.add, ["raw", "convw", "cacc1"], ["cacc1"])
            dstt = B_half[:, :] if which == "B" else C_full[:, hoff:hoff + HT]
            dkey = "Bh" if which == "B" else "Cf"
            act(dstt, xc_gS[1][:, :], AF.Silu, ["cacc1"], [dkey])

        for c in range(NCH):
            cs = hoff + c * 128
            trans(PC[:, :], B_half[:, c * 128:c * 128 + 128], ["Bh"], ["PC"])
            tcopy("vector", BT_c[c][:, :], PC[:, :], ["PC"], [f"BT{c}"])
            mm(PC[:, :], B_half[:, c * 128:c * 128 + 128], C_full[:, cs:cs + 128], True, True,
               ["Bh", "Cf"], ["PC"])
            tcopy("vector", G_c[c][:, :], PC[:, :], ["PC"], [f"G{c}"])
            trans(PC[0:128, 0:64], Dc_pad[:, 1 + cs:1 + cs + 128], ["dcp"], ["PC"], k=64)
            tcopy("vector", DcT_c[c][:, :], PC[0:128, 0:64], ["PC"], [f"DcT{c}"])
            trans(PC[0:128, 0:64], dts_h[:, c * 128:c * 128 + 128], ["dts"], ["PC"], k=64)
            tcopy("vector", dtT_c[c][:, :], PC[0:128, 0:64], ["PC"], [f"dtT{c}"])

        pr.barrier()
        def emit_inproj(g):
            p = g % 2
            szg, xcg = sz_gS[p], xc_gS[p]
            kz, kx = "szg%d" % p, "cacc%d" % p
            inproj_mm(g, zcols, psl(zcols), g % 2)
            for ci, (off, ln) in enumerate(zcols):
                pap, pkey = psl(zcols)[ci]
                src = pap[:, 0:ln] if pap is not PA else pap[:, 129:129 + ln]
                act(szg[:, off - 3:off - 3 + ln], src, AF.Silu, [pkey, kz], [kz])
            dma("scalar", sz_s[g, :, hoff:hoff + HT], szg[:, :], [kz], ["szs%d" % g])
            if debug:
                dma("scalar", dbg["d_sz"][g, :, hoff:hoff + HT], szg[:, :], [kz], ["dbgsz"])
            inproj_mm(32 + g, xcols, psl(xcols), (g + 1) % 2)
            for ci, (off, ln) in enumerate(xcols):
                pap, pkey = psl(xcols)[ci]
                src = pap[:, 0:ln] if pap is not PA else pap[:, 129:129 + ln]
                tcopy("vector", raw[:, off:off + ln], src, [pkey], ["raw"])
            ts("vector", xcg[:, :], raw[:, 0:HT], convw_sb[:, g * 4:g * 4 + 1],
               convb_sb[:, g:g + 1], OP.mult, OP.add, ["raw", "convw", "convb", kx], [kx])
            for d in range(1, 4):
                stt("vector", xcg[:, :], raw[:, d:d + HT], convw_sb[:, g * 4 + d:g * 4 + d + 1],
                    xcg[:, :], OP.mult, OP.add, ["raw", "convw", kx], [kx])
            act(xcg[:, :], xcg[:, :], AF.Silu, [kx], [kx])

        def emit_scan(g):
            p = g % 2
            szg, xcg = sz_gS[p], xc_gS[p]
            kz, kx = "szg%d" % p, "cacc%d" % p
            for c in range(NCH):
                cs = hoff + c * 128
                trans(PC[:, :], xcg[:, c * 128:c * 128 + 128], [kx], ["PC"])
                tcopy("vector", xT_sb[:, :], PC[:, :], ["PC"], ["xT"])
                for j in range(2):
                    hh = 2 * g + j
                    cdecB, E_sb, M_sb = cdecBS[j], E_sbS[j], M_sbS[j]
                    Pm_sb, Cd_sb, Bd_sb, bd_sb = Pm_sbS[j], Cd_sbS[j], Bd_sbS[j], bd_sbS[j]
                    rowdc = rowdcS[j]
                    PAj = PA if j == 0 else P3
                    kPA = "PAw%d" % j
                    kcd, kE, kM, kPm, kCd, kBd, kbd, krd = ("cdB%d" % j, "E%d" % j,
                        "M%d" % j, "Pm%d" % j, "Cd%d" % j, "Bd%d" % j, "bd%d" % j,
                        "rowdc%d" % j)
                    PCj = PC[:, 0:64] if j == 0 else PC[:, 64:128]
                    kPC = "PCa" if j == 0 else "PCb"
                    trans(PR1[0:1, 0:128], DcT_c[c][:, hh:hh + 1], [f"DcT{c}"], ["PR1"])
                    tcopy("vector", rowdc[0:1, 1:129], PR1[0:1, 0:128], ["PR1"], [krd])
                    tcopy("vector", rowdc[0:1, 0:1], rowprev[0:1, hh:hh + 1],
                          ["rowprev"], [krd])
                    tcopy("vector", rowprev[0:1, hh:hh + 1], rowdc[0:1, 128:129],
                          [krd], ["rowprev"])
                    mm(PAj[:, 0:129], ones_row[:, :], rowdc[0:1, 0:129], True, True,
                       ["oner", krd], [kPA])
                    ts("vector", cdecB[:, :], PAj[:, 1:129], PAj[:, 0:1], None, OP.subtract, None,
                       [kPA], [kcd])
                    act(cdecB[:, :], cdecB[:, :], AF.Exp, [kcd], [kcd])
                    ts("vector", E_sb[:, :], PAj[:, 1:129], DcT_c[c][:, hh:hh + 1], None,
                       OP.subtract, None, [kPA, f"DcT{c}"], [kE])
                    tt("gpsimd", E_sb[:, :], E_sb[:, :], am_sb[:, :], OP.add, [kE, "am"], [kE])
                    act(M_sb[:, :], E_sb[:, :], AF.Exp, [kE], [kM])
                    ts("gpsimd", Pm_sb[:, :], G_c[c][:, :], dtT_c[c][:, hh:hh + 1], None,
                       OP.mult, None, [f"G{c}", f"dtT{c}"], [kPm])
                    tt("gpsimd", M_sb[:, :], M_sb[:, :], Pm_sb[:, :], OP.mult, [kM, kPm], [kM])
                    tt("gpsimd", Cd_sb[:, :], C_full[:, cs:cs + 128], cdecB[:, :], OP.mult,
                       ["Cf", kcd], [kCd])
                    mm(P2[64 * j:64 * j + 64, 0:128], xT_sb[:, 64 * j:64 * j + 64], M_sb[:, :],
                       True, False, ["xT", kM], ["P2w%d" % j])
                    mm(P2[64 * j:64 * j + 64, 0:128], ST[:, hh * 64:hh * 64 + 64], Cd_sb[:, :],
                       False, True, ["ST", kCd], ["P2w%d" % j])
                    ts("vector", bd_sb[:, :], PAj[:, 128:129], DcT_c[c][:, hh:hh + 1], None,
                       OP.subtract, None, [kPA, f"DcT{c}"], [kbd])
                    act(bd_sb[:, :], bd_sb[:, :], AF.Exp, [kbd], [kbd])
                    ts("vector", bd_sb[:, :], bd_sb[:, :], dtT_c[c][:, hh:hh + 1], None,
                       OP.mult, None, [kbd, f"dtT{c}"], [kbd])
                    ts("gpsimd", Bd_sb[:, :], BT_c[c][:, :], bd_sb[:, 0:1], None, OP.mult, None,
                       [f"BT{c}", kbd], [kBd])
                    mm(PCj, Bd_sb[:, :], xT_sb[:, 64 * j:64 * j + 64], True, True,
                       [kBd, "xT", "PC"], [kPC, "PC"])
                    stt("vector", ST[:, hh * 64:hh * 64 + 64], ST[:, hh * 64:hh * 64 + 64],
                        cdecB[:, 127:128], PCj, OP.mult, OP.add,
                        ["ST", kcd, kPC, "PC"], ["ST"])
                stt("vector", tmp_sc[:, :], xcg[:, c * 128:c * 128 + 128],
                    drep_sb[:, g:g + 1], P2[:, 0:128], OP.mult, OP.add,
                    [kx, "drep", "P2w0", "P2w1"], ["tmpsc"])
                tt("gpsimd", yg_g[:, c * 128:c * 128 + 128], tmp_sc[:, :],
                   szg[:, c * 128:c * 128 + 128], OP.mult, ["tmpsc", kz], ["ygg"])
            dma("scalar", yg_s[g, :, hoff:hoff + HT], yg_g[:, :], ["ygg"], ["ygs%d" % g])
            if debug:
                dma("scalar", dbg["d_yg"][g, :, hoff:hoff + HT], yg_g[:, :], ["ygg"], ["dbgyg"])

        for g in range(33):
            if g < 32:
                emit_inproj(g)
            if g >= 1:
                emit_scan(g - 1)

    pr.barrier()
    ESAu.close()
    act(cdec[:, :], Dc_pad[:, 1:T + 1], AF.Exp, ["dcp"], ["cdec"])

    dma("scalar", cc_in.ap()[0:128, :], ST[:, :], ["ST"], ["ccin"])
    dma("scalar", cc_in.ap()[128:129, 0:64], Dc_pad[:, T:T + 1], ["dcp"], ["ccin"])
    coll("cc1", cc_in, cc_out, ["ccin"], ["ccout"])
    if debug:
        dma("scalar", dbg["d_st"][:, :], ST[:, :], ["ST"], ["dbgst"])
        dma("scalar", dbg["d_dc"][:, :], Dc_pad[:, :], ["dcp"], ["dbgdc"])
        dma("scalar", dbg["d_dts"][:, 0:HT], dts_h[:, :], ["dts"], ["dbgdts"])
    ESA2 = ExitStack()
    gat_st = sb("gat_st", [128, 4096], stack=ESA2)
    tmp_st = sb("tmp_st", [128, 4096], stack=ESA2)
    dclj = sb("dclj", [1, 64], stack=ESA2)
    grow = sb("grow", [1, 64], stack=ESA2)
    gcol_sb = sb("gcol_sb", [128, 64], stack=ESA2)
    mset("vector", S_in[:, :], 0.0, ["Sin"])
    for j in range(NCORES):
        dma("sync", gat_st[:, :], cc_out.ap()[j * 129:j * 129 + 128, :], ["ccout", "gatst"], ["gatst"])
        dma("sync", dclj[:, :], cc_out.ap()[j * 129 + 128:j * 129 + 129, 0:64], ["ccout", "dclj"], ["dclj"])
        act(grow[:, :], dclj[:, :], AF.Exp, ["dclj"], ["grow"])
        ts("vector", grow[:, :], grow[:, :], 1.0, None, OP.subtract, None, ["grow"], ["grow"])
        ts("vector", grow[:, :], grow[:, :], mask8_sb[0:1, j:j + 1], 1.0, OP.mult, OP.add,
           ["grow", "mask8"], ["grow"])
        mm(PC[:, 0:64], ones_row[:, :], grow[:, :], True, True, ["oner", "grow"], ["PC"])
        tcopy("vector", gcol_sb[:, :], PC[:, 0:64], ["PC"], ["gcol"])
        for ch in range(8):
            ts("gpsimd", tmp_st[:, ch * 512:(ch + 1) * 512], gat_st[:, ch * 512:(ch + 1) * 512],
               mask8_sb[:, j:j + 1], None, OP.mult, None, ["gatst", "mask8"], ["tmpst"])
        for h in range(64):
            stt("vector", S_in[:, h * 64:h * 64 + 64], S_in[:, h * 64:h * 64 + 64],
                gcol_sb[:, h:h + 1], tmp_st[:, h * 64:h * 64 + 64], OP.mult, OP.add,
                ["Sin", "gcol", "tmpst"], ["Sin"])
    if debug:
        dma("scalar", dbg["d_sin"][:, :], S_in[:, :], ["Sin"], ["dbgsin"])
    pr.barrier()
    ESA2.close()
    pr.barrier()
    ESAs.close()

    ES3 = ExitStack()
    ygt = [sb(f"ygt{g}", [128, 512], stack=ES3) for g in range(32)]
    szt = sb("szt", [128, 512], stack=ES3)
    CdA = sb("CdA", [128, 512], stack=ES3)
    cdB2 = sb("cdB2", [128, 512], stack=ES3)
    ohp_sb = sb("ohp_sb", [64, 4096], stack=ES3)
    nw_sb = sb("nw_sb", [128, 32], stack=ES3)
    wosl = [sb("wosl0", [128, 2048], stack=ES3)] * 2
    xs_t = [sb(f"xs3_{m}", [128, 512], stack=ES3) for m in range(16)]
    sq3 = sb("sq3", [128, 512], stack=ES3)
    rowa = sb("rowa", [1, 512], stack=ES3)
    amx3 = sb("amx3", [128, 512], stack=ES3)
    invB = sb("invB3", [128, 512], stack=ES3)
    absx = sb("absx3", [128, 512], stack=ES3)
    lyy = sb("lyy3", [128, 512], stack=ES3)
    ly0 = sb("ly03", [128, 512], stack=ES3)
    lq = sb("lq3", [128, 512], stack=ES3)
    sgn = sb("sgn3", [128, 512], stack=ES3)
    qxb = sb("qxb3", [128, 512], BF16, stack=ES3)
    dma("sync", nw_sb[:, :], nw_d[:, :], [], ["nw"])
    dma("sync", ohp_sb[:, :], ohp_d[:, :], [], ["ohp"])

    TCH = split_cols(0, T, 512)
    for ti, (t0, tc) in enumerate(TCH):
        for g in range(32):
            dma("sync", ygt[g][:, 0:tc], yg_s[g, :, t0:t0 + tc], ["ygs%d" % g, f"3y{g}"], [f"3y{g}"])
            dma("sync", szt[:, 0:tc], sz_s[g, :, t0:t0 + tc], ["szs%d" % g, "szt"], ["szt"])
            mm(P3[:, 0:tc], ohp_sb[:, g * 128:g * 128 + 128], cdec[:, t0:t0 + tc],
               True, True, ["ohp", "cdec"], ["P3"])
            tcopy("vector", cdB2[:, 0:tc], P3[:, 0:tc], ["P3"], ["cdB2"])
            for j in range(2):
                hh = 2 * g + j
                mm(P2[64 * j:64 * j + 64, 0:tc], S_in[:, hh * 64:hh * 64 + 64],
                   C_full[:, t0:t0 + tc], True, True, ["Sin", "Cf"], ["P2"])
            if debug:
                dma("scalar", dbg["d_cdb"][g, :, t0:t0 + tc], cdB2[:, 0:tc],
                    ["cdB2"], ["dbgcdb"])
                tcopy("vector", CdA[:, 0:tc], P2[:, 0:tc], ["P2"], ["CdA"])
                dma("scalar", dbg["d_p2r"][g, :, t0:t0 + tc], CdA[:, 0:tc],
                    ["CdA"], ["dbgp2r"])
            tt("vector", CdA[:, 0:tc], P2[:, 0:tc], cdB2[:, 0:tc], OP.mult,
               ["P2", "cdB2"], ["CdA"])
            tt("gpsimd", CdA[:, 0:tc], CdA[:, 0:tc], szt[:, 0:tc], OP.mult,
               ["CdA", "szt"], ["CdA"])
            if debug:
                dma("scalar", dbg["d_cda"][g, :, t0:t0 + tc], CdA[:, 0:tc],
                    ["CdA"], ["dbgcda"])
            tt("gpsimd", ygt[g][:, 0:tc], ygt[g][:, 0:tc], CdA[:, 0:tc], OP.add,
               [f"3y{g}", "CdA"], [f"3y{g}"])
            if debug:
                dma("scalar", dbg["d_ygt"][g, :, t0:t0 + tc], ygt[g][:, 0:tc],
                    [f"3y{g}"], ["dbgygt"])
            act(sq3[:, 0:tc], ygt[g][:, 0:tc], AF.Square, [f"3y{g}"], ["sq3"])
            mm(PR1[0:1, 0:tc], ones_col[:, :], sq3[:, 0:tc], g == 0, g == 31,
               ["onec", "sq3"], ["PR1"])
        act(rowa[0:1, 0:tc], PR1[0:1, 0:tc], AF.Sqrt, ["PR1"], ["rmsr"],
            bias=1e-5, scale=1.0 / 4096.0)
        recip(rowa[0:1, 0:tc], rowa[0:1, 0:tc], ["rmsr"], ["rmsr"])
        mm(P3[:, 0:tc], ones_row[:, :], rowa[0:1, 0:tc], True, True, ["oner", "rmsr"], ["P3"])
        for g in range(32):
            ts("gpsimd", ygt[g][:, 0:tc], ygt[g][:, 0:tc], nw_sb[:, g:g + 1], None, OP.mult,
               None, [f"3y{g}", "nw"], [f"3y{g}"])
            tt("vector", ygt[g][:, 0:tc], ygt[g][:, 0:tc], P3[:, 0:tc], OP.mult,
               [f"3y{g}", "P3"], [f"3y{g}"])
        for m in range(16):
            pq_ = PQ[m % 4]
            pk = "P%d" % (m % 4)
            for h in range(2):
                dma("sync", wosl[0][:, :], w_out_p[m, :, h * 2048:h * 2048 + 2048],
                    ["wo0"], ["wo0"])
                for k in range(16):
                    mm(pq_[:, 0:tc], wosl[0][:, k * 128:k * 128 + 128],
                       ygt[h * 16 + k][:, 0:tc], h == 0 and k == 0, h == 1 and k == 15,
                       ["wo0", f"3y{h * 16 + k}"], [pk])
            dma("sync", xs_t[m][:, 0:tc], uT[m * 128:m * 128 + 128, 3 + t0:3 + t0 + tc],
                [f"3xs{m}"], [f"3xs{m}"])
            tt("vector", xs_t[m][:, 0:tc], xs_t[m][:, 0:tc], pq_[:, 0:tc], OP.add,
               [f"3xs{m}", pk], [f"3xs{m}"])
        layer_norm(xs_t, [f"3xs{m}" for m in range(16)], tc)
        for m in range(16):
            dma("scalar", x1_s[m, :, t0:t0 + tc], xs_t[m][:, 0:tc], [f"3xs{m}"], ["x1s%d" % m])
        for m in range(16):
            act(absx[:, 0:tc], xs_t[m][:, 0:tc], AF.Abs, [f"3xs{m}"], ["la"])
            if m == 0:
                tcopy("vector", amx3[:, 0:tc], absx[:, 0:tc], ["la"], ["amx3"])
            else:
                tt("vector", amx3[:, 0:tc], amx3[:, 0:tc], absx[:, 0:tc], OP.max,
                   ["amx3", "la"], ["amx3"])
        tree_max(amx3, "amx3", sq3, "sq3", tc)
        ts("vector", amx3[0:1, 0:tc], amx3[0:1, 0:tc], 1.0 / 127.0, 1e-8, OP.mult, OP.max,
           ["amx3"], ["amx3"])
        mm(P2[:, 0:tc], ones_row[:, :], amx3[0:1, 0:tc], True, True, ["oner", "amx3"], ["P2"])
        tcopy("vector", sxT[:, t0:t0 + tc], P2[:, 0:tc], ["P2"], ["sxs"])
        recip(invB[:, 0:tc], sxT[:, t0:t0 + tc], ["sxs"], ["invB"])
        for m in range(16):
            eng = VG[m % 2]
            act(absx[:, 0:tc], xs_t[m][:, 0:tc], AF.Abs, [f"3xs{m}"], ["la"])
            act(sgn[:, 0:tc], xs_t[m][:, 0:tc], AF.Sign, [f"3xs{m}"], ["ls"])
            tt(eng, lyy[:, 0:tc], absx[:, 0:tc], invB[:, 0:tc], OP.mult, ["la", "invB"], ["ly"])
            ladder_round("l", lyy[:, 0:tc], lyy[:, 0:tc], ly0[:, 0:tc], lq[:, 0:tc], tc,
                         [64, 32, 16, 8, 4, 2, 1], "vector", ["ly"])
            tt(eng, lq[:, 0:tc], lq[:, 0:tc], sgn[:, 0:tc], OP.mult, ["lq", "ls"], ["lq"])
            tcopy(eng, qxb[:, 0:tc], lq[:, 0:tc], ["lq"], ["qxb"])
            dma("scalar", qx_s[m, :, t0:t0 + tc], qxb[:, 0:tc], ["qxb"], ["qxs%d" % m])
            if debug:
                dma("scalar", dbg["d_qx"][m, :, t0:t0 + tc], lq[:, 0:tc], ["lq"], ["dbgqx"])
        if debug:
            for m in range(16):
                dma("scalar", dbg["d_x1"][m, :, t0:t0 + tc], xs_t[m][:, 0:tc],
                    [f"3xs{m}"], ["dbg1"])
    pr.barrier()
    ES3.close()
    pr.barrier()
    ESL0.close()

    ESB15 = ExitStack()
    HW = 2816
    wdsl = [sb(f"wdsl{h}", [128, HW], stack=ESB15) for h in range(2)]
    wcol = sb("wcol", [128, 1], stack=ESB15)
    wcol2 = sb("wcol2", [128, 1], stack=ESB15)
    winv = sb("winv", [128, 1], stack=ESB15)
    wyS = [sb(f"wyt{i}", [128, HW], stack=ESB15) for i in range(2)]
    wy0S = [sb(f"wy0t{i}", [128, HW], stack=ESB15) for i in range(2)]
    wqS = [sb(f"wqt{i}", [128, HW], stack=ESB15) for i in range(2)]
    wsgnS = [sb(f"wsgnt{i}", [128, HW], stack=ESB15) for i in range(2)]
    q2sl = sb("q2sl", [128, 5632], BF16, stack=ESB15)
    for m in range(16):
        eng = VG[m % 2]
        for h in range(2):
            dma("sync", wdsl[h][:, :], w_down_p[m, :, h * HW:h * HW + HW],
                ["wdsl%d" % h], ["wdsl%d" % h])
        redmax_free(eng, wcol[:, :], wdsl[0][:, :], ["wdsl0", "wcol"], ["wcol"], absval=True)
        redmax_free(eng, wcol2[:, :], wdsl[1][:, :], ["wdsl1", "wcol2"], ["wcol2"], absval=True)
        tt("vector", wcol[:, :], wcol[:, :], wcol2[:, :], OP.max, ["wcol", "wcol2"], ["wcol"])
        ts("vector", wcol[:, :], wcol[:, :], 1.0 / 127.0, 1e-8, OP.mult, OP.max, ["wcol"], ["wcol"])
        tcopy("vector", s2all[:, m:m + 1], wcol[:, :], ["wcol"], ["s2all"])
        recip(winv[:, :], wcol[:, :], ["wcol", "winv"], ["winv"])
        for h in range(2):
            pp = (2 * m + h) % 2
            wy, wy0, wq, wsgn = wyS[pp], wy0S[pp], wqS[pp], wsgnS[pp]
            pf = "w%d" % pp
            act(wy[:, :], wdsl[h][:, :], AF.Abs, ["wdsl%d" % h, pf + "y"], [pf + "y"])
            act(wsgn[:, :], wdsl[h][:, :], AF.Sign, ["wdsl%d" % h, pf + "s"], [pf + "s"])
            ts("vector", wy[:, :], wy[:, :], winv[:, :], None, OP.mult, None,
               [pf + "y", "winv"], [pf + "y"])
            ladder_round(pf, wy[:, :], wy[:, :], wy0[:, :], wq[:, :], HW,
                         [64, 32, 16, 8, 4, 2, 1], "vector", [pf + "y"])
            tt(eng, wq[:, :], wq[:, :], wsgn[:, :], OP.mult, [pf + "q", pf + "s"], [pf + "q"])
            for kt in range(22):
                gkt = h * 22 + kt
                trans(PC[:, :], wq[:, kt * 128:kt * 128 + 128], [pf + "q"], ["PC"])
                tcopy("vector", q2sl[:, gkt * 128:gkt * 128 + 128], PC[:, :], ["PC"], ["q2sl"])
        dma("scalar", q2L0_s[m], q2sl[:, :], ["q2sl"], ["q2L0"])
    pr.barrier()
    ESB15.close()

    ESB = ExitStack()
    qx_t = [sb(f"qx{k}", [128, T], BF16, stack=ESB) for k in range(16)]
    for k in range(16):
        dma("sync", qx_t[k][:, :], qx_s[k], ["qxs%d" % k, f"qx{k}"], [f"qx{k}"])
    wgsl = sb("wgsl", [128, 2048], stack=ESB)
    wcol = sb("wcolB", [128, 1], stack=ESB)
    winv = sb("winvB", [128, 1], stack=ESB)
    gyS = [sb(f"gyt{i}", [128, 2048], stack=ESB) for i in range(2)]
    gy0S = [sb(f"gy0t{i}", [128, 2048], stack=ESB) for i in range(2)]
    gqS = [sb(f"gqt{i}", [128, 2048], stack=ESB) for i in range(2)]
    gsgnS = [sb(f"gsgnt{i}", [128, 2048], stack=ESB) for i in range(2)]
    s1g = sb("s1g", [128, 1], stack=ESB)
    s1u = sb("s1u", [128, 1], stack=ESB)
    q1g = sb("q1g", [128, 2048], BF16, stack=ESB)
    q1u = sb("q1u", [128, 2048], BF16, stack=ESB)
    gt_sb = sb("gt_sb", [128, 512], stack=ESB)
    ut_sb = sb("ut_sb", [128, 512], stack=ESB)

    def quant_gup(mc, scol, skey, qdst, qkey):
        pp = mc % 2
        gy, gy0, gq, gsgn = gyS[pp], gy0S[pp], gqS[pp], gsgnS[pp]
        pf = "g%d" % pp
        dma("sync", wgsl[:, :], w_gup_p[mc], ["wgsl"], ["wgsl"])
        redmax_free("vector", wcol[:, :], wgsl[:, :], ["wgsl", "wcol"], ["wcol"], absval=True)
        ts("vector", scol, wcol[:, :], 1.0 / 127.0, 1e-8, OP.mult, OP.max, ["wcol"], [skey])
        recip(winv[:, :], scol, [skey, "winv"], ["winv"])
        act(gy[:, :], wgsl[:, :], AF.Abs, ["wgsl", pf + "y"], [pf + "y"])
        act(gsgn[:, :], wgsl[:, :], AF.Sign, ["wgsl", pf + "s"], [pf + "s"])
        ts("vector", gy[:, :], gy[:, :], winv[:, :], None, OP.mult, None,
           [pf + "y", "winv"], [pf + "y"])
        ladder_round(pf, gy[:, :], gy[:, :], gy0[:, :], gq[:, :], 2048,
                     [64, 32, 16, 8, 4, 2, 1], "vector", [pf + "y"])
        tt("vector", gq[:, :], gq[:, :], gsgn[:, :], OP.mult, [pf + "q", pf + "s"], [pf + "q"])
        for kt in range(16):
            trans(PC[:, :], gq[:, kt * 128:kt * 128 + 128], [pf + "q"], ["PC"])
            tcopy("vector", qdst[:, kt * 128:kt * 128 + 128], PC[:, :], ["PC"], [qkey])

    for p in range(44):
        quant_gup(p, s1g[:, :], "s1gk", q1g, "q1gk")
        quant_gup(44 + p, s1u[:, :], "s1uk", q1u, "q1uk")
        if debug and p == 0:
            dma("scalar", dbg["d_q1"][0], q1g[:, :], ["q1gk"], ["dbgq1"])
            dma("scalar", dbg["d_q1"][1], q1u[:, :], ["q1uk"], ["dbgq1"])
            dma("scalar", dbg["d_s1"][0:1, :], s1g[:, :], ["s1gk"], ["dbgs1"])
            dma("scalar", dbg["d_s1"][1:2, :], s1u[:, :], ["s1uk"], ["dbgs1"])
        for ti, (t0, tc) in enumerate(TCH):
            for kt in range(16):
                mm(P0[:, 0:tc], q1g[:, kt * 128:kt * 128 + 128], qx_t[kt][:, t0:t0 + tc],
                   kt == 0, kt == 15, ["q1gk", f"qx{kt}"], ["P0"])
            for kt in range(16):
                mm(P1[:, 0:tc], q1u[:, kt * 128:kt * 128 + 128], qx_t[kt][:, t0:t0 + tc],
                   kt == 0, kt == 15, ["q1uk", f"qx{kt}"], ["P1"])
            stt("vector", gt_sb[:, 0:tc], P0[:, 0:tc], s1g[:, :], sxT[:, t0:t0 + tc],
                OP.mult, OP.mult, ["P0", "s1gk", "sxs"], ["gt"])
            stt("vector", ut_sb[:, 0:tc], P1[:, 0:tc], s1u[:, :], sxT[:, t0:t0 + tc],
                OP.mult, OP.mult, ["P1", "s1uk", "sxs"], ["ut"])
            if debug and p == 0 and ti == 0:
                dma("scalar", dbg["d_gu"][0, :, 0:tc], gt_sb[:, 0:tc], ["gt"], ["dbggu"])
                dma("scalar", dbg["d_gu"][1, :, 0:tc], ut_sb[:, 0:tc], ["ut"], ["dbggu"])
            act(gt_sb[:, 0:tc], gt_sb[:, 0:tc], AF.Silu, ["gt"], ["gt"])
            tt("gpsimd", gt_sb[:, 0:tc], gt_sb[:, 0:tc], ut_sb[:, 0:tc], OP.mult,
               ["gt", "ut"], ["gt"])
            dma("scalar", h0_s[p, :, t0:t0 + tc], gt_sb[:, 0:tc], ["gt"], ["h0s%d" % p])
            if debug:
                dma("scalar", dbg["d_h"][p, :, t0:t0 + tc], gt_sb[:, 0:tc], ["gt"], ["dbgh"])
    pr.barrier()
    ESB.close()

    ESB2 = ExitStack()
    ht_t = [sb(f"ht{p}", [128, 256], stack=ESB2) for p in range(44)]
    hq_t = [sb(f"hq{p}", [128, 256], BF16, stack=ESB2) for p in range(44)]
    q2dl = [sb(f"q2dl{i}", [128, 5632], BF16, stack=ESB2) for i in range(2)]
    x2t = [sb(f"x2t{m}", [128, 256], stack=ESB2) for m in range(16)]
    b2hm = sb("b2hm", [128, 256], stack=ESB2)
    b2inv = sb("b2inv", [128, 256], stack=ESB2)
    b2aS = [sb(f"b2at{i}", [128, 256], stack=ESB2) for i in range(2)]
    b2sS = [sb(f"b2st{i}", [128, 256], stack=ESB2) for i in range(2)]
    b2yS = [sb(f"b2yt{i}", [128, 256], stack=ESB2) for i in range(2)]
    b2y0S = [sb(f"b2y0t{i}", [128, 256], stack=ESB2) for i in range(2)]
    b2qS = [sb(f"b2qt{i}", [128, 256], stack=ESB2) for i in range(2)]
    b2sq = sb("b2sq", [128, 256], stack=ESB2)
    shB = sb("shB", [128, 256], stack=ESB2)
    TC2 = split_cols(0, T, 256)
    for ti, (t0, tc) in enumerate(TC2):
        for p in range(44):
            dma("sync", ht_t[p][:, 0:tc], h0_s[p, :, t0:t0 + tc], ["h0s%d" % p, f"ht{p}"], [f"ht{p}"])
            act(b2aS[p % 2][:, 0:tc], ht_t[p][:, 0:tc], AF.Abs,
                [f"ht{p}", "b2%da" % (p % 2)], ["b2%da" % (p % 2)])
            if p == 0:
                tcopy("vector", b2hm[:, 0:tc], b2aS[0][:, 0:tc], ["b20a"], ["b2hm"])
            else:
                tt("vector", b2hm[:, 0:tc], b2hm[:, 0:tc], b2aS[p % 2][:, 0:tc], OP.max,
                   ["b2hm", "b2%da" % (p % 2)], ["b2hm"])
        tree_max(b2hm, "b2hm", b2yS[0], "b20y", tc)
        ts("vector", b2hm[0:1, 0:tc], b2hm[0:1, 0:tc], 1.0 / 127.0, 1e-8, OP.mult, OP.max,
           ["b2hm"], ["b2hm"])
        mm(P2[:, 0:tc], ones_row[:, :], b2hm[0:1, 0:tc], True, True, ["oner", "b2hm"], ["P2"])
        tcopy("vector", shB[:, 0:tc], P2[:, 0:tc], ["P2"], ["shB"])
        recip(b2inv[:, 0:tc], shB[:, 0:tc], ["shB"], ["b2invB"])
        for p in range(44):
            pp = p % 2
            pf = "b2%d" % pp
            ba, bs_, by, by0, bq_ = b2aS[pp], b2sS[pp], b2yS[pp], b2y0S[pp], b2qS[pp]
            act(ba[:, 0:tc], ht_t[p][:, 0:tc], AF.Abs, [f"ht{p}", pf + "a"], [pf + "a"])
            act(bs_[:, 0:tc], ht_t[p][:, 0:tc], AF.Sign, [f"ht{p}", pf + "s"], [pf + "s"])
            tt("gpsimd", by[:, 0:tc], ba[:, 0:tc], b2inv[:, 0:tc], OP.mult,
               [pf + "a", "b2invB", pf + "y"], [pf + "y"])
            ladder_round(pf, by[:, 0:tc], by[:, 0:tc], by0[:, 0:tc], bq_[:, 0:tc], tc,
                         [64, 32, 16, 8, 4, 2, 1], "vector", [pf + "y"])
            tt("gpsimd", bq_[:, 0:tc], bq_[:, 0:tc], bs_[:, 0:tc], OP.mult,
               [pf + "q", pf + "s"], [pf + "q"])
            tcopy("vector", hq_t[p][:, 0:tc], bq_[:, 0:tc], [pf + "q"], [f"hq{p}"])
        for m in range(16):
            dma("sync", q2dl[m % 2][:, :], q2L0_s[m], ["q2L0", "q2d%d" % (m % 2)], ["q2d%d" % (m % 2)])
            pq_ = PQ[m % 4]
            pk = "P%d" % (m % 4)
            for kt in range(44):
                mm(pq_[:, 0:tc], q2dl[m % 2][:, kt * 128:kt * 128 + 128], hq_t[kt][:, 0:tc],
                   kt == 0, kt == 43, ["q2d%d" % (m % 2), f"hq{kt}"], [pk])
            dma("sync", x2t[m][:, 0:tc], x1_s[m, :, t0:t0 + tc], ["x1s%d" % m, f"B2x{m}"], [f"B2x{m}"])
            stt("vector", b2sq[:, 0:tc], pq_[:, 0:tc], s2all[:, m:m + 1], shB[:, 0:tc],
                OP.mult, OP.mult, [pk, "s2all", "shB"], ["b2o"])
            tt("vector", x2t[m][:, 0:tc], x2t[m][:, 0:tc], b2sq[:, 0:tc], OP.add,
               [f"B2x{m}", "b2o"], [f"B2x{m}"])
        layer_norm(x2t, [f"B2x{m}" for m in range(16)], tc)
        for m in range(16):
            dma("scalar", x2_s[m, :, t0:t0 + tc], x2t[m][:, 0:tc], [f"B2x{m}"], ["x2s%d" % m])
            if debug:
                dma("scalar", dbg["d_x2"][m, :, t0:t0 + tc], x2t[m][:, 0:tc],
                    [f"B2x{m}"], ["dbg2"])
    pr.barrier()
    ESB2.close()

    ESC = ExitStack()
    x2c = [sb(f"x2c{k}", [128, 512], stack=ESC) for k in range(16)]
    wsl2 = [sb(f"wsl2_{i}", [128, 2048], stack=ESC) for i in range(2)]
    bq_sb = sb("bq_sb", [128, 16], stack=ESC)
    bk_sb = sb("bk_sb", [128, 16], stack=ESC)
    bv_sb = sb("bv_sb", [128, 16], stack=ESC)
    bo_sb = sb("bo_sb", [128, 16], stack=ESC)
    carry = sb("carry", [128, 16], stack=ESC)
    kmin = sb("kmin", [128, 512], stack=ESC)
    ek = sb("ek", [128, 512], stack=ESC)
    krel = sb("krel", [128, 512], stack=ESC)
    pkt = sb("pkt", [128, 512], stack=ESC)
    vt_sb = sb("vt_sb", [128, 512], stack=ESC)
    kvt = sb("kvt", [128, 512], stack=ESC)
    dma("sync", bq_sb[:, :], bq_d[:, :], [], ["bq"])
    dma("sync", bk_sb[:, :], bk_d[:, :], [], ["bk"])
    dma("sync", bv_sb[:, :], bv_d[:, :], [], ["bv"])
    dma("sync", bo_sb[:, :], bo_d[:, :], [], ["bo"])
    mset("vector", carry[:, :], 0.0, ["carry"])

    def phi_from_psum(pap, pk, bcol, bkey, out, okey):
        ts("vector", kmin[:, 0:tc], pap[:, 0:tc], bcol, 0.0, OP.add, OP.min,
           [pk, bkey], ["kmin"])
        act(ek[:, 0:tc], kmin[:, 0:tc], AF.Exp, ["kmin"], ["ek"])
        ts("vector", krel[:, 0:tc], pap[:, 0:tc], bcol, 0.0, OP.add, OP.max,
           [pk, bkey], ["krel"])
        tt("gpsimd", out[:, 0:tc], ek[:, 0:tc], krel[:, 0:tc], OP.add, ["ek", "krel"], [okey])

    for ti, (t0, tc) in enumerate(TCH):
        for k in range(16):
            dma("sync", x2c[k][:, 0:tc], x2_s[k, :, t0:t0 + tc], ["x2s%d" % k, f"x2c{k}"], [f"x2c{k}"])
        for m in range(16):
            wb = m % 2
            dma("sync", wsl2[wb][:, :], wk_p[m], ["wsl2_%d" % wb], ["wsl2_%d" % wb])
            for kt in range(16):
                mm(P0[:, 0:tc], wsl2[wb][:, kt * 128:kt * 128 + 128], x2c[kt][:, 0:tc],
                   kt == 0, kt == 15, ["wsl2_%d" % wb, f"x2c{kt}"], ["P0"])
            phi_from_psum(P0, "P0", bk_sb[:, m:m + 1], "bk", pkt, "pkt")
            dma("sync", wsl2[wb][:, :], wv_p[m], ["wsl2_%d" % wb], ["wsl2_%d" % wb])
            for kt in range(16):
                mm(P1[:, 0:tc], wsl2[wb][:, kt * 128:kt * 128 + 128], x2c[kt][:, 0:tc],
                   kt == 0, kt == 15, ["wsl2_%d" % wb, f"x2c{kt}"], ["P1"])
            ts("vector", vt_sb[:, 0:tc], P1[:, 0:tc], bv_sb[:, m:m + 1], None, OP.add, None,
               ["P1", "bv"], ["vt"])
            tt("gpsimd", pkt[:, 0:tc], pkt[:, 0:tc], vt_sb[:, 0:tc], OP.mult,
               ["pkt", "vt"], ["pkt"])
            scan_op("vector", kvt[:, 0:tc], ones[:, 0:tc], pkt[:, 0:tc], ["ones", "pkt"], ["kvt"])
            ts("gpsimd", kvt[:, 0:tc], kvt[:, 0:tc], carry[:, m:m + 1], None, OP.add, None,
               ["kvt", "carry"], ["kvt"])
            tcopy("vector", carry[:, m:m + 1], kvt[:, tc - 1:tc], ["kvt", "carry"], ["carry"])
            dma("scalar", kv_s[m, :, t0:t0 + tc], kvt[:, 0:tc], ["kvt"], ["kvs%d" % m])
            if debug:
                dma("scalar", dbg["d_kv"][m, :, t0:t0 + tc], kvt[:, 0:tc], ["kvt"], ["dbgkv"])
            dma("sync", wsl2[wb][:, :], wq_p[m], ["wsl2_%d" % wb], ["wsl2_%d" % wb])
            for kt in range(16):
                mm(P2[:, 0:tc], wsl2[wb][:, kt * 128:kt * 128 + 128], x2c[kt][:, 0:tc],
                   kt == 0, kt == 15, ["wsl2_%d" % wb, f"x2c{kt}"], ["P2"])
            phi_from_psum(P2, "P2", bq_sb[:, m:m + 1], "bq", pkt, "pkt")
            dma("scalar", pq_s[m, :, t0:t0 + tc], pkt[:, 0:tc], ["pkt"], ["pqs%d" % m])

    ct2_sb = sb("ct2_sb", [16, 128], stack=ESC)
    gat2_sb = sb("gat2_sb", [128, 128], stack=ESC)
    offc_sb = sb("offc_sb", [128, 16], stack=ESC)
    trans(PC[0:16, 0:128], carry[:, :], ["carry"], ["PC"])
    tcopy("vector", ct2_sb[:, :], PC[0:16, 0:128], ["PC"], ["ct2"])
    dma("scalar", cc2_in.ap()[:, :], ct2_sb[:, :], ["ct2"], ["cc2i"])
    coll("cc2", cc2_in, cc2_out, ["cc2i"], ["cc2o"])
    dma("sync", gat2_sb[:, :], cc2_out.ap()[:, :], ["cc2o"], ["gat2"])
    mm(PC[0:16, 0:128], maskexp_sb[:, :], gat2_sb[:, :], True, True, ["maskexp", "gat2"], ["PC"])
    tcopy("vector", ct2_sb[:, :], PC[0:16, 0:128], ["PC"], ["off1"])
    trans(PC[:, 0:16], ct2_sb[:, :], ["off1"], ["PC"], k=16)
    tcopy("vector", offc_sb[:, :], PC[:, 0:16], ["PC"], ["offc"])

    kvc = [sb(f"kvc{k}", [128, 512], stack=ESC) for k in range(16)]
    x3h_t = sb("x3h_t", [128, 512], BF16, stack=ESC)
    x3l_t = sb("x3l_t", [128, 512], BF16, stack=ESC)
    c2r = [sb(f"c2r{i}", [1, 512], stack=ESC) for i in range(3)]
    c2sq = sb("c2sq", [128, 512], stack=ESC)
    for ti, (t0, tc) in enumerate(TCH):
        for k in range(16):
            dma("sync", kvc[k][:, 0:tc], kv_s[k, :, t0:t0 + tc], ["kvs%d" % k, f"kvc{k}"], [f"kvc{k}"])
            ts("vector", kvc[k][:, 0:tc], kvc[k][:, 0:tc], offc_sb[:, k:k + 1], SCALE_L1,
               OP.add, OP.mult, [f"kvc{k}", "offc"], [f"kvc{k}"])
            dma("sync", x2c[k][:, 0:tc], pq_s[k, :, t0:t0 + tc], ["pqs%d" % k, f"x2c{k}"], [f"x2c{k}"])
            tt("gpsimd", kvc[k][:, 0:tc], kvc[k][:, 0:tc], x2c[k][:, 0:tc], OP.mult,
               [f"kvc{k}", f"x2c{k}"], [f"kvc{k}"])
        for m in range(16):
            wb = m % 2
            dma("sync", wsl2[wb][:, :], wo_p[m], ["wsl2_%d" % wb], ["wsl2_%d" % wb])
            pq_ = PQ[m % 4]
            pk = "P%d" % (m % 4)
            for kt in range(16):
                mm(pq_[:, 0:tc], wsl2[wb][:, kt * 128:kt * 128 + 128], kvc[kt][:, 0:tc],
                   kt == 0, kt == 15, ["wsl2_%d" % wb, f"kvc{kt}"], [pk])
            dma("sync", x2c[m][:, 0:tc], x2_s[m, :, t0:t0 + tc], ["x2s%d" % m, f"x2c{m}"], [f"x2c{m}"])
            ts("vector", c2sq[:, 0:tc], pq_[:, 0:tc], bo_sb[:, m:m + 1], None, OP.add, None,
               [pk, "bo"], ["c2o"])
            tt("vector", x2c[m][:, 0:tc], x2c[m][:, 0:tc], c2sq[:, 0:tc], OP.add,
               [f"x2c{m}", "c2o"], [f"x2c{m}"])
        layer_norm(x2c, [f"x2c{m}" for m in range(16)], tc)
        for m in range(16):
            dma("scalar", x3_s[m, :, t0:t0 + tc], x2c[m][:, 0:tc], [f"x2c{m}"], ["x3s%d" % m])
            tcopy("vector", x3h_t[:, 0:tc], x2c[m][:, 0:tc], [f"x2c{m}"], ["x3ht"])
            tt("vector", x3l_t[:, 0:tc], x2c[m][:, 0:tc], x3h_t[:, 0:tc], OP.subtract,
               [f"x2c{m}", "x3ht"], ["x3lt"])
            dma("scalar", x3h_s[m, :, t0:t0 + tc], x3h_t[:, 0:tc], ["x3ht"], ["x3hs%d" % m])
            dma("scalar", x3l_s[m, :, t0:t0 + tc], x3l_t[:, 0:tc], ["x3lt"], ["x3ls%d" % m])
            if debug:
                dma("scalar", dbg["d_x3"][m, :, t0:t0 + tc], x2c[m][:, 0:tc],
                    [f"x2c{m}"], ["dbg3"])
    pr.barrier()
    ESC.close()

    ESX = ExitStack()
    xmT = sb("xmT", [128, T], stack=ESX)
    ESDw = ExitStack()
    s1c = sb("s1c", [128, 1], stack=ESDw)
    th1c = sb("th1c", [128, 1], stack=ESDw)
    s2c_ = sb("s2c_", [128, 1], stack=ESDw)
    th2c = sb("th2c", [128, 1], stack=ESDw)

    ESD0 = ExitStack()
    djunk = sb("djunk", [128, 8192], stack=ESD0)
    dacc1 = sb("dacc1", [128, 1], stack=ESD0)
    dacc2 = sb("dacc2", [128, 1], stack=ESD0)
    dtmp = sb("dtmp", [128, 1], stack=ESD0)
    cfsl = [sb(f"cfsl{i}", [128, 2048], stack=ESD0) for i in range(2)]
    cpsl = sb("cpsl", [128, 8192], stack=ESD0)
    dsg = sb("dsg", [128, 8192], stack=ESD0)
    dqb = sb("dqb", [128, 8192], BF16, stack=ESD0)
    mset("vector", dacc1[:, :], 0.0, ["dacc1"])
    mset("vector", dacc2[:, :], 0.0, ["dacc2"])
    for mc in range(64):
        wb = mc % 2
        dma("sync", cfsl[wb][:, :], cfc_p[mc], ["cfsl%d" % wb], ["cfsl%d" % wb])
        act(djunk[:, 0:2048], cfsl[wb][:, :], AF.Abs, ["cfsl%d" % wb, "djunk", "dtmp"],
            ["djunk", "dtmp"], accum=dtmp[:, :])
        tt("vector", dacc1[:, :], dacc1[:, :], dtmp[:, :], OP.add, ["dacc1", "dtmp"], ["dacc1"])
    for mc in range(16):
        dma("sync", cpsl[:, :], cprT_p[mc], ["cpsl"], ["cpsl"])
        act(djunk[:, :], cpsl[:, :], AF.Abs, ["cpsl", "djunk", "dtmp"], ["djunk", "dtmp"],
            accum=dtmp[:, :])
        tt("vector", dacc2[:, :], dacc2[:, :], dtmp[:, :], OP.add, ["dacc2", "dtmp"], ["dacc2"])
    for (dac, dk, cA, ck, tA, tk) in ((dacc1, "dacc1", s1c, "s1c", th1c, "th1"),
                                      (dacc2, "dacc2", s2c_, "s2c", th2c, "th2")):
        mm(PR1[0:1, 0:1], ones_col[:, :], dac[:, 0:1], True, True, ["onec", dk], ["PR1"])
        tcopy("vector", dac[0:1, 0:1], PR1[0:1, 0:1], ["PR1", dk], [dk])
        ts("vector", dac[0:1, 0:1], dac[0:1, 0:1], 1.0 / (2048.0 * 8192.0), 1e-5,
           OP.mult, OP.max, [dk], [dk])
        mm(PC[:, 0:1], ones_row[:, :], dac[0:1, 0:1], True, True, ["oner", dk], ["PC"])
        tcopy("vector", cA[:, :], PC[:, 0:1], ["PC"], [ck])
        ts("vector", tA[:, :], cA[:, :], 0.5, None, OP.mult, None, [ck], [tk])
    tcopy("vector", s2cL1[:, :], s2c_[:, :], ["s2c"], ["s2cl1"])

    def ternary(slab, skey, jnk, jkey, sgnb, sgkey, qout, qkey, th, thkey, n, eng):
        act(jnk[:, 0:n], slab, AF.Abs, [skey, jkey], [jkey])
        act(sgnb[:, 0:n], slab, AF.Sign, [skey, sgkey], [sgkey])
        ts("vector", jnk[:, 0:n], jnk[:, 0:n], th, None, OP.is_gt, None,
           [jkey, thkey], [jkey])
        tt(eng, jnk[:, 0:n], jnk[:, 0:n], sgnb[:, 0:n], OP.mult, [jkey, sgkey], [jkey])
        tcopy(eng, qout[:, 0:n], jnk[:, 0:n], [jkey], [qkey])

    for mc in range(16):
        dma("sync", cpsl[:, :], cprT_p[mc], ["cpsl"], ["cpsl"])
        ternary(cpsl[:, :], "cpsl", djunk, "djunk", dsg, "dsg", dqb, "dqb",
                th2c[:, :], "th2", 8192, VG[mc % 2])
        dma("scalar", q2L1_s[mc], dqb[:, :], ["dqb"], ["q2L1"])
    pr.barrier()
    ESD0.close()

    ESD1 = ExitStack()
    x3h_r = [sb(f"x3hr{k}", [128, T], BF16, stack=ESD1) for k in range(16)]
    x3l_r = [sb(f"x3lr{k}", [128, T], BF16, stack=ESD1) for k in range(16)]
    for k in range(16):
        dma("sync", x3h_r[k][:, :], x3h_s[k], ["x3hs%d" % k, f"x3hr{k}"], [f"x3hr{k}"])
        dma("sync", x3l_r[k][:, :], x3l_s[k], ["x3ls%d" % k, f"x3lr{k}"], [f"x3lr{k}"])
    cfsl = [sb(f"cfslB{i}", [128, 2048], stack=ESD1) for i in range(2)]
    dj2 = sb("dj2", [128, 2048], stack=ESD1)
    dsg2 = sb("dsg2", [128, 2048], stack=ESD1)
    dqb2 = sb("dqb2", [128, 2048], BF16, stack=ESD1)
    drh = sb("drh", [128, 512], stack=ESD1)
    drt = sb("drt", [128, 512], stack=ESD1)
    mset("vector", xmT[:, :], 0.0, ["rmaxr"])
    for mc in range(64):
        wb = mc % 2
        dma("sync", cfsl[wb][:, :], cfc_p[mc], ["cfsl%d" % wb], ["cfsl%d" % wb])
        ternary(cfsl[wb][:, :], "cfsl%d" % wb, dj2, "dj2", dsg2, "dsg2", dqb2, "dqb2",
                th1c[:, :], "th1", 2048, VG[mc % 2])
        for ti, (t0, tc) in enumerate(TCH):
            pq_ = PQ[ti % 4]
            pk = "P%d" % (ti % 4)
            for kt in range(16):
                mm(pq_[:, 0:tc], dqb2[:, kt * 128:kt * 128 + 128], x3h_r[kt][:, t0:t0 + tc],
                   kt == 0, False, ["dqb2", f"x3hr{kt}"], [pk])
            for kt in range(16):
                mm(pq_[:, 0:tc], dqb2[:, kt * 128:kt * 128 + 128], x3l_r[kt][:, t0:t0 + tc],
                   False, kt == 15, ["dqb2", f"x3lr{kt}"], [pk])
            ts("vector", drh[:, 0:tc], pq_[:, 0:tc], s1c[:, :], 0.0, OP.mult, OP.max,
               [pk, "s1c"], ["drh"])
            act(drt[:, 0:tc], drh[:, 0:tc], AF.Square, ["drh"], ["drt"])
            tt("vector", xmT[:, t0:t0 + tc], xmT[:, t0:t0 + tc], drt[:, 0:tc], OP.max,
               ["rmaxr", "drt"], ["rmaxr"])
            dma("scalar", r_s[mc, :, t0:t0 + tc], drt[:, 0:tc], ["drt"], ["rs%d" % mc])
    for (t0, tc) in TCH:
        tcopy("vector", drh[:, 0:tc], xmT[:, t0:t0 + tc], ["rmaxr"], ["drh"])
        tree_max(drh, "drh", drt, "drt", tc)
        ts("vector", drh[0:1, 0:tc], drh[0:1, 0:tc], 1.0, 1e-5, OP.mult, OP.max,
           ["drh"], ["drh"])
        mm(P2[:, 0:tc], ones_row[:, :], drh[0:1, 0:tc], True, True, ["oner", "drh"], ["P2"])
        tcopy("vector", xmT[:, t0:t0 + tc], P2[:, 0:tc], ["P2", "rmaxr"], ["xmr"])
    pr.barrier()
    ESD1.close()
    pr.barrier()
    ESDw.close()

    ESD2 = ExitStack()
    rt_t = [sb(f"rt_t{i}", [128, 256], stack=ESD2) for i in range(2)]
    hqL = [sb(f"hqL{k}", [128, 256], BF16, stack=ESD2) for k in range(64)]
    q2sl2 = [sb(f"q2sl2{i}", [128, 8192], BF16, stack=ESD2) for i in range(2)]
    x3c = [sb(f"x3c{m}", [128, 256], stack=ESD2) for m in range(16)]
    d2inv = sb("d2inv", [128, 256], stack=ESD2)
    d2yS = [sb(f"d2yt{i}", [128, 256], stack=ESD2) for i in range(2)]
    d2y0S = [sb(f"d2y0t{i}", [128, 256], stack=ESD2) for i in range(2)]
    d2qS = [sb(f"d2qt{i}", [128, 256], stack=ESD2) for i in range(2)]
    d2sq = sb("d2sq", [128, 256], stack=ESD2)
    d2o = sb("d2o", [128, 256], stack=ESD2)
    d2sc = sb("d2sc", [128, 256], stack=ESD2)
    for ti, (t0, tc) in enumerate(TC2):
        recip(d2inv[:, 0:tc], xmT[:, t0:t0 + tc], ["xmr"], ["d2invB"])
        ts("vector", d2inv[:, 0:tc], d2inv[:, 0:tc], 255.0, None, OP.mult, None,
           ["d2invB"], ["d2invB"])
        ts("vector", d2sc[:, 0:tc], xmT[:, t0:t0 + tc], s2cL1[:, 0:1], 1.0 / 255.0,
           OP.mult, OP.mult, ["xmr", "s2cl1"], ["d2scB"])
        for kt in range(64):
            pp = kt % 2
            pf = "d2%d" % pp
            dy, dy0, dq = d2yS[pp], d2y0S[pp], d2qS[pp]
            dma("sync", rt_t[pp][:, 0:tc], r_s[kt, :, t0:t0 + tc],
                ["rs%d" % kt, "rt%d" % pp], ["rt%d" % pp])
            tt("gpsimd", dy[:, 0:tc], rt_t[pp][:, 0:tc], d2inv[:, 0:tc], OP.mult,
               ["rt%d" % pp, "d2invB", pf + "y"], [pf + "y"])
            ladder_round(pf, dy[:, 0:tc], dy[:, 0:tc], dy0[:, 0:tc], dq[:, 0:tc], tc,
                         [128, 64, 32, 16, 8, 4, 2, 1], "vector", [pf + "y"])
            tcopy("vector", hqL[kt][:, 0:tc], dq[:, 0:tc], [pf + "q"], [f"hqL{kt}"])
        for m in range(16):
            wb = m % 2
            dma("sync", q2sl2[wb][:, :], q2L1_s[m], ["q2L1", "q2s2%d" % wb], ["q2s2%d" % wb])
            pq_ = PQ[m % 4]
            pk = "P%d" % (m % 4)
            for kt in range(64):
                mm(pq_[:, 0:tc], q2sl2[wb][:, kt * 128:kt * 128 + 128], hqL[kt][:, 0:tc],
                   kt == 0, kt == 63, ["q2s2%d" % wb, f"hqL{kt}"], [pk])
            dma("sync", x3c[m][:, 0:tc], x3_s[m, :, t0:t0 + tc], ["x3s%d" % m, f"x3c{m}"], [f"x3c{m}"])
            tt("vector", d2o[:, 0:tc], pq_[:, 0:tc], d2sc[:, 0:tc], OP.mult,
               [pk, "d2scB"], ["d2o"])
            tt("vector", x3c[m][:, 0:tc], x3c[m][:, 0:tc], d2o[:, 0:tc], OP.add,
               [f"x3c{m}", "d2o"], [f"x3c{m}"])
        layer_norm(x3c, [f"x3c{m}" for m in range(16)], tc)
        for m in range(16):
            dma("scalar", outT[m * 128:m * 128 + 128, t0:t0 + tc], x3c[m][:, 0:tc],
                [f"x3c{m}"], ["outd"])
    pr.barrier()
    ESD2.close()
    ESX.close()

    pr.finalize(nc, ES)
    ES.close()
    nc._pr = pr
    return nc


def _pack_lhsT(w, mtile=128):
    M, K = w.shape
    nmc = (M + mtile - 1) // mtile
    pad = nmc * mtile - M
    if pad:
        w = np.concatenate([w, np.zeros((pad, K), w.dtype)], 0)
    nkt = K // 128
    a = w.reshape(nmc, mtile, nkt, 128).transpose(0, 3, 2, 1)
    return np.ascontiguousarray(a.reshape(nmc, 128, K))


def kernel(L0_attn_u, L0_attn_W_in, L0_attn_conv_w, L0_attn_conv_b, L0_attn_dt_bias,
           L0_attn_A_log, L0_attn_D, L0_attn_norm_w, L0_attn_W_out, L0_ffn_w_gate_up,
           L0_ffn_w_down, L1_attn_Wq, L1_attn_bq, L1_attn_Wk, L1_attn_bk, L1_attn_Wv,
           L1_attn_bv, L1_attn_Wo, L1_attn_bo, L1_ffn_c_fc, L1_ffn_c_proj):
    S = L0_attn_u.shape[1]
    T = S // NCORES
    nc = build(T)
    maps = make_inmaps(T, L0_attn_u, L0_attn_W_in, L0_attn_conv_w, L0_attn_conv_b,
                       L0_attn_dt_bias, L0_attn_A_log, L0_attn_D, L0_attn_norm_w,
                       L0_attn_W_out, L0_ffn_w_gate_up, L0_ffn_w_down, L1_attn_Wq,
                       L1_attn_bq, L1_attn_Wk, L1_attn_bk, L1_attn_Wv, L1_attn_bv,
                       L1_attn_Wo, L1_attn_bo, L1_ffn_c_fc, L1_ffn_c_proj)
    res = run_bass_kernel_spmd(nc, maps, core_ids=list(range(NCORES)))
    out = np.concatenate([np.ascontiguousarray(r["outT"]).T for r in res.results], axis=0)
    return out.reshape(1, S, D).astype(np.float32)


def make_inmaps(T, u, W_in, conv_w, conv_b, dt_bias, A_log, Dh, norm_w, W_out,
                w_gate_up, w_down, Wq, bq, Wk, bk, Wv, bv, Wo, bo, c_fc, c_proj):
    f32 = np.float32
    uT_full = np.ascontiguousarray(np.asarray(u, f32)[0].T)
    S = uT_full.shape[1]
    uT_pad = np.concatenate([np.zeros((D, 3), f32), uT_full], axis=1)
    shared = {}
    shared["w_in_p"] = _pack_lhsT(np.asarray(W_in, f32))
    cw = np.asarray(conv_w, f32)[:, 0, :]
    shared["convw"] = np.ascontiguousarray(
        cw.reshape(34, 128, 4).transpose(1, 0, 2).reshape(128, 34 * 4))
    shared["convb"] = np.ascontiguousarray(np.asarray(conv_b, f32).reshape(34, 128).T)
    shared["dtb"] = np.asarray(dt_bias, f32).reshape(64, 1)
    shared["alog"] = np.asarray(A_log, f32).reshape(64, 1)
    shared["drep"] = np.ascontiguousarray(
        np.repeat(np.asarray(Dh, f32), 64).reshape(32, 128).T)
    shared["nw"] = np.ascontiguousarray(np.asarray(norm_w, f32).reshape(32, 128).T)
    shared["w_out_p"] = _pack_lhsT(np.asarray(W_out, f32))
    shared["w_gup_p"] = np.ascontiguousarray(np.asarray(w_gate_up, f32).reshape(88, 128, 2048))
    shared["w_down_p"] = np.ascontiguousarray(np.asarray(w_down, f32).reshape(16, 128, 5632))
    for nm, W in (("wq_p", Wq), ("wk_p", Wk), ("wv_p", Wv), ("wo_p", Wo)):
        shared[nm] = _pack_lhsT(np.asarray(W, f32))
    for nm, b in (("bq", bq), ("bk", bk), ("bv", bv), ("bo", bo)):
        shared[nm] = np.ascontiguousarray(np.asarray(b, f32).reshape(16, 128).T)
    cfc = np.asarray(c_fc, f32)
    shared["cfc_p"] = np.ascontiguousarray(
        cfc.reshape(16, 128, 64, 128).transpose(2, 1, 0, 3).reshape(64, 128, 2048))
    cpr = np.asarray(c_proj, f32)
    shared["cprT_p"] = np.ascontiguousarray(
        cpr.reshape(16, 128, 64, 128).transpose(0, 3, 2, 1).reshape(16, 128, 8192))
    shared["eye"] = np.eye(128, dtype=f32)
    ohp = np.zeros((64, 32, 128), f32)
    for g in range(32):
        ohp[2 * g, g, 0:64] = 1.0
        ohp[2 * g + 1, g, 64:128] = 1.0
    shared["ohp"] = np.ascontiguousarray(ohp.reshape(64, 4096))
    si = np.arange(128)[:, None]
    tj = np.arange(128)[None, :]
    shared["am"] = np.where(si <= tj, 0.0, -1e30).astype(f32)
    maps = []
    for i in range(NCORES):
        m = dict(shared)
        m["uT"] = np.ascontiguousarray(uT_pad[:, i * T:i * T + T + 3])
        m["mask8"] = np.broadcast_to(
            (np.arange(8)[None, :] < i).astype(f32), (128, 8)).copy()
        me = np.zeros((128, 16), f32)
        for j in range(8):
            if j < i:
                me[16 * j:16 * j + 16, :] = np.eye(16, dtype=f32)
        m["maskexp"] = me
        maps.append(m)
    return maps
